# version 24
# baseline (speedup 1.0000x reference)
import numpy as np
import ml_dtypes
import concourse.bass as bass
import concourse.mybir as mybir
from concourse.bass_utils import run_bass_kernel_spmd

F32 = mybir.dt.float32
BF16 = mybir.dt.bfloat16
AF = mybir.ActivationFunctionType
ALU = mybir.AluOpType
AX = mybir.AxisListType.X
BFNP = ml_dtypes.bfloat16

D = 1024
T = 4096
NCH = 8
TC = 512
NIC = 8
NHP = 8
NCORE = 8

(C_INV1024, C_EIGHTH, C_M05, C_THREE, C_NEG1, C_HALF, C_EPS, C_M18, C_PI2,
 C_MI1024) = range(10)


def build_nc(detect_races=True):
    nc = bass.Bass("TRN2", target_bir_lowering=False, debug=False,
                   dynamic_dma_scratch_size=16384,
                   detect_race_conditions=detect_races)

    xT = nc.dram_tensor("xT", [D, T], F32, kind="ExternalInput").ap()
    w_d = {
        "k": nc.dram_tensor("wkT", [D, D], F32, kind="ExternalInput").ap(),
        "v": nc.dram_tensor("wvT", [D, D], F32, kind="ExternalInput").ap(),
        "q": nc.dram_tensor("wqT", [D, D], F32, kind="ExternalInput").ap(),
        "o": nc.dram_tensor("woT", [D, D], F32, kind="ExternalInput").ap(),
    }
    th_d = nc.dram_tensor("theta", [D, D], F32, kind="ExternalInput").ap()
    thn_d = nc.dram_tensor("theta_n", [D, D], F32, kind="ExternalInput").ap()
    ph_d = nc.dram_tensor("phi", [D, D], F32, kind="ExternalInput").ap()
    phn_d = nc.dram_tensor("phi_n", [D, D], F32, kind="ExternalInput").ap()
    bigmask_d = nc.dram_tensor("bigmask", [128, 896], BF16, kind="ExternalInput").ap()
    onescol_d = nc.dram_tensor("onescol", [128, 1], BF16, kind="ExternalInput").ap()
    allones_d = nc.dram_tensor("allones", [128, 128], BF16, kind="ExternalInput").ap()
    yT = nc.dram_tensor("yT", [D, T], F32, kind="ExternalOutput").ap()

    WIDX = {"k": 0, "v": 1, "q": 2, "o": 3, "cos": 4, "sin": 5}
    wscr = [nc.dram_tensor(f"wscr{i}", [128, 8192], BF16, kind="Internal").ap()
            for i in range(6)]
    rscr_d = nc.dram_tensor("rscr", [1, 512], BF16, kind="Internal").ap()

    A = nc.alloc_sbuf_tensor
    kT = A("kT", [128, 32768], BF16)
    vT = A("vT", [128, 32768], BF16)
    xf = A("xf", [128, 4096], F32)
    xb = A("xb", [128, 4096], BF16)
    qam = A("qam", [128, 4096], BF16)
    oy = A("oy", [128, 4096], BF16)
    pB = A("pB", [128, 2048], BF16)
    wsl = A("wsl", [128, 8192], BF16)
    bigmask = A("bigmask_s", [128, 896], BF16)
    bc1 = A("bc1", [128, 512], BF16)
    rowsF = A("rowsF", [128, 512], F32)
    rowsB = A("rowsB", [128, 1024], BF16)
    wsums = A("wsums", [128, 5120], BF16)
    sq = A("sq", [128, 1024], BF16)
    onescol = A("onescol_s", [128, 1], BF16)
    onescol_f = A("onescol_f", [128, 1], F32)
    allones = A("allones_s", [128, 128], BF16)
    ccol = A("ccol", [128, 10], F32)
    erow = A("erow", [128, 16], F32)
    ecol4 = A("ecol4", [128, 4], F32)

    bcf = nc.alloc_sbuf_tensor_at("bcf", [128, 512], F32,
                                  offset=nc.lookup_mloc(pB).addr)
    bc4 = nc.alloc_sbuf_tensor_at("bc4", [128, 512], BF16,
                                  offset=nc.lookup_mloc(pB).addr + 2048)
    kaddr = nc.lookup_mloc(kT).addr
    vaddr = nc.lookup_mloc(vT).addr
    stage_f = nc.alloc_sbuf_tensor_at("stage_f", [128, 8192], F32, offset=kaddr)
    stage_b = nc.alloc_sbuf_tensor_at("stage_b", [128, 8192], BF16, offset=kaddr + 32768)
    stage2_f = nc.alloc_sbuf_tensor_at("stage2_f", [128, 8192], F32, offset=vaddr)
    stage_b2 = nc.alloc_sbuf_tensor_at("stage_b2", [128, 8192], BF16,
                                       offset=vaddr + 32768)

    parena = nc.alloc_psum_tensor("parena", [128, 2048], F32)
    pO = nc.alloc_psum_tensor("pO", [128, 512], F32)
    pstat = nc.alloc_psum_tensor("pstat", [128, 512], F32)
    parena2 = nc.alloc_psum_tensor("parena2", [128, 1024], F32)
    pbc = parena2[:, 0:512]
    px = parena2[:, 512:1024]

    SETS = [parena[:, 0:1024], parena[:, 1024:2048], parena2[:, 0:1024]]

    sem_names = ["pe", "act", "dve", "ldg", "lds", "out",
                 "ld0", "ld1", "ldx", "ldn", "lsw", "lsw2", "rc", "rc2"]
    sems = {k: nc.alloc_semaphore(k) for k in sem_names}
    q = {k: [] for k in ("tensor", "scalar", "vector", "gpsimd", "sync")}
    n = {k: 0 for k in sem_names}

    def on(eng, f):
        q[eng].append(f)

    def w(eng, key, v):
        if v and v > 0:
            on(eng, lambda e, s=sems[key], vv=v: e.wait_ge(s, vv))

    def tick(eng, key, f, k=1):
        n[key] += k
        on(eng, lambda e, ff=f, s=sems[key], kk=k: ff(e).then_inc(s, kk))
        return n[key]

    def ccol_(i):
        return ccol[:, i:i + 1]

    def crow_(i):
        return ccol[0:1, i:i + 1]

    RFA = bcf[0:1, 0:512]
    RFB = rowsF[0:1, 0:512]
    RB_R = rowsB[0:1, 0:512]
    RB_MR = rowsB[0:1, 512:1024]

    WS = {"k": wsums[0:1, 0:1024], "v": wsums[0:1, 1024:2048],
          "q": wsums[0:1, 2048:3072], "cos": wsums[0:1, 3072:4096],
          "sin": wsums[0:1, 4096:5120]}

    tick("sync", "lds", lambda e: e.dma_start(out=bigmask[:, :], in_=bigmask_d), 16)
    tick("sync", "lds", lambda e: e.dma_start(out=onescol[:, :], in_=onescol_d), 16)
    m_consts = tick("sync", "lds", lambda e: e.dma_start(out=allones[:, :], in_=allones_d), 16)

    for idx, val in [(C_INV1024, 1.0 / 1024), (C_EIGHTH, 0.125), (C_M05, -0.5),
                     (C_THREE, 3.0), (C_NEG1, -1.0), (C_HALF, 0.5), (C_EPS, 1e-5),
                     (C_PI2, float(np.pi / 2)), (C_MI1024, -1.0 / 1024)]:
        on("vector", lambda e, i=idx, v=val: e.memset(ccol[:, i:i + 1], v))
    on("vector", lambda e: e.memset(onescol_f[:, :], 1.0))
    m_memset = tick("vector", "dve",
                    lambda e: e.memset(ccol[:, C_M18:C_M18 + 1], -0.125))

    w("tensor", "lds", m_consts)
    w("scalar", "dve", m_memset)

    stageB = [stage_b, stage_b2]
    ssem = ["lsw", "lsw2"]
    buf_free_pe = [0, 0]
    buf_free_st = [0, 0]
    wscr_ready = {}

    prev_wsum_dve = [0]
    wsum_pe_last = [0]

    def startup_wsum(name, sb):
        w("tensor", "dve", prev_wsum_dve[0])
        for jh in range(2):
            dst = pstat[0:1, :] if jh == 0 else pbc[0:1, :]
            for ic in range(NIC):
                def f(e, d=dst, i=ic, j=jh, sb=sb):
                    return e.matmul(d, onescol[:, :],
                                    sb[:, i * 1024 + j * 512: i * 1024 + (j + 1) * 512],
                                    start=(i == 0), stop=(i == 7))
                if ic == 7 and jh == 1:
                    wsum_pe_last[0] = tick("tensor", "pe", f)
                else:
                    on("tensor", lambda e, ff=f: ff(e))
        w("vector", "pe", wsum_pe_last[0])
        on("vector", lambda e, nm=name: e.tensor_scalar(
            WS[nm][:, 0:512], pstat[0:1, :], crow_(C_NEG1), None, ALU.mult))
        mdone = tick("vector", "dve", lambda e, nm=name: e.tensor_scalar(
            WS[nm][:, 512:1024], pbc[0:1, :], crow_(C_NEG1), None, ALU.mult))
        prev_wsum_dve[0] = mdone
        return mdone

    wsum_dve = 0
    bi = 0
    for name in ["k", "v", "q"]:
        sb = stageB[bi]
        w("gpsimd", "pe", buf_free_pe[bi])
        w("gpsimd", ssem[bi], buf_free_st[bi])
        m_cast = tick("gpsimd", "ldg", lambda e, s=w_d[name], sb=sb: e.dma_start(
            out=sb[:, :], in_=s.rearrange("(ic p) j -> p ic j", p=128)), 16)
        w("tensor", "ldg", m_cast)
        wsum_dve = startup_wsum(name, sb)
        buf_free_pe[bi] = wsum_pe_last[0]
        w("sync", "dve", wsum_dve)
        buf_free_st[bi] = tick("sync", ssem[bi], lambda e, wi=WIDX[name], sb=sb:
                               e.dma_start(out=wscr[wi], in_=sb[:, :]), 16)
        wscr_ready[name] = (ssem[bi], buf_free_st[bi])
        bi ^= 1

    sb = stageB[bi]
    w("gpsimd", "pe", buf_free_pe[bi])
    w("gpsimd", ssem[bi], buf_free_st[bi])
    m_cast = tick("gpsimd", "ldg", lambda e, sb=sb: e.dma_start(
        out=sb[:, :], in_=w_d["o"].rearrange("(ic p) j -> p ic j", p=128)), 16)
    w("vector", "ldg", m_cast)
    wo_center = 0
    for ic in range(NIC):
        on("vector", lambda e, i=ic, sb=sb: e.tensor_reduce(
            erow[:, 8:9], sb[:, i * 1024:(i + 1) * 1024], AX, ALU.add))
        mneg = tick("vector", "dve", lambda e: e.tensor_scalar(
            erow[:, 8:9], erow[:, 8:9], ccol_(C_MI1024), None, ALU.mult))
        w("vector", "dve", mneg)
        wo_center = tick("vector", "dve", lambda e, i=ic, sb=sb: e.tensor_scalar(
            sb[:, i * 1024:(i + 1) * 1024],
            sb[:, i * 1024:(i + 1) * 1024], erow[:, 8:9], None, ALU.add))
    w("sync", "dve", wo_center)
    buf_free_st[bi] = tick("sync", ssem[bi], lambda e, sb=sb: e.dma_start(
        out=wscr[WIDX["o"]], in_=sb[:, :]), 16)
    wscr_ready["o"] = (ssem[bi], buf_free_st[bi])
    bi ^= 1

    on("scalar", lambda e: e.preload_act_table(AF.Sin))
    prev_sin = 0
    prev_add = 0
    for name, base_d, noise_d, use_cos in [("cos", th_d, thn_d, True),
                                           ("sin", ph_d, phn_d, False)]:
        sb = stageB[bi]
        w("sync", "act", prev_sin)
        mf = tick("sync", "lds", lambda e, s=base_d: e.dma_start(
            out=stage_f[:, :], in_=s.rearrange("(ic p) j -> p ic j", p=128)), 16)
        w("gpsimd", "dve", prev_add)
        mg = tick("gpsimd", "ldn", lambda e, s=noise_d: e.dma_start(
            out=stage2_f[:, :], in_=s.rearrange("(ic p) j -> p ic j", p=128)), 16)
        w("vector", "lds", mf)
        w("vector", "ldn", mg)
        add_done = tick("vector", "dve", lambda e: e.tensor_tensor(
            stage_f[:, :], stage_f[:, :], stage2_f[:, :], ALU.add))
        prev_add = add_done
        w("scalar", "dve", add_done)
        w("scalar", ssem[bi], buf_free_st[bi])
        w("scalar", "pe", buf_free_pe[bi])
        if use_cos:
            sin_done = tick("scalar", "act", lambda e, sb=sb: e.activation(
                sb[:, :], stage_f[:, :], AF.Sin, bias=ccol_(C_PI2)))
        else:
            sin_done = tick("scalar", "act", lambda e, sb=sb: e.activation(
                sb[:, :], stage_f[:, :], AF.Sin))
        prev_sin = sin_done
        w("tensor", "act", sin_done)
        wsum_dve = startup_wsum(name, sb)
        buf_free_pe[bi] = wsum_pe_last[0]
        w("sync", "dve", wsum_dve)
        w("sync", "act", sin_done)
        buf_free_st[bi] = tick("sync", ssem[bi], lambda e, wi=WIDX[name], sb=sb:
                               e.dma_start(out=wscr[wi], in_=sb[:, :]), 16)
        wscr_ready[name] = (ssem[bi], buf_free_st[bi])
        bi ^= 1

    startup_lsw = buf_free_st[0]
    startup_lsw2 = buf_free_st[1]
    startup_b0_pe = buf_free_pe[0]
    startup_wsum_dve = wsum_dve

    startup_sin_act = prev_sin

    from concourse.hw_specs import get_activation_tables
    tables = list(get_activation_tables(nc.m.arch).keys())
    nle_idx = tables.index("natural_log_exp_and_others")
    on("scalar", lambda e, idx=nle_idx: e.add_instruction(mybir.InstLoadActFuncSet(
        name=e.bass.get_next_instruction_name(), act_func_set_id=idx,
        ins=[], outs=[])))

    half_seq = 0
    slot_free = [0, 0]
    slot_of = {}

    def stream_half(c, name, h):
        nonlocal half_seq
        slot = half_seq % 2
        half_seq += 1
        sem = "ld0" if slot == 0 else "ld1"
        w("gpsimd", "pe", slot_free[slot])
        w("gpsimd", *wscr_ready[name])
        src = wscr[WIDX[name]].rearrange("p (ic hh j) -> p ic hh j", ic=8, hh=2)[:, :, h, :]
        dst = wsl[:, slot * 4096:(slot + 1) * 4096].rearrange("p (ic j) -> p ic j", ic=8)
        m = tick("gpsimd", sem, lambda e, s=src, d=dst: e.dma_start(out=d, in_=s), 16)
        slot_of[(c, name, h)] = (slot, (sem, m))

    def wap(slot, ic, sub):
        return wsl[:, slot * 4096 + ic * 512 + sub * 128: slot * 4096 + ic * 512 + (sub + 1) * 128]

    def wap512(slot, ic):
        return wsl[:, slot * 4096 + ic * 512: slot * 4096 + (ic + 1) * 512]

    bankF = lambda b: parena[:, b * 512:(b + 1) * 512]
    bank_free = [0, 0, 0, 0]
    bank_free_sem = ["dve", "dve", "dve", "dve"]
    pstat_free = startup_wsum_dve
    pstat_free_sem = "dve"
    px_free_sem = "dve"
    pbc_free = startup_wsum_dve
    pbc_free_sem = "dve"
    px_free = 0
    oev = 0
    oy_scaled = 0
    p3_evac_last = 0
    r1_store = 0
    rowsB_free_pe = 0
    last_bcast_pe = 0
    last_exp3 = [0, 0, 0]
    last_av = {0: 0, 1: 0}
    res2_prev = 0
    zrow_free = 0
    zfA = 0
    zfB = 0
    zb_mm_last = 0
    m_xb_next = None
    sq_free = [0, 0]

    def pe_bank_wait(b):
        w("tensor", bank_free_sem[b], bank_free[b])

    def ln_rows(stats_pe, has_mean, centered=False, fold_r=True):
        nonlocal pstat_free, pbc_free, pstat_free_sem, pbc_free_sem
        pstat_free_sem = "dve"
        pbc_free_sem = "dve"
        w("vector", "pe", stats_pe)
        w("vector", "pe", rowsB_free_pe)
        if has_mean and centered:
            on("vector", lambda e: e.tensor_scalar(RFA, pstat[0:1, :], crow_(C_INV1024), None, ALU.mult))
            mark = tick("vector", "dve", lambda e: e.tensor_scalar(
                RFB, pbc[0:1, :], crow_(C_INV1024), None, ALU.mult))
        elif has_mean:
            on("vector", lambda e: e.tensor_scalar(RFA, pstat[0:1, :], crow_(C_INV1024), None, ALU.mult))
            on("vector", lambda e: e.scalar_tensor_tensor(
                RFB, RFA, crow_(C_NEG1), RFA, ALU.mult, ALU.mult))
            mark = tick("vector", "dve", lambda e: e.scalar_tensor_tensor(
                RFB, pbc[0:1, :], crow_(C_INV1024), RFB, ALU.mult, ALU.add))
        else:
            mark = tick("vector", "dve", lambda e: e.tensor_scalar(
                RFB, pstat[0:1, :], crow_(C_INV1024), None, ALU.mult))
        pstat_free = mark
        pbc_free = mark
        w("scalar", "dve", mark)
        on("scalar", lambda e: e.activation(RFB, RFB, AF.Ln, bias=crow_(C_EPS)))
        r_act = tick("scalar", "act", lambda e: e.activation(RFB, RFB, AF.Exp, scale=crow_(C_M05)))
        w("vector", "act", r_act)
        if has_mean:
            if fold_r:
                on("vector", lambda e: e.tensor_tensor(RFA, RFA, RFB, ALU.mult))
            on("vector", lambda e: e.tensor_copy(RB_R, RFB))
            return tick("vector", "dve", lambda e: e.tensor_copy(RB_MR, RFA))
        else:
            return tick("vector", "dve", lambda e: e.tensor_copy(RB_R, RFB))

    def bcast(row_ap, dst_ap, row_ready_dve):
        nonlocal pbc_free, last_bcast_pe, pbc_free_sem
        w("tensor", pbc_free_sem, pbc_free)
        pbc_free_sem = "dve"
        w("tensor", "dve", row_ready_dve)
        m = tick("tensor", "pe", lambda e, r=row_ap: e.matmul(
            pbc[:, :], allones[0:1, :], r, start=True, stop=True))
        last_bcast_pe = m
        w("vector", "pe", m)
        pbc_free = tick("vector", "dve", lambda e, d=dst_ap: e.tensor_copy(d, pbc[:, :]))
        return pbc_free

    out_hist = [0]
    for c in range(NCH):
        w("sync", "out", out_hist[c])
        m_xf = tick("sync", "lds", lambda e, cc=c: e.dma_start(
            out=xf[:, :].rearrange("p (ic t) -> p ic t", ic=8),
            in_=xT.rearrange("(ic p) t -> p ic t", p=128)[:, :, cc * TC:(cc + 1) * TC]), 16)
        if c == 0:
            m_xb = tick("gpsimd", "ldx", lambda e, cc=c: e.dma_start(
                out=xb[:, :].rearrange("p (ic t) -> p ic t", ic=8),
                in_=xT.rearrange("(ic p) t -> p ic t", p=128)[:, :, cc * TC:(cc + 1) * TC]), 16)
        else:
            m_xb = m_xb_next

        stream_half(c, "k", 0)
        stream_half(c, "k", 1)

        if c == 0:
            w("vector", "act", startup_sin_act)
            w("vector", "lsw", startup_lsw)
            w("vector", "lsw2", startup_lsw2)
            w("vector", "pe", startup_b0_pe)
        w("scalar", "ldx", m_xb)
        stats_last = 0
        for ic in range(NIC):
            sh = ic % 2
            w("scalar", "pe", sq_free[sh])
            mact = tick("scalar", "act", lambda e, i=ic, s=sh: e.activation(
                sq[:, s * 512:(s + 1) * 512], xb[:, i * 512:(i + 1) * 512],
                AF.Square))
            w("tensor", "ldx", m_xb)
            w("tensor", "act", mact)
            if ic == 0:
                w("tensor", pstat_free_sem, pstat_free)
                w("tensor", pbc_free_sem, pbc_free)
            on("tensor", lambda e, i=ic: e.matmul(
                pstat[0:1, :], onescol[:, :], xb[:, i * 512:(i + 1) * 512],
                start=(i == 0), stop=(i == 7)))

            def fsq(e, s=sh, i=ic):
                return e.matmul(pbc[0:1, :], onescol[:, :], sq[:, s * 512:(s + 1) * 512],
                                start=(i == 0), stop=(i == 7))
            stats_last = tick("tensor", "pe", fsq)
            sq_free[sh] = stats_last
        ln1 = ln_rows(stats_last, has_mean=True, fold_r=False)
        bcast(RB_R, bc1[:, :], ln1)
        w("sync", "dve", ln1)
        w("sync", "dve", p3_evac_last)
        r1_store = tick("sync", "rc", lambda e: e.dma_start(
            out=rscr_d, in_=RB_R[0:1, 0:512]), 16)
        w("gpsimd", "rc", r1_store)
        mrc = tick("gpsimd", "rc2", lambda e: e.dma_start(
            out=ecol4[:, :], in_=rscr_d.rearrange("o (s p) -> (o p) s", p=128),
            allow_slow_non_contiguous=True), 16)

        slot_k0, mk0 = slot_of[(c, "k", 0)]
        slot_k1, mk1 = slot_of[(c, "k", 1)]
        for jc in range(8):
            slot, mld = (slot_k0, mk0) if jc < 4 else (slot_k1, mk1)
            b = jc % 2
            pe_bank_wait(b)
            w("tensor", mld[0], mld[1])
            w("tensor", "ldx", m_xb)
            for ic in range(NIC):
                on("tensor", lambda e, i=ic, s=slot, j=jc, bb=b: e.matmul(
                    bankF(bb), wap(s, i, j % 4), xb[:, i * 512:(i + 1) * 512],
                    start=(i == 0), stop=False))

            if jc == 0:
                w("tensor", "dve", ln1)

            def fmean(e, j=jc, bb=b):
                return e.matmul(bankF(bb), WS["k"][:, j * 128:(j + 1) * 128], RB_MR,
                                start=False, stop=True)
            mlast = tick("tensor", "pe", fmean)
            if jc == 3:
                slot_free[slot_k0] = mlast
            if jc == 7:
                slot_free[slot_k1] = mlast
            w("vector", "pe", mlast)
            ev = tick("vector", "dve", lambda e, j=jc, cc=c, bb=b: e.tensor_tensor(
                kT[:, j * 4096 + cc * TC: j * 4096 + (cc + 1) * TC], bankF(bb),
                bc1[:, :], ALU.mult))
            bank_free[b] = ev
            bank_free_sem[b] = "dve"
        stream_half(c, "v", 0)
        stream_half(c, "v", 1)

        slot_v0, mv0 = slot_of[(c, "v", 0)]
        slot_v1, mv1 = slot_of[(c, "v", 1)]
        for jh in range(2):
            slot, mld = (slot_v0, mv0) if jh == 0 else (slot_v1, mv1)
            for st in range(4):
                b = 2 + st % 2
                pe_bank_wait(b)
                w("tensor", mld[0], mld[1])
                for ic in range(NIC):
                    on("tensor", lambda e, i=ic, s=slot, t=st, bb=b: e.matmul(
                        bankF(bb), xb[:, i * 512 + t * 128: i * 512 + (t + 1) * 128],
                        wap512(s, i), start=(i == 0), stop=False))

                if jh == 0 and st == 0:
                    w("tensor", "dve", ln1)

                def fmeanv(e, t=st, j=jh, bb=b):
                    return e.matmul(bankF(bb), RB_MR[:, t * 128:(t + 1) * 128],
                                    WS["v"][:, j * 512:(j + 1) * 512], start=False, stop=True)
                mlast = tick("tensor", "pe", fmeanv)
                if st == 3:
                    if jh == 0:
                        slot_free[slot_v0] = mlast
                    else:
                        slot_free[slot_v1] = mlast
                w("vector", "pe", mlast)
                if jh == 0 and st == 0:
                    w("vector", "rc2", mrc)
                ev = tick("vector", "dve", lambda e, t=st, j=jh, cc=c, bb=b: e.tensor_scalar(
                    vT[:, (cc * 4 + t) * 1024 + j * 512: (cc * 4 + t) * 1024 + (j + 1) * 512],
                    bankF(bb), ecol4[:, t:t + 1], None, ALU.mult))
                bank_free[b] = ev
                bank_free_sem[b] = "dve"
        p3_evac_last = n["dve"]
        stream_half(c, "q", 0)
        stream_half(c, "q", 1)

        slot_q0, mq0 = slot_of[(c, "q", 0)]
        slot_q1, mq1 = slot_of[(c, "q", 1)]
        q_evac = 0
        for jc in range(8):
            slot, mld = (slot_q0, mq0) if jc < 4 else (slot_q1, mq1)
            b = jc % 2
            pe_bank_wait(b)
            w("tensor", mld[0], mld[1])
            w("tensor", "ldx", m_xb)
            for ic in range(NIC):
                on("tensor", lambda e, i=ic, s=slot, j=jc, bb=b: e.matmul(
                    bankF(bb), wap(s, i, j % 4), xb[:, i * 512:(i + 1) * 512],
                    start=(i == 0), stop=False))

            def fmeanq(e, j=jc, bb=b):
                return e.matmul(bankF(bb), WS["q"][:, j * 128:(j + 1) * 128], RB_MR,
                                start=False, stop=True)
            mlast = tick("tensor", "pe", fmeanq)
            if jc == 3:
                slot_free[slot_q0] = mlast
            if jc == 7:
                slot_free[slot_q1] = mlast
            w("vector", "pe", mlast)
            q_evac = tick("vector", "dve", lambda e, j=jc, bb=b: e.tensor_tensor(
                qam[:, j * 512:(j + 1) * 512], bankF(bb), bc1[:, :], ALU.mult))
            bank_free[b] = q_evac
            bank_free_sem[b] = "dve"
        xb_free_pe = n["pe"]
        rowsB_free_pe = n["pe"]
        stream_half(c, "o", 0)
        stream_half(c, "o", 1)

        if c + 1 < NCH:
            w("gpsimd", "pe", xb_free_pe)
            m_xb_next = tick("gpsimd", "ldx", lambda e, cc=c + 1: e.dma_start(
                out=xb[:, :].rearrange("p (ic t) -> p ic t", ic=8),
                in_=xT.rearrange("(ic p) t -> p ic t", p=128)[:, :, cc * TC:(cc + 1) * TC]), 16)

        kts = 4 * c + 4
        w("scalar", "dve", res2_prev)
        if c == 0:
            w("vector", "lds", m_consts)
        pending_early1 = None
        pending_early2 = None
        pending_late = None
        for hp in range(NHP):
            if hp == 0:
                w("tensor", pstat_free_sem, pstat_free)
                w("tensor", px_free_sem, px_free)
                w("tensor", "dve", res2_prev)
            def emit_av(K, pp, ex_a, ex_b, avwA, avwB, off, P=hp, KTS=None):
                if K == 0:
                    w("tensor", "act", oev)
                w("tensor", "act", ex_a)
                if avwA:
                    w("tensor", "dve", avwA)

                def favA(e, P=P, K=K, pp=pp, KTS=KTS, o=off):
                    return e.matmul(pO[0:64, o:512],
                                    vT[:, K * 1024 + P * 128: K * 1024 + P * 128 + 64],
                                    pB[:, pp * 1024 + o:pp * 1024 + 512],
                                    start=(K == 0), stop=(K == KTS - 1),
                                    tile_position=(0, 0), skip_group_check=True)
                on("tensor", lambda e, f=favA: f(e))
                w("tensor", "act", ex_b)
                if avwB:
                    w("tensor", "dve", avwB)

                def favB(e, P=P, K=K, pp=pp, KTS=KTS, o=off):
                    return e.matmul(pO[64:128, o:512],
                                    vT[:, K * 1024 + P * 128 + 64: K * 1024 + P * 128 + 128],
                                    pB[:, pp * 1024 + 512 + o:pp * 1024 + 1024],
                                    start=(K == 0), stop=(K == KTS - 1),
                                    tile_position=(0, 64), skip_group_check=True)
                on("tensor", lambda e, f=favB: f(e))

                if K == 0:
                    w("tensor", pstat_free_sem, pstat_free)

                def fzA(e, K=K, pp=pp, KTS=KTS, o=off):
                    return e.matmul(pstat[64:65, o:512], onescol[:, :],
                                    pB[:, pp * 1024 + o:pp * 1024 + 512],
                                    start=(K == 0), stop=(K == KTS - 1),
                                    tile_position=(0, 64), skip_group_check=True)

                def fzB(e, K=K, pp=pp, KTS=KTS, o=off):
                    return e.matmul(pstat[32:33, o:512], onescol[:, :],
                                    pB[:, pp * 1024 + 512 + o:pp * 1024 + 1024],
                                    start=(K == 0), stop=(K == KTS - 1),
                                    tile_position=(0, 32), skip_group_check=True)
                on("tensor", lambda e, f=fzA: f(e))
                last_av[pp] = tick("tensor", "pe", fzB)

            av_q = []
            for kt in range(kts):
                p = kt % 2
                s3 = kt % 3
                sAP = SETS[s3]
                m = kt - 4 * c
                off = m * 128 if m > 0 else 0
                w("tensor", "act", last_exp3[s3])
                if hp == 0 and kt < 3:
                    if s3 < 2:
                        pe_bank_wait(2 * s3)
                        pe_bank_wait(2 * s3 + 1)
                    else:
                        w("tensor", pbc_free_sem, pbc_free)
                        w("tensor", px_free_sem, px_free)
                    w("tensor", "dve", q_evac)

                def fsA(e, P=hp, K=kt, sAP=sAP, o=off):
                    return e.matmul(sAP[:, o:512],
                                    kT[0:64, P * 4096 + K * 128:P * 4096 + (K + 1) * 128],
                                    qam[0:64, P * 512 + o:(P + 1) * 512],
                                    start=True, stop=True)

                def fsB(e, P=hp, K=kt, sAP=sAP, o=off):
                    return e.matmul(sAP[:, 512 + o:1024],
                                    kT[64:128, P * 4096 + K * 128:P * 4096 + (K + 1) * 128],
                                    qam[64:128, P * 512 + o:(P + 1) * 512],
                                    start=True, stop=True)
                sc_a = tick("tensor", "pe", fsA)
                sc_done = tick("tensor", "pe", fsB)
                if kt == 2 and pending_late is not None:
                    pending_late()
                    pending_late = None
                if len(av_q) == 2:
                    emit_av(*av_q.pop(0), KTS=kts)

                if kt == 0 and pending_early1 is not None:
                    pending_early1()
                    pending_early1 = None

                w("scalar", "pe", sc_done)
                w("scalar", "pe", last_av[p])

                if off == 0:
                    def fex(e, pp=p, sAP=sAP):
                        return e.activation(pB[:, pp * 1024:(pp + 1) * 1024],
                                            sAP[:, :],
                                            AF.Exp, scale=ccol_(C_EIGHTH))
                else:
                    def fex(e, pp=p, o=off, sAP=sAP):
                        return e.activation(
                            pB[:, pp * 1024:(pp + 1) * 1024].rearrange(
                                "q (b v) -> q b v", b=2)[:, :, o:],
                            sAP.rearrange("q (b v) -> q b v", b=2)[:, :, o:],
                            AF.Exp, scale=ccol_(C_EIGHTH))
                ex_a = ex_b = tick("scalar", "act", fex)
                last_exp3[s3] = ex_b
                if kt == 0 and pending_early2 is not None:
                    pending_early2()
                    pending_early2 = None

                avwA = avwB = 0
                if m >= 0:
                    w("vector", "act", ex_a)
                    avwA = tick("vector", "dve", lambda e, pp=p, mm=m, o=off: e.tensor_tensor(
                        pB[:, pp * 1024 + o:pp * 1024 + 512],
                        pB[:, pp * 1024 + o:pp * 1024 + 512],
                        bigmask[:, (3 - mm) * 128 + o:(3 - mm) * 128 + 512], ALU.mult))
                    w("vector", "act", ex_b)
                    avwB = tick("vector", "dve", lambda e, pp=p, mm=m, o=off: e.tensor_tensor(
                        pB[:, pp * 1024 + 512 + o:pp * 1024 + 1024],
                        pB[:, pp * 1024 + 512 + o:pp * 1024 + 1024],
                        bigmask[:, (3 - mm) * 128 + o:(3 - mm) * 128 + 512], ALU.mult))

                av_q.append((kt, p, ex_a, ex_b, avwA, avwB, off))
            for entry in av_q:
                emit_av(*entry, KTS=kts)
            av_q = []

            def run_end_early1(P=hp, la=last_av[(kts - 1) % 2]):
                nonlocal zfA, oev
                w("scalar", "pe", la)
                oev = tick("scalar", "act", lambda e: e.activation(
                    oy[:, P * 512:(P + 1) * 512], pO[:, :], AF.Copy))
                on("scalar", lambda e: e.activation(
                    pstat[64:65, :], pstat[64:65, :], AF.Ln))
                zfA = tick("scalar", "act", lambda e: e.activation(
                    sq[64:65, 0:512], pstat[64:65, :], AF.Exp,
                    scale=ccol[64:65, C_NEG1:C_NEG1 + 1]))

            def run_end_early2():
                nonlocal zfB, zrow_free
                on("scalar", lambda e: e.activation(
                    pstat[32:33, :], pstat[32:33, :], AF.Ln))
                zfB = tick("scalar", "act", lambda e: e.activation(
                    sq[32:33, 512:1024], pstat[32:33, :], AF.Exp,
                    scale=ccol[32:33, C_NEG1:C_NEG1 + 1]))
                zrow_free = zfB

            def run_end_late(P=hp):
                nonlocal pstat_free, pstat_free_sem, zb_mm_last, oy_scaled
                w("tensor", "act", zrow_free)
                w("tensor", pstat_free_sem, pstat_free)
                on("tensor", lambda e: e.matmul(
                    pstat[0:64, :], allones[64:65, 0:64], sq[64:65, 0:512],
                    start=True, stop=True, tile_position=(64, 0),
                    skip_group_check=True))
                zb_mm_last = tick("tensor", "pe", lambda e: e.matmul(
                    pstat[64:128, :], allones[32:33, 0:64], sq[32:33, 512:1024],
                    start=True, stop=True, tile_position=(32, 64),
                    skip_group_check=True))
                w("vector", "pe", zb_mm_last)
                w("vector", "act", oev)
                pstat_free = tick("vector", "dve", lambda e: e.tensor_copy(
                    sq[:, 0:512], pstat[:, :]))
                pstat_free_sem = "dve"
                oy_scaled = tick("vector", "dve", lambda e: e.tensor_tensor(
                    oy[:, P * 512:(P + 1) * 512], oy[:, P * 512:(P + 1) * 512],
                    sq[:, 0:512], ALU.mult))
            pending_early1 = run_end_early1
            pending_early2 = run_end_early2
            pending_late = run_end_late
        pending_early1()
        pending_early1 = None
        pending_early2()
        pending_early2 = None
        pending_late()
        pending_late = None

        bank_free[0] = bank_free[1] = last_exp3[0]
        bank_free_sem[0] = bank_free_sem[1] = "act"
        bank_free[2] = bank_free[3] = last_exp3[1]
        bank_free_sem[2] = bank_free_sem[3] = "act"
        pbc_free = last_exp3[2]
        pbc_free_sem = "act"
        px_free = last_exp3[2]
        px_free_sem = "act"

        import os as _os
        if _os.environ.get("DBG_OY") == "1" and c == 0:
            d_oy = nc.dram_tensor("d_oy", [128, 4096], BF16, kind="ExternalOutput").ap()
            w("sync", "dve", oy_scaled)
            tick("sync", "out", lambda e: e.dma_start(out=d_oy, in_=oy[:, :]), 16)

        slot_o0, mo0 = slot_of[(c, "o", 0)]
        slot_o1, mo1 = slot_of[(c, "o", 1)]
        attn_evac = {}
        stats_last = 0
        for oc in range(8):
            slot, mld = (slot_o0, mo0) if oc < 4 else (slot_o1, mo1)
            b = oc % 2
            pe_bank_wait(b)
            w("tensor", mld[0], mld[1])
            w("tensor", "dve", oy_scaled)
            mlast = 0
            for jc in range(NIC):
                def fwo(e, j=jc, s=slot, o=oc, bb=b):
                    return e.matmul(bankF(bb), wap(s, j, o % 4), oy[:, j * 512:(j + 1) * 512],
                                    start=(j == 0), stop=(j == 7))
                if jc == 7:
                    mlast = tick("tensor", "pe", fwo)
                else:
                    on("tensor", lambda e, ff=fwo: ff(e))
            if oc == 3:
                slot_free[slot_o0] = mlast
            if oc == 7:
                slot_free[slot_o1] = mlast
            w("vector", "pe", mlast)
            aev = tick("vector", "dve", lambda e, o=oc, bb=b: e.tensor_copy(
                qam[:, o * 512:(o + 1) * 512], bankF(bb)))
            attn_evac[oc] = aev
            bank_free[b] = aev
            bank_free_sem[b] = "dve"
            sh = oc % 2
            w("vector", "pe", sq_free[sh])
            if oc == 0:
                w("vector", "pe", zb_mm_last)
            msq = tick("vector", "dve", lambda e, o=oc, s=sh: e.tensor_tensor(
                sq[:, s * 512:(s + 1) * 512], qam[:, o * 512:(o + 1) * 512],
                qam[:, o * 512:(o + 1) * 512], ALU.mult))
            w("tensor", "dve", msq)
            if oc == 0:
                w("tensor", pstat_free_sem, pstat_free)

            def fst(e, o=oc, s=sh):
                return e.matmul(pstat[0:1, :], onescol[:, :], sq[:, s * 512:(s + 1) * 512],
                                start=(o == 0), stop=(o == 7))
            stats_last = tick("tensor", "pe", fst)
            sq_free[sh] = stats_last
        ln3 = ln_rows(stats_last, has_mean=False)
        stream_half(c, "cos", 0)
        stream_half(c, "cos", 1)

        bcast(RB_R, bc1[:, :], ln3)
        rowsB_free_pe = last_bcast_pe
        stats_last = 0
        for ic in range(NIC):
            sh = ic % 2
            w("vector", "dve", attn_evac[ic])
            if ic == 0:
                w("vector", "lds", m_xf)
            on("vector", lambda e, i=ic: e.tensor_tensor(
                pO[:, 0:512], qam[:, i * 512:(i + 1) * 512], bc1[:, :], ALU.mult))
            tick("vector", "dve", lambda e, i=ic: e.tensor_tensor(
                xf[:, i * 512:(i + 1) * 512], xf[:, i * 512:(i + 1) * 512],
                pO[:, 0:512], ALU.add))
            w("scalar", "pe", sq_free[sh])
            w("scalar", "dve", n["dve"])
            mact = tick("scalar", "act", lambda e, i=ic, s=sh: e.activation(
                sq[:, s * 512:(s + 1) * 512], xf[:, i * 512:(i + 1) * 512],
                AF.Square))
            w("tensor", "act", mact)
            if ic == 0:
                w("tensor", pbc_free_sem, pbc_free)
                w("tensor", pstat_free_sem, pstat_free)

            def fsq2(e, s=sh, i=ic):
                return e.matmul(pbc[0:1, :], onescol[:, :], sq[:, s * 512:(s + 1) * 512],
                                start=(i == 0), stop=(i == 7))
            sq_free[sh] = tick("tensor", "pe", fsq2)
            f2 = lambda e, i=ic: e.matmul(
                pstat[0:1, :], onescol_f[:, :], xf[:, i * 512:(i + 1) * 512],
                start=(i == 0), stop=(i == 7))
            stats_last = tick("tensor", "pe", f2)
        ln2 = ln_rows(stats_last, has_mean=True)
        bcast(RB_R, bc1[:, :], ln2)
        w("vector", "pe", last_bcast_pe)
        w("vector", "rc", r1_store)
        mlo = tick("vector", "dve", lambda e: e.scalar_tensor_tensor(
            RB_R, RB_MR, crow_(C_NEG1), RFA, ALU.mult, ALU.add))
        w("tensor", pbc_free_sem, pbc_free)
        w("tensor", "dve", mlo)
        mhb = tick("tensor", "pe", lambda e: e.matmul(
            pbc[:, :], allones[0:1, :], RB_MR, start=True, stop=True))
        last_bcast_pe = mhb
        w("vector", "pe", mhb)
        pbc_free = tick("vector", "dve", lambda e: e.tensor_scalar(
            pB[:, 512:1024], pbc[:, :], ccol_(C_NEG1), None, ALU.mult))
        yn_done = 0
        for ic in range(NIC):
            on("vector", lambda e, i=ic: e.tensor_tensor(
                pO[:, 0:512], xf[:, i * 512:(i + 1) * 512], bc1[:, :], ALU.mult))
            yn_done = tick("vector", "dve", lambda e, i=ic: e.tensor_tensor(
                oy[:, i * 512:(i + 1) * 512], pO[:, 0:512], pB[:, 512:1024],
                ALU.add))

        mag_done = {-1: 0}
        stats_sq_last = 0
        real_evac = {}
        ffn_last = {}
        for ch in range(2):
            slot_c, mc_ = slot_of[(c, "cos", ch)]
            for oc in range(ch * 4, ch * 4 + 4):
                bb = oc % 4
                pe_bank_wait(bb)
                w("tensor", mc_[0], mc_[1])
                if oc == ch * 4:
                    w("tensor", "dve", yn_done)
                for ic in range(NIC):
                    on("tensor", lambda e, i=ic, s=slot_c, o=oc, bk=bb: e.matmul(
                        bankF(bk), wap(s, i, o % 4), oy[:, i * 512:(i + 1) * 512],
                        start=(i == 0), stop=False))

                def fmean3(e, o=oc, bk=bb):
                    return e.matmul(bankF(bk), WS["cos"][:, o * 128:(o + 1) * 128], RB_R,
                                    start=False, stop=True)
                ffn_last[("r", oc)] = tick("tensor", "pe", fmean3)
                w("vector", "pe", ffn_last[("r", oc)])
                real_evac[oc] = tick("vector", "dve", lambda e, o=oc, bk=bb: e.tensor_copy(
                    qam[:, o * 512:(o + 1) * 512], bankF(bk)))
                bank_free[bb] = real_evac[oc]
                bank_free_sem[bb] = "dve"
            if ch == 0:
                slot_free[slot_of[(c, "cos", 0)][0]] = ffn_last[("r", 3)]
                stream_half(c, "sin", 0)
            else:
                slot_free[slot_of[(c, "cos", 1)][0]] = ffn_last[("r", 7)]
                stream_half(c, "sin", 1)
        pend_mean4 = None
        mean4_pe = 0

        def emit_mean4(o, mact):
            nonlocal mean4_pe
            w("tensor", "act", mact)
            if o == 0:
                w("tensor", pstat_free_sem, pstat_free)
            mean4_pe = tick("tensor", "pe", lambda e, o=o: e.matmul(
                pstat[0:1, :], onescol[:, :], qam[:, o * 512:(o + 1) * 512],
                start=(o == 0), stop=(o == 7)))

        for ch in range(2):
            slot_s, ms_ = slot_of[(c, "sin", ch)]
            for oc in range(ch * 4, ch * 4 + 4):
                bb = oc % 4
                pe_bank_wait(bb)
                w("tensor", ms_[0], ms_[1])
                for ic in range(NIC):
                    on("tensor", lambda e, i=ic, s=slot_s, o=oc, bk=bb: e.matmul(
                        bankF(bk), wap(s, i, o % 4), oy[:, i * 512:(i + 1) * 512],
                        start=(i == 0), stop=False))

                def fmean4(e, o=oc, bk=bb):
                    return e.matmul(bankF(bk), WS["sin"][:, o * 128:(o + 1) * 128], RB_R,
                                    start=False, stop=True)
                ffn_last[("i", oc)] = tick("tensor", "pe", fmean4)
                if pend_mean4 is not None:
                    emit_mean4(*pend_mean4)
                    pend_mean4 = None
                w("vector", "pe", ffn_last[("i", oc)])
                w("vector", "pe", sq_free[0])
                w("vector", "act", mag_done[oc - 1])
                on("vector", lambda e, bk=bb: e.tensor_copy(
                    pB[:, 0:512], bankF(bk)))
                on("vector", lambda e, o=oc: e.tensor_tensor(
                    sq[:, 0:512], qam[:, o * 512:(o + 1) * 512],
                    qam[:, o * 512:(o + 1) * 512], ALU.mult))
                on("vector", lambda e: e.tensor_tensor(
                    sq[:, 512:1024], pB[:, 0:512], pB[:, 0:512], ALU.mult))
                m2 = tick("vector", "dve", lambda e: e.tensor_tensor(
                    pO[:, 0:512], sq[:, 0:512], sq[:, 512:1024], ALU.add))
                bank_free[bb] = m2
                bank_free_sem[bb] = "dve"
                w("scalar", "dve", m2)
                on("scalar", lambda e: e.activation(pO[:, 0:512], pO[:, 0:512], AF.Ln))
                mag_act = tick("scalar", "act", lambda e, o=oc: e.activation(
                    qam[:, o * 512:(o + 1) * 512], pO[:, 0:512], AF.Exp,
                    scale=ccol_(C_HALF)))
                mag_done[oc] = mag_act
                pend_mean4 = (oc, mag_act)
            if ch == 0:
                slot_free[slot_of[(c, "sin", 0)][0]] = ffn_last[("i", 3)]
            else:
                slot_free[slot_of[(c, "sin", 1)][0]] = ffn_last[("i", 7)]
        emit_mean4(*pend_mean4)
        pend_mean4 = None
        w("vector", "pe", mean4_pe)
        mrow = tick("vector", "dve", lambda e: e.tensor_scalar(
            RB_R, pstat[0:1, :], crow_(C_MI1024), None, ALU.mult))
        bcast(RB_R, bc1[:, :], mrow)
        w("vector", "act", mag_done[7])
        stats_sq_last = 0
        for oc in range(NIC):
            sh = oc % 2
            w("vector", "pe", sq_free[sh])
            on("vector", lambda e, o=oc: e.tensor_tensor(
                pB[:, 0:512], qam[:, o * 512:(o + 1) * 512], bc1[:, :], ALU.add))
            mdve = tick("vector", "dve", lambda e, o=oc, s=sh: e.tensor_tensor(
                sq[:, s * 512:(s + 1) * 512], pB[:, 0:512],
                qam[:, o * 512:(o + 1) * 512], ALU.mult))
            w("tensor", "dve", mdve)
            if oc == 0:
                w("tensor", pbc_free_sem, pbc_free)
            stats_sq_last = tick("tensor", "pe", lambda e, o=oc, s=sh: e.matmul(
                pbc[0:1, :], onescol[:, :], sq[:, s * 512:(s + 1) * 512],
                start=(o == 0), stop=(o == 7)))
            sq_free[sh] = stats_sq_last
        rowsB_free_pe = n["pe"]
        ln4 = ln_rows(stats_sq_last, has_mean=True, centered=True)

        bcast(RB_R, bc4[:, :], ln4)
        w("tensor", pbc_free_sem, pbc_free)
        w("tensor", "dve", ln4)
        mb4 = tick("tensor", "pe", lambda e: e.matmul(
            pbc[:, :], allones[0:1, :], RB_MR, start=True, stop=True))
        last_bcast_pe = mb4
        w("vector", "pe", mb4)
        pbc_free = tick("vector", "dve", lambda e: e.tensor_scalar(
            bcf[:, :], pbc[:, :], ccol_(C_NEG1), None, ALU.mult))
        rowsB_free_pe = last_bcast_pe
        res2_done = 0
        for ic in range(NIC):
            w("vector", "act", mag_done[ic])
            on("vector", lambda e, i=ic: e.tensor_tensor(
                pO[:, 0:512], qam[:, i * 512:(i + 1) * 512], bc4[:, :], ALU.mult))
            on("vector", lambda e, i=ic: e.tensor_tensor(
                xf[:, i * 512:(i + 1) * 512], xf[:, i * 512:(i + 1) * 512],
                pO[:, 0:512], ALU.add))
            res2_done = tick("vector", "dve", lambda e, i=ic: e.tensor_tensor(
                xf[:, i * 512:(i + 1) * 512], xf[:, i * 512:(i + 1) * 512],
                bcf[:, :], ALU.add))
        res2_prev = res2_done
        w("sync", "dve", res2_done)
        tick("sync", "out", lambda e, cc=c: e.dma_start(
            out=yT.rearrange("(ic p) t -> p ic t", p=128)[:, :, cc * TC:(cc + 1) * TC],
            in_=xf[:, :].rearrange("p (ic t) -> p ic t", ic=8)), 16)
        out_hist.append(n["out"])

    w("sync", "out", n["out"])

    with nc.Block() as blk:
        blk.tensor(lambda e: [f(e) for f in q["tensor"]])
        blk.scalar(lambda e: [f(e) for f in q["scalar"]])
        blk.vector(lambda e: [f(e) for f in q["vector"]])
        blk.gpsimd(lambda e: [f(e) for f in q["gpsimd"]])
        blk.sync(lambda e: [f(e) for f in q["sync"]])
    return nc


def kernel(L0_attn_x, L0_attn_Wq, L0_attn_Wk, L0_attn_Wv, L0_attn_Wo,
           L0_ffn_theta, L0_ffn_phi, L0_ffn_theta_noise, L0_ffn_phi_noise):
    x = np.asarray(L0_attn_x, dtype=np.float32)
    wq = np.ascontiguousarray(np.asarray(L0_attn_Wq, dtype=np.float32).T)
    wk = np.ascontiguousarray(np.asarray(L0_attn_Wk, dtype=np.float32).T)
    wv = np.ascontiguousarray(np.asarray(L0_attn_Wv, dtype=np.float32).T)
    wo = np.ascontiguousarray(np.asarray(L0_attn_Wo, dtype=np.float32).T)
    th = np.ascontiguousarray(np.asarray(L0_ffn_theta, dtype=np.float32))
    ph = np.ascontiguousarray(np.asarray(L0_ffn_phi, dtype=np.float32))
    thn = np.ascontiguousarray(np.asarray(L0_ffn_theta_noise, dtype=np.float32))
    phn = np.ascontiguousarray(np.asarray(L0_ffn_phi_noise, dtype=np.float32))

    bigmask = (np.arange(128)[:, None] <= (np.arange(896)[None, :] - 384)
               ).astype(BFNP)
    onescol = np.ones((128, 1), dtype=BFNP)
    allones = np.ones((128, 128), dtype=BFNP)

    nc = build_nc()
    in_maps = []
    for b in range(NCORE):
        in_maps.append({
            "xT": np.ascontiguousarray(x[b].T),
            "wkT": wk, "wvT": wv, "wqT": wq, "woT": wo,
            "theta": th, "theta_n": thn, "phi": ph, "phi_n": phn,
            "bigmask": bigmask, "onescol": onescol, "allones": allones,
        })
    res = run_bass_kernel_spmd(nc, in_maps, core_ids=list(range(NCORE)))
    out = np.stack([np.asarray(res.results[b]["yT"], dtype=np.float32).T
                    for b in range(NCORE)], axis=0)
    return out


# version 25
# speedup vs baseline: 1.0229x; 1.0229x over previous
import numpy as np
import ml_dtypes
import concourse.bass as bass
import concourse.mybir as mybir
from concourse.bass_utils import run_bass_kernel_spmd

F32 = mybir.dt.float32
BF16 = mybir.dt.bfloat16
AF = mybir.ActivationFunctionType
ALU = mybir.AluOpType
AX = mybir.AxisListType.X
BFNP = ml_dtypes.bfloat16

D = 1024
T = 4096
NCH = 8
TC = 512
NIC = 8
NHP = 8
NCORE = 8

(C_INV1024, C_EIGHTH, C_M05, C_THREE, C_NEG1, C_HALF, C_EPS, C_M18, C_PI2,
 C_MI1024) = range(10)


def build_nc(detect_races=True):
    nc = bass.Bass("TRN2", target_bir_lowering=False, debug=False,
                   dynamic_dma_scratch_size=16384,
                   detect_race_conditions=detect_races)

    xT = nc.dram_tensor("xT", [D, T], F32, kind="ExternalInput").ap()
    w_d = {
        "k": nc.dram_tensor("wkT", [D, D], F32, kind="ExternalInput").ap(),
        "v": nc.dram_tensor("wvT", [D, D], F32, kind="ExternalInput").ap(),
        "q": nc.dram_tensor("wqT", [D, D], F32, kind="ExternalInput").ap(),
        "o": nc.dram_tensor("woT", [D, D], F32, kind="ExternalInput").ap(),
    }
    th_d = nc.dram_tensor("theta", [D, D], F32, kind="ExternalInput").ap()
    thn_d = nc.dram_tensor("theta_n", [D, D], F32, kind="ExternalInput").ap()
    ph_d = nc.dram_tensor("phi", [D, D], F32, kind="ExternalInput").ap()
    phn_d = nc.dram_tensor("phi_n", [D, D], F32, kind="ExternalInput").ap()
    bigmask_d = nc.dram_tensor("bigmask", [128, 896], BF16, kind="ExternalInput").ap()
    onescol_d = nc.dram_tensor("onescol", [128, 1], BF16, kind="ExternalInput").ap()
    allones_d = nc.dram_tensor("allones", [128, 128], BF16, kind="ExternalInput").ap()
    yT = nc.dram_tensor("yT", [D, T], F32, kind="ExternalOutput").ap()

    WIDX = {"k": 0, "v": 1, "q": 2, "o": 3, "cos": 4, "sin": 5}
    wscr = [nc.dram_tensor(f"wscr{i}", [128, 8192], BF16, kind="Internal").ap()
            for i in range(6)]
    rscr_d = nc.dram_tensor("rscr", [1, 512], BF16, kind="Internal").ap()

    A = nc.alloc_sbuf_tensor
    kT = A("kT", [128, 32768], BF16)
    vT = A("vT", [128, 32768], BF16)
    xf = A("xf", [128, 4096], F32)
    xb = A("xb", [128, 4096], BF16)
    qam = A("qam", [128, 4096], BF16)
    oy = A("oy", [128, 4096], BF16)
    pB = A("pB", [128, 2048], BF16)
    wsl = A("wsl", [128, 8192], BF16)
    bigmask = A("bigmask_s", [128, 896], BF16)
    bc1 = A("bc1", [128, 512], BF16)
    rowsF = A("rowsF", [128, 512], F32)
    rowsB = A("rowsB", [128, 1024], BF16)
    wsums = A("wsums", [128, 5120], BF16)
    sq = A("sq", [128, 1024], BF16)
    onescol = A("onescol_s", [128, 1], BF16)
    onescol_f = A("onescol_f", [128, 1], F32)
    allones = A("allones_s", [128, 128], BF16)
    ccol = A("ccol", [128, 10], F32)
    erow = A("erow", [128, 16], F32)
    ecol4 = A("ecol4", [128, 4], F32)

    bcf = nc.alloc_sbuf_tensor_at("bcf", [128, 512], F32,
                                  offset=nc.lookup_mloc(pB).addr)
    bc4 = nc.alloc_sbuf_tensor_at("bc4", [128, 512], BF16,
                                  offset=nc.lookup_mloc(pB).addr + 2048)
    kaddr = nc.lookup_mloc(kT).addr
    vaddr = nc.lookup_mloc(vT).addr
    stage_f = nc.alloc_sbuf_tensor_at("stage_f", [128, 8192], F32, offset=kaddr)
    stage_b = nc.alloc_sbuf_tensor_at("stage_b", [128, 8192], BF16, offset=kaddr + 32768)
    stage2_f = nc.alloc_sbuf_tensor_at("stage2_f", [128, 8192], F32, offset=vaddr)
    stage_b2 = nc.alloc_sbuf_tensor_at("stage_b2", [128, 8192], BF16,
                                       offset=vaddr + 32768)

    parena = nc.alloc_psum_tensor("parena", [128, 2048], F32)
    pO = nc.alloc_psum_tensor("pO", [128, 512], F32)
    pstat = nc.alloc_psum_tensor("pstat", [128, 512], F32)
    parena2 = nc.alloc_psum_tensor("parena2", [128, 1024], F32)
    pbc = parena2[:, 0:512]
    px = parena2[:, 512:1024]

    SETS = [parena[:, 0:1024], parena[:, 1024:2048], parena2[:, 0:1024]]

    sem_names = ["pe", "act", "dve", "ldg", "lds", "out",
                 "ld0", "ld1", "ldx", "ldn", "lsw", "lsw2", "rc", "rc2"]
    sems = {k: nc.alloc_semaphore(k) for k in sem_names}
    q = {k: [] for k in ("tensor", "scalar", "vector", "gpsimd", "sync")}
    n = {k: 0 for k in sem_names}

    def on(eng, f):
        q[eng].append(f)

    def w(eng, key, v):
        if v and v > 0:
            on(eng, lambda e, s=sems[key], vv=v: e.wait_ge(s, vv))

    def tick(eng, key, f, k=1):
        n[key] += k
        on(eng, lambda e, ff=f, s=sems[key], kk=k: ff(e).then_inc(s, kk))
        return n[key]

    def ccol_(i):
        return ccol[:, i:i + 1]

    def crow_(i):
        return ccol[0:1, i:i + 1]

    RFA = bcf[0:1, 0:512]
    RFB = rowsF[0:1, 0:512]
    RB_R = rowsB[0:1, 0:512]
    RB_MR = rowsB[0:1, 512:1024]

    WS = {"k": wsums[0:1, 0:1024], "v": wsums[0:1, 1024:2048],
          "q": wsums[0:1, 2048:3072], "cos": wsums[0:1, 3072:4096],
          "sin": wsums[0:1, 4096:5120]}

    tick("sync", "lds", lambda e: e.dma_start(out=bigmask[:, :], in_=bigmask_d), 16)
    tick("sync", "lds", lambda e: e.dma_start(out=onescol[:, :], in_=onescol_d), 16)
    m_consts = tick("sync", "lds", lambda e: e.dma_start(out=allones[:, :], in_=allones_d), 16)

    for idx, val in [(C_INV1024, 1.0 / 1024), (C_EIGHTH, 0.125), (C_M05, -0.5),
                     (C_THREE, 3.0), (C_NEG1, -1.0), (C_HALF, 0.5), (C_EPS, 1e-5),
                     (C_PI2, float(np.pi / 2)), (C_MI1024, -1.0 / 1024)]:
        on("vector", lambda e, i=idx, v=val: e.memset(ccol[:, i:i + 1], v))
    on("vector", lambda e: e.memset(onescol_f[:, :], 1.0))
    m_memset = tick("vector", "dve",
                    lambda e: e.memset(ccol[:, C_M18:C_M18 + 1], -0.125))

    w("tensor", "lds", m_consts)
    w("scalar", "dve", m_memset)

    stageB = [stage_b, stage_b2]
    ssem = ["lsw", "lsw2"]
    buf_free_pe = [0, 0]
    buf_free_st = [0, 0]
    wscr_ready = {}

    prev_wsum_dve = [0]
    wsum_pe_last = [0]

    def startup_wsum(name, sb):
        w("tensor", "dve", prev_wsum_dve[0])
        for jh in range(2):
            dst = pstat[0:1, :] if jh == 0 else pbc[0:1, :]
            for ic in range(NIC):
                def f(e, d=dst, i=ic, j=jh, sb=sb):
                    return e.matmul(d, onescol[:, :],
                                    sb[:, i * 1024 + j * 512: i * 1024 + (j + 1) * 512],
                                    start=(i == 0), stop=(i == 7))
                if ic == 7 and jh == 1:
                    wsum_pe_last[0] = tick("tensor", "pe", f)
                else:
                    on("tensor", lambda e, ff=f: ff(e))
        w("vector", "pe", wsum_pe_last[0])
        on("vector", lambda e, nm=name: e.tensor_scalar(
            WS[nm][:, 0:512], pstat[0:1, :], crow_(C_NEG1), None, ALU.mult))
        mdone = tick("vector", "dve", lambda e, nm=name: e.tensor_scalar(
            WS[nm][:, 512:1024], pbc[0:1, :], crow_(C_NEG1), None, ALU.mult))
        prev_wsum_dve[0] = mdone
        return mdone

    wsum_dve = 0
    bi = 0
    for name in ["k", "v", "q"]:
        sb = stageB[bi]
        w("gpsimd", "pe", buf_free_pe[bi])
        w("gpsimd", ssem[bi], buf_free_st[bi])
        m_cast = tick("gpsimd", "ldg", lambda e, s=w_d[name], sb=sb: e.dma_start(
            out=sb[:, :], in_=s.rearrange("(ic p) j -> p ic j", p=128)), 16)
        w("tensor", "ldg", m_cast)
        wsum_dve = startup_wsum(name, sb)
        buf_free_pe[bi] = wsum_pe_last[0]
        w("sync", "dve", wsum_dve)
        buf_free_st[bi] = tick("sync", ssem[bi], lambda e, wi=WIDX[name], sb=sb:
                               e.dma_start(out=wscr[wi], in_=sb[:, :]), 16)
        wscr_ready[name] = (ssem[bi], buf_free_st[bi])
        bi ^= 1

    sb = stageB[bi]
    w("gpsimd", "pe", buf_free_pe[bi])
    w("gpsimd", ssem[bi], buf_free_st[bi])
    m_cast = tick("gpsimd", "ldg", lambda e, sb=sb: e.dma_start(
        out=sb[:, :], in_=w_d["o"].rearrange("(ic p) j -> p ic j", p=128)), 16)
    w("vector", "ldg", m_cast)
    wo_center = 0
    for ic in range(NIC):
        on("vector", lambda e, i=ic, sb=sb: e.tensor_reduce(
            erow[:, 8:9], sb[:, i * 1024:(i + 1) * 1024], AX, ALU.add))
        mneg = tick("vector", "dve", lambda e: e.tensor_scalar(
            erow[:, 8:9], erow[:, 8:9], ccol_(C_MI1024), None, ALU.mult))
        w("vector", "dve", mneg)
        wo_center = tick("vector", "dve", lambda e, i=ic, sb=sb: e.tensor_scalar(
            sb[:, i * 1024:(i + 1) * 1024],
            sb[:, i * 1024:(i + 1) * 1024], erow[:, 8:9], None, ALU.add))
    w("sync", "dve", wo_center)
    buf_free_st[bi] = tick("sync", ssem[bi], lambda e, sb=sb: e.dma_start(
        out=wscr[WIDX["o"]], in_=sb[:, :]), 16)
    wscr_ready["o"] = (ssem[bi], buf_free_st[bi])
    bi ^= 1

    on("scalar", lambda e: e.preload_act_table(AF.Sin))
    prev_sin = 0
    prev_add = 0
    for name, base_d, noise_d, use_cos in [("cos", th_d, thn_d, True),
                                           ("sin", ph_d, phn_d, False)]:
        sb = stageB[bi]
        w("sync", "act", prev_sin)
        mf = tick("sync", "lds", lambda e, s=base_d: e.dma_start(
            out=stage_f[:, :], in_=s.rearrange("(ic p) j -> p ic j", p=128)), 16)
        w("gpsimd", "dve", prev_add)
        mg = tick("gpsimd", "ldn", lambda e, s=noise_d: e.dma_start(
            out=stage2_f[:, :], in_=s.rearrange("(ic p) j -> p ic j", p=128)), 16)
        w("vector", "lds", mf)
        w("vector", "ldn", mg)
        add_done = tick("vector", "dve", lambda e: e.tensor_tensor(
            stage_f[:, :], stage_f[:, :], stage2_f[:, :], ALU.add))
        prev_add = add_done
        w("scalar", "dve", add_done)
        w("scalar", ssem[bi], buf_free_st[bi])
        w("scalar", "pe", buf_free_pe[bi])
        if use_cos:
            sin_done = tick("scalar", "act", lambda e, sb=sb: e.activation(
                sb[:, :], stage_f[:, :], AF.Sin, bias=ccol_(C_PI2)))
        else:
            sin_done = tick("scalar", "act", lambda e, sb=sb: e.activation(
                sb[:, :], stage_f[:, :], AF.Sin))
        prev_sin = sin_done
        w("tensor", "act", sin_done)
        wsum_dve = startup_wsum(name, sb)
        buf_free_pe[bi] = wsum_pe_last[0]
        w("sync", "dve", wsum_dve)
        w("sync", "act", sin_done)
        buf_free_st[bi] = tick("sync", ssem[bi], lambda e, wi=WIDX[name], sb=sb:
                               e.dma_start(out=wscr[wi], in_=sb[:, :]), 16)
        wscr_ready[name] = (ssem[bi], buf_free_st[bi])
        bi ^= 1

    startup_lsw = buf_free_st[0]
    startup_lsw2 = buf_free_st[1]
    startup_b0_pe = buf_free_pe[0]
    startup_wsum_dve = wsum_dve

    startup_sin_act = prev_sin

    from concourse.hw_specs import get_activation_tables
    tables = list(get_activation_tables(nc.m.arch).keys())
    nle_idx = tables.index("natural_log_exp_and_others")
    on("scalar", lambda e, idx=nle_idx: e.add_instruction(mybir.InstLoadActFuncSet(
        name=e.bass.get_next_instruction_name(), act_func_set_id=idx,
        ins=[], outs=[])))

    half_seq = 0
    slot_free = [0, 0]
    slot_of = {}

    def stream_half(c, name, h):
        nonlocal half_seq
        slot = half_seq % 2
        half_seq += 1
        sem = "ld0" if slot == 0 else "ld1"
        w("gpsimd", "pe", slot_free[slot])
        w("gpsimd", *wscr_ready[name])
        src = wscr[WIDX[name]].rearrange("p (ic hh j) -> p ic hh j", ic=8, hh=2)[:, :, h, :]
        dst = wsl[:, slot * 4096:(slot + 1) * 4096].rearrange("p (ic j) -> p ic j", ic=8)
        m = tick("gpsimd", sem, lambda e, s=src, d=dst: e.dma_start(out=d, in_=s), 16)
        slot_of[(c, name, h)] = (slot, (sem, m))

    def wap(slot, ic, sub):
        return wsl[:, slot * 4096 + ic * 512 + sub * 128: slot * 4096 + ic * 512 + (sub + 1) * 128]

    def wap512(slot, ic):
        return wsl[:, slot * 4096 + ic * 512: slot * 4096 + (ic + 1) * 512]

    bankF = lambda b: parena[:, b * 512:(b + 1) * 512]
    bank_free = [0, 0, 0, 0]
    bank_free_sem = ["dve", "dve", "dve", "dve"]
    pstat_free = startup_wsum_dve
    pstat_free_sem = "dve"
    px_free_sem = "dve"
    pbc_free = startup_wsum_dve
    pbc_free_sem = "dve"
    px_free = 0
    oev = 0
    oy_scaled = 0
    p3_evac_last = 0
    r1_store = 0
    rowsB_free_pe = 0
    last_bcast_pe = 0
    last_exp3 = [0, 0, 0]
    last_av = {0: 0, 1: 0}
    res2_prev = 0
    zrow_free = 0
    zfA = 0
    zfB = 0
    zb_mm_last = 0
    m_xb_next = None
    sq_free = [0, 0]

    def pe_bank_wait(b):
        w("tensor", bank_free_sem[b], bank_free[b])

    def ln_rows(stats_pe, has_mean, centered=False, fold_r=True):
        nonlocal pstat_free, pbc_free, pstat_free_sem, pbc_free_sem
        pstat_free_sem = "dve"
        pbc_free_sem = "dve"
        w("vector", "pe", stats_pe)
        w("vector", "pe", rowsB_free_pe)
        if has_mean and centered:
            on("vector", lambda e: e.tensor_scalar(RFA, pstat[0:1, :], crow_(C_INV1024), None, ALU.mult))
            mark = tick("vector", "dve", lambda e: e.tensor_scalar(
                RFB, pbc[0:1, :], crow_(C_INV1024), None, ALU.mult))
        elif has_mean:
            on("vector", lambda e: e.tensor_scalar(RFA, pstat[0:1, :], crow_(C_INV1024), None, ALU.mult))
            on("vector", lambda e: e.scalar_tensor_tensor(
                RFB, RFA, crow_(C_NEG1), RFA, ALU.mult, ALU.mult))
            mark = tick("vector", "dve", lambda e: e.scalar_tensor_tensor(
                RFB, pbc[0:1, :], crow_(C_INV1024), RFB, ALU.mult, ALU.add))
        else:
            mark = tick("vector", "dve", lambda e: e.tensor_scalar(
                RFB, pstat[0:1, :], crow_(C_INV1024), None, ALU.mult))
        pstat_free = mark
        pbc_free = mark
        w("scalar", "dve", mark)
        on("scalar", lambda e: e.activation(RFB, RFB, AF.Ln, bias=crow_(C_EPS)))
        r_act = tick("scalar", "act", lambda e: e.activation(RFB, RFB, AF.Exp, scale=crow_(C_M05)))
        w("vector", "act", r_act)
        if has_mean:
            if fold_r:
                on("vector", lambda e: e.tensor_tensor(RFA, RFA, RFB, ALU.mult))
            on("vector", lambda e: e.tensor_copy(RB_R, RFB))
            return tick("vector", "dve", lambda e: e.tensor_copy(RB_MR, RFA))
        else:
            return tick("vector", "dve", lambda e: e.tensor_copy(RB_R, RFB))

    def bcast(row_ap, dst_ap, row_ready_dve):
        nonlocal pbc_free, last_bcast_pe, pbc_free_sem
        w("tensor", pbc_free_sem, pbc_free)
        pbc_free_sem = "dve"
        w("tensor", "dve", row_ready_dve)
        m = tick("tensor", "pe", lambda e, r=row_ap: e.matmul(
            pbc[:, :], allones[0:1, :], r, start=True, stop=True))
        last_bcast_pe = m
        w("vector", "pe", m)
        pbc_free = tick("vector", "dve", lambda e, d=dst_ap: e.tensor_copy(d, pbc[:, :]))
        return pbc_free

    out_hist = [0]
    for c in range(NCH):
        w("sync", "out", out_hist[c])
        m_xf = tick("sync", "lds", lambda e, cc=c: e.dma_start(
            out=xf[:, :].rearrange("p (ic t) -> p ic t", ic=8),
            in_=xT.rearrange("(ic p) t -> p ic t", p=128)[:, :, cc * TC:(cc + 1) * TC]), 16)
        if c == 0:
            m_xb = tick("gpsimd", "ldx", lambda e, cc=c: e.dma_start(
                out=xb[:, :].rearrange("p (ic t) -> p ic t", ic=8),
                in_=xT.rearrange("(ic p) t -> p ic t", p=128)[:, :, cc * TC:(cc + 1) * TC]), 16)
        else:
            m_xb = m_xb_next

        stream_half(c, "k", 0)
        stream_half(c, "k", 1)

        if c == 0:
            w("vector", "act", startup_sin_act)
            w("vector", "lsw", startup_lsw)
            w("vector", "lsw2", startup_lsw2)
            w("vector", "pe", startup_b0_pe)
        w("scalar", "ldx", m_xb)
        stats_last = 0
        for ic in range(NIC):
            sh = ic % 2
            w("scalar", "pe", sq_free[sh])
            mact = tick("scalar", "act", lambda e, i=ic, s=sh: e.activation(
                sq[:, s * 512:(s + 1) * 512], xb[:, i * 512:(i + 1) * 512],
                AF.Square))
            w("tensor", "ldx", m_xb)
            w("tensor", "act", mact)
            if ic == 0:
                w("tensor", pstat_free_sem, pstat_free)
                w("tensor", pbc_free_sem, pbc_free)
            on("tensor", lambda e, i=ic: e.matmul(
                pstat[0:1, :], onescol[:, :], xb[:, i * 512:(i + 1) * 512],
                start=(i == 0), stop=(i == 7)))

            def fsq(e, s=sh, i=ic):
                return e.matmul(pbc[0:1, :], onescol[:, :], sq[:, s * 512:(s + 1) * 512],
                                start=(i == 0), stop=(i == 7))
            stats_last = tick("tensor", "pe", fsq)
            sq_free[sh] = stats_last
        ln1 = ln_rows(stats_last, has_mean=True, fold_r=False)
        bcast(RB_R, bc1[:, :], ln1)
        w("sync", "dve", ln1)
        w("sync", "dve", p3_evac_last)
        r1_store = tick("sync", "rc", lambda e: e.dma_start(
            out=rscr_d, in_=RB_R[0:1, 0:512]), 16)
        w("gpsimd", "rc", r1_store)
        mrc = tick("gpsimd", "rc2", lambda e: e.dma_start(
            out=ecol4[:, :], in_=rscr_d.rearrange("o (s p) -> (o p) s", p=128),
            allow_slow_non_contiguous=True), 16)

        slot_k0, mk0 = slot_of[(c, "k", 0)]
        slot_k1, mk1 = slot_of[(c, "k", 1)]
        for jc in range(8):
            slot, mld = (slot_k0, mk0) if jc < 4 else (slot_k1, mk1)
            b = jc % 2
            pe_bank_wait(b)
            w("tensor", mld[0], mld[1])
            w("tensor", "ldx", m_xb)
            for ic in range(NIC):
                on("tensor", lambda e, i=ic, s=slot, j=jc, bb=b: e.matmul(
                    bankF(bb), wap(s, i, j % 4), xb[:, i * 512:(i + 1) * 512],
                    start=(i == 0), stop=False))

            if jc == 0:
                w("tensor", "dve", ln1)

            def fmean(e, j=jc, bb=b):
                return e.matmul(bankF(bb), WS["k"][:, j * 128:(j + 1) * 128], RB_MR,
                                start=False, stop=True)
            mlast = tick("tensor", "pe", fmean)
            if jc == 3:
                slot_free[slot_k0] = mlast
            if jc == 7:
                slot_free[slot_k1] = mlast
            w("vector", "pe", mlast)
            ev = tick("vector", "dve", lambda e, j=jc, cc=c, bb=b: e.tensor_tensor(
                kT[:, j * 4096 + cc * TC: j * 4096 + (cc + 1) * TC], bankF(bb),
                bc1[:, :], ALU.mult))
            bank_free[b] = ev
            bank_free_sem[b] = "dve"
        stream_half(c, "v", 0)
        stream_half(c, "v", 1)

        slot_v0, mv0 = slot_of[(c, "v", 0)]
        slot_v1, mv1 = slot_of[(c, "v", 1)]
        for jh in range(2):
            slot, mld = (slot_v0, mv0) if jh == 0 else (slot_v1, mv1)
            for st in range(4):
                b = 2 + st % 2
                pe_bank_wait(b)
                w("tensor", mld[0], mld[1])
                for ic in range(NIC):
                    on("tensor", lambda e, i=ic, s=slot, t=st, bb=b: e.matmul(
                        bankF(bb), xb[:, i * 512 + t * 128: i * 512 + (t + 1) * 128],
                        wap512(s, i), start=(i == 0), stop=False))

                if jh == 0 and st == 0:
                    w("tensor", "dve", ln1)

                def fmeanv(e, t=st, j=jh, bb=b):
                    return e.matmul(bankF(bb), RB_MR[:, t * 128:(t + 1) * 128],
                                    WS["v"][:, j * 512:(j + 1) * 512], start=False, stop=True)
                mlast = tick("tensor", "pe", fmeanv)
                if st == 3:
                    if jh == 0:
                        slot_free[slot_v0] = mlast
                    else:
                        slot_free[slot_v1] = mlast
                w("vector", "pe", mlast)
                if jh == 0 and st == 0:
                    w("vector", "rc2", mrc)
                ev = tick("vector", "dve", lambda e, t=st, j=jh, cc=c, bb=b: e.tensor_scalar(
                    vT[:, (cc * 4 + t) * 1024 + j * 512: (cc * 4 + t) * 1024 + (j + 1) * 512],
                    bankF(bb), ecol4[:, t:t + 1], None, ALU.mult))
                bank_free[b] = ev
                bank_free_sem[b] = "dve"
        p3_evac_last = n["dve"]
        stream_half(c, "q", 0)
        stream_half(c, "q", 1)

        slot_q0, mq0 = slot_of[(c, "q", 0)]
        slot_q1, mq1 = slot_of[(c, "q", 1)]
        q_evac = 0
        for jc in range(8):
            slot, mld = (slot_q0, mq0) if jc < 4 else (slot_q1, mq1)
            b = jc % 2
            pe_bank_wait(b)
            w("tensor", mld[0], mld[1])
            w("tensor", "ldx", m_xb)
            for ic in range(NIC):
                on("tensor", lambda e, i=ic, s=slot, j=jc, bb=b: e.matmul(
                    bankF(bb), wap(s, i, j % 4), xb[:, i * 512:(i + 1) * 512],
                    start=(i == 0), stop=False))

            def fmeanq(e, j=jc, bb=b):
                return e.matmul(bankF(bb), WS["q"][:, j * 128:(j + 1) * 128], RB_MR,
                                start=False, stop=True)
            mlast = tick("tensor", "pe", fmeanq)
            if jc == 3:
                slot_free[slot_q0] = mlast
            if jc == 7:
                slot_free[slot_q1] = mlast
            w("vector", "pe", mlast)
            q_evac = tick("vector", "dve", lambda e, j=jc, bb=b: e.tensor_tensor(
                qam[:, j * 512:(j + 1) * 512], bankF(bb), bc1[:, :], ALU.mult))
            bank_free[b] = q_evac
            bank_free_sem[b] = "dve"
        xb_free_pe = n["pe"]
        rowsB_free_pe = n["pe"]
        stream_half(c, "o", 0)
        stream_half(c, "o", 1)

        if c + 1 < NCH:
            w("gpsimd", "pe", xb_free_pe)
            m_xb_next = tick("gpsimd", "ldx", lambda e, cc=c + 1: e.dma_start(
                out=xb[:, :].rearrange("p (ic t) -> p ic t", ic=8),
                in_=xT.rearrange("(ic p) t -> p ic t", p=128)[:, :, cc * TC:(cc + 1) * TC]), 16)

        kts = 4 * c + 4
        w("scalar", "dve", res2_prev)
        if c == 0:
            w("vector", "lds", m_consts)
        pending_early1 = None
        pending_early2 = None
        pending_late = None
        for hp in range(NHP):
            if hp == 0:
                w("tensor", pstat_free_sem, pstat_free)
                w("tensor", px_free_sem, px_free)
                w("tensor", "dve", res2_prev)
            def emit_av(K, pp, ex_a, ex_b, avwA, avwB, off, P=hp, KTS=None):
                if K == 0:
                    w("tensor", "act", oev)
                w("tensor", "act", ex_a)
                if avwA:
                    w("tensor", "dve", avwA)

                def favA(e, P=P, K=K, pp=pp, KTS=KTS, o=off):
                    return e.matmul(pO[0:64, o:512],
                                    vT[:, K * 1024 + P * 128: K * 1024 + P * 128 + 64],
                                    pB[:, pp * 1024 + o:pp * 1024 + 512],
                                    start=(K == 0), stop=(K == KTS - 1),
                                    tile_position=(0, 0), skip_group_check=True)
                on("tensor", lambda e, f=favA: f(e))
                w("tensor", "act", ex_b)
                if avwB:
                    w("tensor", "dve", avwB)

                def favB(e, P=P, K=K, pp=pp, KTS=KTS, o=off):
                    return e.matmul(pO[64:128, o:512],
                                    vT[:, K * 1024 + P * 128 + 64: K * 1024 + P * 128 + 128],
                                    pB[:, pp * 1024 + 512 + o:pp * 1024 + 1024],
                                    start=(K == 0), stop=(K == KTS - 1),
                                    tile_position=(0, 64), skip_group_check=True)
                on("tensor", lambda e, f=favB: f(e))

                if K == 0:
                    w("tensor", pstat_free_sem, pstat_free)

                def fzA(e, K=K, pp=pp, KTS=KTS, o=off):
                    return e.matmul(pstat[64:65, o:512], onescol[:, :],
                                    pB[:, pp * 1024 + o:pp * 1024 + 512],
                                    start=(K == 0), stop=(K == KTS - 1),
                                    tile_position=(0, 64), skip_group_check=True)

                def fzB(e, K=K, pp=pp, KTS=KTS, o=off):
                    return e.matmul(pstat[32:33, o:512], onescol[:, :],
                                    pB[:, pp * 1024 + 512 + o:pp * 1024 + 1024],
                                    start=(K == 0), stop=(K == KTS - 1),
                                    tile_position=(0, 32), skip_group_check=True)
                on("tensor", lambda e, f=fzA: f(e))
                last_av[pp] = tick("tensor", "pe", fzB)

            av_q = []
            for kt in range(kts):
                p = kt % 2
                s3 = kt % 3
                sAP = SETS[s3]
                m = kt - 4 * c
                off = m * 128 if m > 0 else 0
                w("tensor", "act", last_exp3[s3])
                if hp == 0 and kt < 3:
                    if s3 < 2:
                        pe_bank_wait(2 * s3)
                        pe_bank_wait(2 * s3 + 1)
                    else:
                        w("tensor", pbc_free_sem, pbc_free)
                        w("tensor", px_free_sem, px_free)
                    w("tensor", "dve", q_evac)

                def fsA(e, P=hp, K=kt, sAP=sAP, o=off):
                    return e.matmul(sAP[:, o:512],
                                    kT[0:64, P * 4096 + K * 128:P * 4096 + (K + 1) * 128],
                                    qam[0:64, P * 512 + o:(P + 1) * 512],
                                    start=True, stop=True)

                def fsB(e, P=hp, K=kt, sAP=sAP, o=off):
                    return e.matmul(sAP[:, 512 + o:1024],
                                    kT[64:128, P * 4096 + K * 128:P * 4096 + (K + 1) * 128],
                                    qam[64:128, P * 512 + o:(P + 1) * 512],
                                    start=True, stop=True)
                sc_a = tick("tensor", "pe", fsA)
                sc_done = tick("tensor", "pe", fsB)
                if kt == 2 and pending_late is not None:
                    pending_late()
                    pending_late = None
                if len(av_q) == 2:
                    emit_av(*av_q.pop(0), KTS=kts)

                if kt == 0 and pending_early1 is not None:
                    pending_early1()
                    pending_early1 = None
                if kt == 0 and pending_early2 is not None:
                    pending_early2()
                    pending_early2 = None

                w("scalar", "pe", sc_done)
                w("scalar", "pe", last_av[p])

                if off == 0:
                    def fex(e, pp=p, sAP=sAP):
                        return e.activation(pB[:, pp * 1024:(pp + 1) * 1024],
                                            sAP[:, :],
                                            AF.Exp, scale=ccol_(C_EIGHTH))
                else:
                    def fex(e, pp=p, o=off, sAP=sAP):
                        return e.activation(
                            pB[:, pp * 1024:(pp + 1) * 1024].rearrange(
                                "q (b v) -> q b v", b=2)[:, :, o:],
                            sAP.rearrange("q (b v) -> q b v", b=2)[:, :, o:],
                            AF.Exp, scale=ccol_(C_EIGHTH))
                ex_a = ex_b = tick("scalar", "act", fex)
                last_exp3[s3] = ex_b

                avwA = avwB = 0
                if m >= 0:
                    w("vector", "act", ex_a)
                    avwA = tick("vector", "dve", lambda e, pp=p, mm=m, o=off: e.tensor_tensor(
                        pB[:, pp * 1024 + o:pp * 1024 + 512],
                        pB[:, pp * 1024 + o:pp * 1024 + 512],
                        bigmask[:, (3 - mm) * 128 + o:(3 - mm) * 128 + 512], ALU.mult))
                    w("vector", "act", ex_b)
                    avwB = tick("vector", "dve", lambda e, pp=p, mm=m, o=off: e.tensor_tensor(
                        pB[:, pp * 1024 + 512 + o:pp * 1024 + 1024],
                        pB[:, pp * 1024 + 512 + o:pp * 1024 + 1024],
                        bigmask[:, (3 - mm) * 128 + o:(3 - mm) * 128 + 512], ALU.mult))

                av_q.append((kt, p, ex_a, ex_b, avwA, avwB, off))
            for entry in av_q:
                emit_av(*entry, KTS=kts)
            av_q = []

            def run_end_early1(P=hp, la=last_av[(kts - 1) % 2]):
                nonlocal zfA, oev
                w("scalar", "pe", la)
                oev = tick("scalar", "act", lambda e: e.activation(
                    oy[:, P * 512:(P + 1) * 512], pO[:, :], AF.Copy))
                on("scalar", lambda e: e.activation(
                    pstat[64:65, :], pstat[64:65, :], AF.Ln))
                zfA = tick("scalar", "act", lambda e: e.activation(
                    sq[64:65, 0:512], pstat[64:65, :], AF.Exp,
                    scale=ccol[64:65, C_NEG1:C_NEG1 + 1]))

            def run_end_early2():
                nonlocal zfB, zrow_free
                on("scalar", lambda e: e.activation(
                    pstat[32:33, :], pstat[32:33, :], AF.Ln))
                zfB = tick("scalar", "act", lambda e: e.activation(
                    sq[32:33, 512:1024], pstat[32:33, :], AF.Exp,
                    scale=ccol[32:33, C_NEG1:C_NEG1 + 1]))
                zrow_free = zfB

            def run_end_late(P=hp):
                nonlocal pstat_free, pstat_free_sem, zb_mm_last, oy_scaled
                w("tensor", "act", zrow_free)
                w("tensor", pstat_free_sem, pstat_free)
                on("tensor", lambda e: e.matmul(
                    pstat[0:64, :], allones[64:65, 0:64], sq[64:65, 0:512],
                    start=True, stop=True, tile_position=(64, 0),
                    skip_group_check=True))
                zb_mm_last = tick("tensor", "pe", lambda e: e.matmul(
                    pstat[64:128, :], allones[32:33, 0:64], sq[32:33, 512:1024],
                    start=True, stop=True, tile_position=(32, 64),
                    skip_group_check=True))
                w("vector", "pe", zb_mm_last)
                w("vector", "act", oev)
                pstat_free = tick("vector", "dve", lambda e: e.tensor_copy(
                    sq[:, 0:512], pstat[:, :]))
                pstat_free_sem = "dve"
                oy_scaled = tick("vector", "dve", lambda e: e.tensor_tensor(
                    oy[:, P * 512:(P + 1) * 512], oy[:, P * 512:(P + 1) * 512],
                    sq[:, 0:512], ALU.mult))
            pending_early1 = run_end_early1
            pending_early2 = run_end_early2
            pending_late = run_end_late
        pending_early1()
        pending_early1 = None
        pending_early2()
        pending_early2 = None
        pending_late()
        pending_late = None

        bank_free[0] = bank_free[1] = last_exp3[0]
        bank_free_sem[0] = bank_free_sem[1] = "act"
        bank_free[2] = bank_free[3] = last_exp3[1]
        bank_free_sem[2] = bank_free_sem[3] = "act"
        pbc_free = last_exp3[2]
        pbc_free_sem = "act"
        px_free = last_exp3[2]
        px_free_sem = "act"

        import os as _os
        if _os.environ.get("DBG_OY") == "1" and c == 0:
            d_oy = nc.dram_tensor("d_oy", [128, 4096], BF16, kind="ExternalOutput").ap()
            w("sync", "dve", oy_scaled)
            tick("sync", "out", lambda e: e.dma_start(out=d_oy, in_=oy[:, :]), 16)

        slot_o0, mo0 = slot_of[(c, "o", 0)]
        slot_o1, mo1 = slot_of[(c, "o", 1)]
        attn_evac = {}
        stats_last = 0
        for oc in range(8):
            slot, mld = (slot_o0, mo0) if oc < 4 else (slot_o1, mo1)
            b = oc % 2
            pe_bank_wait(b)
            w("tensor", mld[0], mld[1])
            w("tensor", "dve", oy_scaled)
            mlast = 0
            for jc in range(NIC):
                def fwo(e, j=jc, s=slot, o=oc, bb=b):
                    return e.matmul(bankF(bb), wap(s, j, o % 4), oy[:, j * 512:(j + 1) * 512],
                                    start=(j == 0), stop=(j == 7))
                if jc == 7:
                    mlast = tick("tensor", "pe", fwo)
                else:
                    on("tensor", lambda e, ff=fwo: ff(e))
            if oc == 3:
                slot_free[slot_o0] = mlast
            if oc == 7:
                slot_free[slot_o1] = mlast
            w("vector", "pe", mlast)
            aev = tick("vector", "dve", lambda e, o=oc, bb=b: e.tensor_copy(
                qam[:, o * 512:(o + 1) * 512], bankF(bb)))
            attn_evac[oc] = aev
            bank_free[b] = aev
            bank_free_sem[b] = "dve"
            sh = oc % 2
            w("vector", "pe", sq_free[sh])
            if oc == 0:
                w("vector", "pe", zb_mm_last)
            msq = tick("vector", "dve", lambda e, o=oc, s=sh: e.tensor_tensor(
                sq[:, s * 512:(s + 1) * 512], qam[:, o * 512:(o + 1) * 512],
                qam[:, o * 512:(o + 1) * 512], ALU.mult))
            w("tensor", "dve", msq)
            if oc == 0:
                w("tensor", pstat_free_sem, pstat_free)

            def fst(e, o=oc, s=sh):
                return e.matmul(pstat[0:1, :], onescol[:, :], sq[:, s * 512:(s + 1) * 512],
                                start=(o == 0), stop=(o == 7))
            stats_last = tick("tensor", "pe", fst)
            sq_free[sh] = stats_last
        ln3 = ln_rows(stats_last, has_mean=False)
        stream_half(c, "cos", 0)
        stream_half(c, "cos", 1)

        bcast(RB_R, bc1[:, :], ln3)
        rowsB_free_pe = last_bcast_pe
        stats_last = 0
        for ic in range(NIC):
            sh = ic % 2
            w("vector", "dve", attn_evac[ic])
            if ic == 0:
                w("vector", "lds", m_xf)
            on("vector", lambda e, i=ic: e.tensor_tensor(
                pO[:, 0:512], qam[:, i * 512:(i + 1) * 512], bc1[:, :], ALU.mult))
            tick("vector", "dve", lambda e, i=ic: e.tensor_tensor(
                xf[:, i * 512:(i + 1) * 512], xf[:, i * 512:(i + 1) * 512],
                pO[:, 0:512], ALU.add))
            w("scalar", "pe", sq_free[sh])
            w("scalar", "dve", n["dve"])
            mact = tick("scalar", "act", lambda e, i=ic, s=sh: e.activation(
                sq[:, s * 512:(s + 1) * 512], xf[:, i * 512:(i + 1) * 512],
                AF.Square))
            w("tensor", "act", mact)
            if ic == 0:
                w("tensor", pbc_free_sem, pbc_free)
                w("tensor", pstat_free_sem, pstat_free)

            def fsq2(e, s=sh, i=ic):
                return e.matmul(pbc[0:1, :], onescol[:, :], sq[:, s * 512:(s + 1) * 512],
                                start=(i == 0), stop=(i == 7))
            sq_free[sh] = tick("tensor", "pe", fsq2)
            f2 = lambda e, i=ic: e.matmul(
                pstat[0:1, :], onescol_f[:, :], xf[:, i * 512:(i + 1) * 512],
                start=(i == 0), stop=(i == 7))
            stats_last = tick("tensor", "pe", f2)
        ln2 = ln_rows(stats_last, has_mean=True)
        bcast(RB_R, bc1[:, :], ln2)
        w("vector", "pe", last_bcast_pe)
        w("vector", "rc", r1_store)
        mlo = tick("vector", "dve", lambda e: e.scalar_tensor_tensor(
            RB_R, RB_MR, crow_(C_NEG1), RFA, ALU.mult, ALU.add))
        w("tensor", pbc_free_sem, pbc_free)
        w("tensor", "dve", mlo)
        mhb = tick("tensor", "pe", lambda e: e.matmul(
            pbc[:, :], allones[0:1, :], RB_MR, start=True, stop=True))
        last_bcast_pe = mhb
        w("vector", "pe", mhb)
        pbc_free = tick("vector", "dve", lambda e: e.tensor_scalar(
            pB[:, 512:1024], pbc[:, :], ccol_(C_NEG1), None, ALU.mult))
        yn_done = 0
        for ic in range(NIC):
            on("vector", lambda e, i=ic: e.tensor_tensor(
                pO[:, 0:512], xf[:, i * 512:(i + 1) * 512], bc1[:, :], ALU.mult))
            yn_done = tick("vector", "dve", lambda e, i=ic: e.tensor_tensor(
                oy[:, i * 512:(i + 1) * 512], pO[:, 0:512], pB[:, 512:1024],
                ALU.add))

        mag_done = {-1: 0}
        stats_sq_last = 0
        real_evac = {}
        ffn_last = {}
        for ch in range(2):
            slot_c, mc_ = slot_of[(c, "cos", ch)]
            for oc in range(ch * 4, ch * 4 + 4):
                bb = oc % 4
                pe_bank_wait(bb)
                w("tensor", mc_[0], mc_[1])
                if oc == ch * 4:
                    w("tensor", "dve", yn_done)
                for ic in range(NIC):
                    on("tensor", lambda e, i=ic, s=slot_c, o=oc, bk=bb: e.matmul(
                        bankF(bk), wap(s, i, o % 4), oy[:, i * 512:(i + 1) * 512],
                        start=(i == 0), stop=False))

                def fmean3(e, o=oc, bk=bb):
                    return e.matmul(bankF(bk), WS["cos"][:, o * 128:(o + 1) * 128], RB_R,
                                    start=False, stop=True)
                ffn_last[("r", oc)] = tick("tensor", "pe", fmean3)
                w("vector", "pe", ffn_last[("r", oc)])
                real_evac[oc] = tick("vector", "dve", lambda e, o=oc, bk=bb: e.tensor_copy(
                    qam[:, o * 512:(o + 1) * 512], bankF(bk)))
                bank_free[bb] = real_evac[oc]
                bank_free_sem[bb] = "dve"
            if ch == 0:
                slot_free[slot_of[(c, "cos", 0)][0]] = ffn_last[("r", 3)]
                stream_half(c, "sin", 0)
            else:
                slot_free[slot_of[(c, "cos", 1)][0]] = ffn_last[("r", 7)]
                stream_half(c, "sin", 1)
        pend_mean4 = None
        mean4_pe = 0

        def emit_mean4(o, mact):
            nonlocal mean4_pe
            w("tensor", "act", mact)
            if o == 0:
                w("tensor", pstat_free_sem, pstat_free)
            mean4_pe = tick("tensor", "pe", lambda e, o=o: e.matmul(
                pstat[0:1, :], onescol[:, :], qam[:, o * 512:(o + 1) * 512],
                start=(o == 0), stop=(o == 7)))

        for ch in range(2):
            slot_s, ms_ = slot_of[(c, "sin", ch)]
            for oc in range(ch * 4, ch * 4 + 4):
                bb = oc % 4
                pe_bank_wait(bb)
                w("tensor", ms_[0], ms_[1])
                for ic in range(NIC):
                    on("tensor", lambda e, i=ic, s=slot_s, o=oc, bk=bb: e.matmul(
                        bankF(bk), wap(s, i, o % 4), oy[:, i * 512:(i + 1) * 512],
                        start=(i == 0), stop=False))

                def fmean4(e, o=oc, bk=bb):
                    return e.matmul(bankF(bk), WS["sin"][:, o * 128:(o + 1) * 128], RB_R,
                                    start=False, stop=True)
                ffn_last[("i", oc)] = tick("tensor", "pe", fmean4)
                if pend_mean4 is not None:
                    emit_mean4(*pend_mean4)
                    pend_mean4 = None
                w("vector", "pe", ffn_last[("i", oc)])
                w("vector", "pe", sq_free[0])
                w("vector", "act", mag_done[oc - 1])
                on("vector", lambda e, bk=bb: e.tensor_copy(
                    pB[:, 0:512], bankF(bk)))
                on("vector", lambda e, o=oc: e.tensor_tensor(
                    sq[:, 0:512], qam[:, o * 512:(o + 1) * 512],
                    qam[:, o * 512:(o + 1) * 512], ALU.mult))
                on("vector", lambda e: e.tensor_tensor(
                    sq[:, 512:1024], pB[:, 0:512], pB[:, 0:512], ALU.mult))
                m2 = tick("vector", "dve", lambda e: e.tensor_tensor(
                    pO[:, 0:512], sq[:, 0:512], sq[:, 512:1024], ALU.add))
                bank_free[bb] = m2
                bank_free_sem[bb] = "dve"
                w("scalar", "dve", m2)
                on("scalar", lambda e: e.activation(pO[:, 0:512], pO[:, 0:512], AF.Ln))
                mag_act = tick("scalar", "act", lambda e, o=oc: e.activation(
                    qam[:, o * 512:(o + 1) * 512], pO[:, 0:512], AF.Exp,
                    scale=ccol_(C_HALF)))
                mag_done[oc] = mag_act
                pend_mean4 = (oc, mag_act)
            if ch == 0:
                slot_free[slot_of[(c, "sin", 0)][0]] = ffn_last[("i", 3)]
            else:
                slot_free[slot_of[(c, "sin", 1)][0]] = ffn_last[("i", 7)]
        emit_mean4(*pend_mean4)
        pend_mean4 = None
        w("vector", "pe", mean4_pe)
        mrow = tick("vector", "dve", lambda e: e.tensor_scalar(
            RB_R, pstat[0:1, :], crow_(C_MI1024), None, ALU.mult))
        bcast(RB_R, bc1[:, :], mrow)
        w("vector", "act", mag_done[7])
        stats_sq_last = 0
        for oc in range(NIC):
            sh = oc % 2
            w("vector", "pe", sq_free[sh])
            on("vector", lambda e, o=oc: e.tensor_tensor(
                pB[:, 0:512], qam[:, o * 512:(o + 1) * 512], bc1[:, :], ALU.add))
            mdve = tick("vector", "dve", lambda e, o=oc, s=sh: e.tensor_tensor(
                sq[:, s * 512:(s + 1) * 512], pB[:, 0:512],
                qam[:, o * 512:(o + 1) * 512], ALU.mult))
            w("tensor", "dve", mdve)
            if oc == 0:
                w("tensor", pbc_free_sem, pbc_free)
            stats_sq_last = tick("tensor", "pe", lambda e, o=oc, s=sh: e.matmul(
                pbc[0:1, :], onescol[:, :], sq[:, s * 512:(s + 1) * 512],
                start=(o == 0), stop=(o == 7)))
            sq_free[sh] = stats_sq_last
        rowsB_free_pe = n["pe"]
        ln4 = ln_rows(stats_sq_last, has_mean=True, centered=True)

        bcast(RB_R, bc4[:, :], ln4)
        w("tensor", pbc_free_sem, pbc_free)
        w("tensor", "dve", ln4)
        mb4 = tick("tensor", "pe", lambda e: e.matmul(
            pbc[:, :], allones[0:1, :], RB_MR, start=True, stop=True))
        last_bcast_pe = mb4
        w("vector", "pe", mb4)
        pbc_free = tick("vector", "dve", lambda e: e.tensor_scalar(
            bcf[:, :], pbc[:, :], ccol_(C_NEG1), None, ALU.mult))
        rowsB_free_pe = last_bcast_pe
        res2_done = 0
        for ic in range(NIC):
            w("vector", "act", mag_done[ic])
            on("vector", lambda e, i=ic: e.tensor_tensor(
                pO[:, 0:512], qam[:, i * 512:(i + 1) * 512], bc4[:, :], ALU.mult))
            on("vector", lambda e, i=ic: e.tensor_tensor(
                xf[:, i * 512:(i + 1) * 512], xf[:, i * 512:(i + 1) * 512],
                pO[:, 0:512], ALU.add))
            res2_done = tick("vector", "dve", lambda e, i=ic: e.tensor_tensor(
                xf[:, i * 512:(i + 1) * 512], xf[:, i * 512:(i + 1) * 512],
                bcf[:, :], ALU.add))
        res2_prev = res2_done
        w("sync", "dve", res2_done)
        tick("sync", "out", lambda e, cc=c: e.dma_start(
            out=yT.rearrange("(ic p) t -> p ic t", p=128)[:, :, cc * TC:(cc + 1) * TC],
            in_=xf[:, :].rearrange("p (ic t) -> p ic t", ic=8)), 16)
        out_hist.append(n["out"])

    w("sync", "out", n["out"])

    with nc.Block() as blk:
        blk.tensor(lambda e: [f(e) for f in q["tensor"]])
        blk.scalar(lambda e: [f(e) for f in q["scalar"]])
        blk.vector(lambda e: [f(e) for f in q["vector"]])
        blk.gpsimd(lambda e: [f(e) for f in q["gpsimd"]])
        blk.sync(lambda e: [f(e) for f in q["sync"]])
    return nc


def kernel(L0_attn_x, L0_attn_Wq, L0_attn_Wk, L0_attn_Wv, L0_attn_Wo,
           L0_ffn_theta, L0_ffn_phi, L0_ffn_theta_noise, L0_ffn_phi_noise):
    x = np.asarray(L0_attn_x, dtype=np.float32)
    wq = np.ascontiguousarray(np.asarray(L0_attn_Wq, dtype=np.float32).T)
    wk = np.ascontiguousarray(np.asarray(L0_attn_Wk, dtype=np.float32).T)
    wv = np.ascontiguousarray(np.asarray(L0_attn_Wv, dtype=np.float32).T)
    wo = np.ascontiguousarray(np.asarray(L0_attn_Wo, dtype=np.float32).T)
    th = np.ascontiguousarray(np.asarray(L0_ffn_theta, dtype=np.float32))
    ph = np.ascontiguousarray(np.asarray(L0_ffn_phi, dtype=np.float32))
    thn = np.ascontiguousarray(np.asarray(L0_ffn_theta_noise, dtype=np.float32))
    phn = np.ascontiguousarray(np.asarray(L0_ffn_phi_noise, dtype=np.float32))

    bigmask = (np.arange(128)[:, None] <= (np.arange(896)[None, :] - 384)
               ).astype(BFNP)
    onescol = np.ones((128, 1), dtype=BFNP)
    allones = np.ones((128, 128), dtype=BFNP)

    nc = build_nc()
    in_maps = []
    for b in range(NCORE):
        in_maps.append({
            "xT": np.ascontiguousarray(x[b].T),
            "wkT": wk, "wvT": wv, "wqT": wq, "woT": wo,
            "theta": th, "theta_n": thn, "phi": ph, "phi_n": phn,
            "bigmask": bigmask, "onescol": onescol, "allones": allones,
        })
    res = run_bass_kernel_spmd(nc, in_maps, core_ids=list(range(NCORE)))
    out = np.stack([np.asarray(res.results[b]["yT"], dtype=np.float32).T
                    for b in range(NCORE)], axis=0)
    return out


# version 26 (final)
# speedup vs baseline: 1.0268x; 1.0038x over previous
import numpy as np
import ml_dtypes
import concourse.bass as bass
import concourse.mybir as mybir
from concourse.bass_utils import run_bass_kernel_spmd

F32 = mybir.dt.float32
BF16 = mybir.dt.bfloat16
AF = mybir.ActivationFunctionType
ALU = mybir.AluOpType
AX = mybir.AxisListType.X
BFNP = ml_dtypes.bfloat16

D = 1024
T = 4096
NCH = 8
TC = 512
NIC = 8
NHP = 8
NCORE = 8

(C_INV1024, C_EIGHTH, C_M05, C_THREE, C_NEG1, C_HALF, C_EPS, C_M18, C_PI2,
 C_MI1024) = range(10)


def build_nc(detect_races=True):
    nc = bass.Bass("TRN2", target_bir_lowering=False, debug=False,
                   dynamic_dma_scratch_size=16384,
                   detect_race_conditions=detect_races)

    xT = nc.dram_tensor("xT", [D, T], F32, kind="ExternalInput").ap()
    w_d = {
        "k": nc.dram_tensor("wkT", [D, D], F32, kind="ExternalInput").ap(),
        "v": nc.dram_tensor("wvT", [D, D], F32, kind="ExternalInput").ap(),
        "q": nc.dram_tensor("wqT", [D, D], F32, kind="ExternalInput").ap(),
        "o": nc.dram_tensor("woT", [D, D], F32, kind="ExternalInput").ap(),
    }
    th_d = nc.dram_tensor("theta", [D, D], F32, kind="ExternalInput").ap()
    thn_d = nc.dram_tensor("theta_n", [D, D], F32, kind="ExternalInput").ap()
    ph_d = nc.dram_tensor("phi", [D, D], F32, kind="ExternalInput").ap()
    phn_d = nc.dram_tensor("phi_n", [D, D], F32, kind="ExternalInput").ap()
    bigmask_d = nc.dram_tensor("bigmask", [128, 896], BF16, kind="ExternalInput").ap()
    onescol_d = nc.dram_tensor("onescol", [128, 1], BF16, kind="ExternalInput").ap()
    allones_d = nc.dram_tensor("allones", [128, 128], BF16, kind="ExternalInput").ap()
    yT = nc.dram_tensor("yT", [D, T], F32, kind="ExternalOutput").ap()

    WIDX = {"k": 0, "v": 1, "q": 2, "o": 3, "cos": 4, "sin": 5}
    wscr = [nc.dram_tensor(f"wscr{i}", [128, 8192], BF16, kind="Internal").ap()
            for i in range(6)]
    rscr_d = nc.dram_tensor("rscr", [1, 512], BF16, kind="Internal").ap()

    A = nc.alloc_sbuf_tensor
    kT = A("kT", [128, 32768], BF16)
    vT = A("vT", [128, 32768], BF16)
    xf = A("xf", [128, 4096], F32)
    xb = A("xb", [128, 4096], BF16)
    qam = A("qam", [128, 4096], BF16)
    oy = A("oy", [128, 4096], BF16)
    pB = A("pB", [128, 2048], BF16)
    wsl = A("wsl", [128, 8192], BF16)
    bigmask = A("bigmask_s", [128, 896], BF16)
    bc1 = A("bc1", [128, 512], BF16)
    rowsF = A("rowsF", [128, 512], F32)
    rowsB = A("rowsB", [128, 1024], BF16)
    wsums = A("wsums", [128, 5120], BF16)
    sq = A("sq", [128, 1024], BF16)
    onescol = A("onescol_s", [128, 1], BF16)
    onescol_f = A("onescol_f", [128, 1], F32)
    allones = A("allones_s", [128, 128], BF16)
    ccol = A("ccol", [128, 10], F32)
    erow = A("erow", [128, 16], F32)
    ecol4 = A("ecol4", [128, 4], F32)

    bcf = nc.alloc_sbuf_tensor_at("bcf", [128, 512], F32,
                                  offset=nc.lookup_mloc(pB).addr)
    bc4 = nc.alloc_sbuf_tensor_at("bc4", [128, 512], BF16,
                                  offset=nc.lookup_mloc(pB).addr + 2048)
    kaddr = nc.lookup_mloc(kT).addr
    vaddr = nc.lookup_mloc(vT).addr
    stage_f = nc.alloc_sbuf_tensor_at("stage_f", [128, 8192], F32, offset=kaddr)
    stage_b = nc.alloc_sbuf_tensor_at("stage_b", [128, 8192], BF16, offset=kaddr + 32768)
    stage2_f = nc.alloc_sbuf_tensor_at("stage2_f", [128, 8192], F32, offset=vaddr)
    stage_b2 = nc.alloc_sbuf_tensor_at("stage_b2", [128, 8192], BF16,
                                       offset=vaddr + 32768)

    parena = nc.alloc_psum_tensor("parena", [128, 2048], F32)
    pO = nc.alloc_psum_tensor("pO", [128, 512], F32)
    pstat = nc.alloc_psum_tensor("pstat", [128, 512], F32)
    parena2 = nc.alloc_psum_tensor("parena2", [128, 1024], F32)
    pbc = parena2[:, 0:512]
    px = parena2[:, 512:1024]

    SETS = [parena[:, 0:1024], parena[:, 1024:2048], parena2[:, 0:1024]]

    sem_names = ["pe", "act", "dve", "ldg", "lds", "out",
                 "ld0", "ld1", "ldx", "ldn", "lsw", "lsw2", "rc", "rc2"]
    sems = {k: nc.alloc_semaphore(k) for k in sem_names}
    q = {k: [] for k in ("tensor", "scalar", "vector", "gpsimd", "sync")}
    n = {k: 0 for k in sem_names}

    def on(eng, f):
        q[eng].append(f)

    def w(eng, key, v):
        if v and v > 0:
            on(eng, lambda e, s=sems[key], vv=v: e.wait_ge(s, vv))

    def tick(eng, key, f, k=1):
        n[key] += k
        on(eng, lambda e, ff=f, s=sems[key], kk=k: ff(e).then_inc(s, kk))
        return n[key]

    def ccol_(i):
        return ccol[:, i:i + 1]

    def crow_(i):
        return ccol[0:1, i:i + 1]

    RFA = bcf[0:1, 0:512]
    RFB = rowsF[0:1, 0:512]
    RB_R = rowsB[0:1, 0:512]
    RB_MR = rowsB[0:1, 512:1024]

    WS = {"k": wsums[0:1, 0:1024], "v": wsums[0:1, 1024:2048],
          "q": wsums[0:1, 2048:3072], "cos": wsums[0:1, 3072:4096],
          "sin": wsums[0:1, 4096:5120]}

    tick("sync", "lds", lambda e: e.dma_start(out=bigmask[:, :], in_=bigmask_d), 16)
    tick("sync", "lds", lambda e: e.dma_start(out=onescol[:, :], in_=onescol_d), 16)
    m_consts = tick("sync", "lds", lambda e: e.dma_start(out=allones[:, :], in_=allones_d), 16)

    for idx, val in [(C_INV1024, 1.0 / 1024), (C_EIGHTH, 0.125), (C_M05, -0.5),
                     (C_THREE, 3.0), (C_NEG1, -1.0), (C_HALF, 0.5), (C_EPS, 1e-5),
                     (C_PI2, float(np.pi / 2)), (C_MI1024, -1.0 / 1024)]:
        on("vector", lambda e, i=idx, v=val: e.memset(ccol[:, i:i + 1], v))
    on("vector", lambda e: e.memset(onescol_f[:, :], 1.0))
    m_memset = tick("vector", "dve",
                    lambda e: e.memset(ccol[:, C_M18:C_M18 + 1], -0.125))

    w("tensor", "lds", m_consts)
    w("scalar", "dve", m_memset)

    stageB = [stage_b, stage_b2]
    ssem = ["lsw", "lsw2"]
    buf_free_pe = [0, 0]
    buf_free_st = [0, 0]
    wscr_ready = {}

    prev_wsum_dve = [0]
    wsum_pe_last = [0]

    def startup_wsum(name, sb):
        w("tensor", "dve", prev_wsum_dve[0])
        for jh in range(2):
            dst = pstat[0:1, :] if jh == 0 else pbc[0:1, :]
            for ic in range(NIC):
                def f(e, d=dst, i=ic, j=jh, sb=sb):
                    return e.matmul(d, onescol[:, :],
                                    sb[:, i * 1024 + j * 512: i * 1024 + (j + 1) * 512],
                                    start=(i == 0), stop=(i == 7))
                if ic == 7 and jh == 1:
                    wsum_pe_last[0] = tick("tensor", "pe", f)
                else:
                    on("tensor", lambda e, ff=f: ff(e))
        w("vector", "pe", wsum_pe_last[0])
        on("vector", lambda e, nm=name: e.tensor_scalar(
            WS[nm][:, 0:512], pstat[0:1, :], crow_(C_NEG1), None, ALU.mult))
        mdone = tick("vector", "dve", lambda e, nm=name: e.tensor_scalar(
            WS[nm][:, 512:1024], pbc[0:1, :], crow_(C_NEG1), None, ALU.mult))
        prev_wsum_dve[0] = mdone
        return mdone

    wsum_dve = 0
    bi = 0
    for name in ["k", "v", "q"]:
        sb = stageB[bi]
        w("gpsimd", "pe", buf_free_pe[bi])
        w("gpsimd", ssem[bi], buf_free_st[bi])
        m_cast = tick("gpsimd", "ldg", lambda e, s=w_d[name], sb=sb: e.dma_start(
            out=sb[:, :], in_=s.rearrange("(ic p) j -> p ic j", p=128)), 16)
        w("tensor", "ldg", m_cast)
        wsum_dve = startup_wsum(name, sb)
        buf_free_pe[bi] = wsum_pe_last[0]
        w("sync", "dve", wsum_dve)
        buf_free_st[bi] = tick("sync", ssem[bi], lambda e, wi=WIDX[name], sb=sb:
                               e.dma_start(out=wscr[wi], in_=sb[:, :]), 16)
        wscr_ready[name] = (ssem[bi], buf_free_st[bi])
        bi ^= 1

    sb = stageB[bi]
    w("gpsimd", "pe", buf_free_pe[bi])
    w("gpsimd", ssem[bi], buf_free_st[bi])
    m_cast = tick("gpsimd", "ldg", lambda e, sb=sb: e.dma_start(
        out=sb[:, :], in_=w_d["o"].rearrange("(ic p) j -> p ic j", p=128)), 16)
    w("vector", "ldg", m_cast)
    wo_center = 0
    for ic in range(NIC):
        on("vector", lambda e, i=ic, sb=sb: e.tensor_reduce(
            erow[:, 8:9], sb[:, i * 1024:(i + 1) * 1024], AX, ALU.add))
        mneg = tick("vector", "dve", lambda e: e.tensor_scalar(
            erow[:, 8:9], erow[:, 8:9], ccol_(C_MI1024), None, ALU.mult))
        w("vector", "dve", mneg)
        wo_center = tick("vector", "dve", lambda e, i=ic, sb=sb: e.tensor_scalar(
            sb[:, i * 1024:(i + 1) * 1024],
            sb[:, i * 1024:(i + 1) * 1024], erow[:, 8:9], None, ALU.add))
    w("sync", "dve", wo_center)
    buf_free_st[bi] = tick("sync", ssem[bi], lambda e, sb=sb: e.dma_start(
        out=wscr[WIDX["o"]], in_=sb[:, :]), 16)
    wscr_ready["o"] = (ssem[bi], buf_free_st[bi])
    bi ^= 1

    on("scalar", lambda e: e.preload_act_table(AF.Sin))
    prev_sin = 0
    prev_add = 0
    for name, base_d, noise_d, use_cos in [("cos", th_d, thn_d, True),
                                           ("sin", ph_d, phn_d, False)]:
        sb = stageB[bi]
        w("sync", "act", prev_sin)
        mf = tick("sync", "lds", lambda e, s=base_d: e.dma_start(
            out=stage_f[:, :], in_=s.rearrange("(ic p) j -> p ic j", p=128)), 16)
        w("gpsimd", "dve", prev_add)
        mg = tick("gpsimd", "ldn", lambda e, s=noise_d: e.dma_start(
            out=stage2_f[:, :], in_=s.rearrange("(ic p) j -> p ic j", p=128)), 16)
        w("vector", "lds", mf)
        w("vector", "ldn", mg)
        add_done = tick("vector", "dve", lambda e: e.tensor_tensor(
            stage_f[:, :], stage_f[:, :], stage2_f[:, :], ALU.add))
        prev_add = add_done
        w("scalar", "dve", add_done)
        w("scalar", ssem[bi], buf_free_st[bi])
        w("scalar", "pe", buf_free_pe[bi])
        if use_cos:
            sin_done = tick("scalar", "act", lambda e, sb=sb: e.activation(
                sb[:, :], stage_f[:, :], AF.Sin, bias=ccol_(C_PI2)))
        else:
            sin_done = tick("scalar", "act", lambda e, sb=sb: e.activation(
                sb[:, :], stage_f[:, :], AF.Sin))
        prev_sin = sin_done
        w("tensor", "act", sin_done)
        wsum_dve = startup_wsum(name, sb)
        buf_free_pe[bi] = wsum_pe_last[0]
        w("sync", "dve", wsum_dve)
        w("sync", "act", sin_done)
        buf_free_st[bi] = tick("sync", ssem[bi], lambda e, wi=WIDX[name], sb=sb:
                               e.dma_start(out=wscr[wi], in_=sb[:, :]), 16)
        wscr_ready[name] = (ssem[bi], buf_free_st[bi])
        bi ^= 1

    startup_lsw = buf_free_st[0]
    startup_lsw2 = buf_free_st[1]
    startup_b0_pe = buf_free_pe[0]
    startup_wsum_dve = wsum_dve

    startup_sin_act = prev_sin

    from concourse.hw_specs import get_activation_tables
    tables = list(get_activation_tables(nc.m.arch).keys())
    nle_idx = tables.index("natural_log_exp_and_others")
    on("scalar", lambda e, idx=nle_idx: e.add_instruction(mybir.InstLoadActFuncSet(
        name=e.bass.get_next_instruction_name(), act_func_set_id=idx,
        ins=[], outs=[])))

    half_seq = 0
    slot_free = [0, 0]
    slot_of = {}

    def stream_half(c, name, h):
        nonlocal half_seq
        slot = half_seq % 2
        half_seq += 1
        sem = "ld0" if slot == 0 else "ld1"
        w("gpsimd", "pe", slot_free[slot])
        w("gpsimd", *wscr_ready[name])
        src = wscr[WIDX[name]].rearrange("p (ic hh j) -> p ic hh j", ic=8, hh=2)[:, :, h, :]
        dst = wsl[:, slot * 4096:(slot + 1) * 4096].rearrange("p (ic j) -> p ic j", ic=8)
        m = tick("gpsimd", sem, lambda e, s=src, d=dst: e.dma_start(out=d, in_=s), 16)
        slot_of[(c, name, h)] = (slot, (sem, m))

    def wap(slot, ic, sub):
        return wsl[:, slot * 4096 + ic * 512 + sub * 128: slot * 4096 + ic * 512 + (sub + 1) * 128]

    def wap512(slot, ic):
        return wsl[:, slot * 4096 + ic * 512: slot * 4096 + (ic + 1) * 512]

    bankF = lambda b: parena[:, b * 512:(b + 1) * 512]
    bank_free = [0, 0, 0, 0]
    bank_free_sem = ["dve", "dve", "dve", "dve"]
    pstat_free = startup_wsum_dve
    pstat_free_sem = "dve"
    px_free_sem = "dve"
    pbc_free = startup_wsum_dve
    pbc_free_sem = "dve"
    px_free = 0
    oev = 0
    oy_scaled = 0
    p3_evac_last = 0
    r1_store = 0
    rowsB_free_pe = 0
    last_bcast_pe = 0
    last_exp3 = [0, 0, 0]
    last_av = {0: 0, 1: 0}
    res2_prev = 0
    zrow_free = 0
    zfA = 0
    zfB = 0
    zb_mm_last = 0
    m_xb_next = None
    sq_free = [0, 0]

    def pe_bank_wait(b):
        w("tensor", bank_free_sem[b], bank_free[b])

    def ln_rows(stats_pe, has_mean, centered=False, fold_r=True):
        nonlocal pstat_free, pbc_free, pstat_free_sem, pbc_free_sem
        pstat_free_sem = "dve"
        pbc_free_sem = "dve"
        w("vector", "pe", stats_pe)
        w("vector", "pe", rowsB_free_pe)
        if has_mean and centered:
            on("vector", lambda e: e.tensor_scalar(RFA, pstat[0:1, :], crow_(C_INV1024), None, ALU.mult))
            mark = tick("vector", "dve", lambda e: e.tensor_scalar(
                RFB, pbc[0:1, :], crow_(C_INV1024), None, ALU.mult))
        elif has_mean:
            on("vector", lambda e: e.tensor_scalar(RFA, pstat[0:1, :], crow_(C_INV1024), None, ALU.mult))
            on("vector", lambda e: e.scalar_tensor_tensor(
                RFB, RFA, crow_(C_NEG1), RFA, ALU.mult, ALU.mult))
            mark = tick("vector", "dve", lambda e: e.scalar_tensor_tensor(
                RFB, pbc[0:1, :], crow_(C_INV1024), RFB, ALU.mult, ALU.add))
        else:
            mark = tick("vector", "dve", lambda e: e.tensor_scalar(
                RFB, pstat[0:1, :], crow_(C_INV1024), None, ALU.mult))
        pstat_free = mark
        pbc_free = mark
        w("scalar", "dve", mark)
        on("scalar", lambda e: e.activation(RFB, RFB, AF.Ln, bias=crow_(C_EPS)))
        r_act = tick("scalar", "act", lambda e: e.activation(RFB, RFB, AF.Exp, scale=crow_(C_M05)))
        w("vector", "act", r_act)
        if has_mean:
            if fold_r:
                on("vector", lambda e: e.tensor_tensor(RFA, RFA, RFB, ALU.mult))
            on("vector", lambda e: e.tensor_copy(RB_R, RFB))
            return tick("vector", "dve", lambda e: e.tensor_copy(RB_MR, RFA))
        else:
            return tick("vector", "dve", lambda e: e.tensor_copy(RB_R, RFB))

    def bcast(row_ap, dst_ap, row_ready_dve):
        nonlocal pbc_free, last_bcast_pe, pbc_free_sem
        w("tensor", pbc_free_sem, pbc_free)
        pbc_free_sem = "dve"
        w("tensor", "dve", row_ready_dve)
        m = tick("tensor", "pe", lambda e, r=row_ap: e.matmul(
            pbc[:, :], allones[0:1, :], r, start=True, stop=True))
        last_bcast_pe = m
        w("vector", "pe", m)
        pbc_free = tick("vector", "dve", lambda e, d=dst_ap: e.tensor_copy(d, pbc[:, :]))
        return pbc_free

    out_hist = [0]
    for c in range(NCH):
        w("sync", "out", out_hist[c])
        m_xf = tick("sync", "lds", lambda e, cc=c: e.dma_start(
            out=xf[:, :].rearrange("p (ic t) -> p ic t", ic=8),
            in_=xT.rearrange("(ic p) t -> p ic t", p=128)[:, :, cc * TC:(cc + 1) * TC]), 16)
        if c == 0:
            m_xb = tick("gpsimd", "ldx", lambda e, cc=c: e.dma_start(
                out=xb[:, :].rearrange("p (ic t) -> p ic t", ic=8),
                in_=xT.rearrange("(ic p) t -> p ic t", p=128)[:, :, cc * TC:(cc + 1) * TC]), 16)
        else:
            m_xb = m_xb_next

        stream_half(c, "k", 0)
        stream_half(c, "k", 1)

        if c == 0:
            w("vector", "act", startup_sin_act)
            w("vector", "lsw", startup_lsw)
            w("vector", "lsw2", startup_lsw2)
            w("vector", "pe", startup_b0_pe)
        w("scalar", "ldx", m_xb)
        stats_last = 0
        for ic in range(NIC):
            sh = ic % 2
            w("scalar", "pe", sq_free[sh])
            mact = tick("scalar", "act", lambda e, i=ic, s=sh: e.activation(
                sq[:, s * 512:(s + 1) * 512], xb[:, i * 512:(i + 1) * 512],
                AF.Square))
            w("tensor", "ldx", m_xb)
            w("tensor", "act", mact)
            if ic == 0:
                w("tensor", pstat_free_sem, pstat_free)
                w("tensor", pbc_free_sem, pbc_free)
            on("tensor", lambda e, i=ic: e.matmul(
                pstat[0:1, :], onescol[:, :], xb[:, i * 512:(i + 1) * 512],
                start=(i == 0), stop=(i == 7)))

            def fsq(e, s=sh, i=ic):
                return e.matmul(pbc[0:1, :], onescol[:, :], sq[:, s * 512:(s + 1) * 512],
                                start=(i == 0), stop=(i == 7))
            stats_last = tick("tensor", "pe", fsq)
            sq_free[sh] = stats_last
        ln1 = ln_rows(stats_last, has_mean=True, fold_r=False)
        bcast(RB_R, bc1[:, :], ln1)
        w("sync", "dve", ln1)
        w("sync", "dve", p3_evac_last)
        r1_store = tick("sync", "rc", lambda e: e.dma_start(
            out=rscr_d, in_=RB_R[0:1, 0:512]), 16)
        w("gpsimd", "rc", r1_store)
        mrc = tick("gpsimd", "rc2", lambda e: e.dma_start(
            out=ecol4[:, :], in_=rscr_d.rearrange("o (s p) -> (o p) s", p=128),
            allow_slow_non_contiguous=True), 16)

        slot_k0, mk0 = slot_of[(c, "k", 0)]
        slot_k1, mk1 = slot_of[(c, "k", 1)]
        for jc in range(8):
            slot, mld = (slot_k0, mk0) if jc < 4 else (slot_k1, mk1)
            b = jc % 2
            pe_bank_wait(b)
            w("tensor", mld[0], mld[1])
            w("tensor", "ldx", m_xb)
            for ic in range(NIC):
                on("tensor", lambda e, i=ic, s=slot, j=jc, bb=b: e.matmul(
                    bankF(bb), wap(s, i, j % 4), xb[:, i * 512:(i + 1) * 512],
                    start=(i == 0), stop=False))

            if jc == 0:
                w("tensor", "dve", ln1)

            def fmean(e, j=jc, bb=b):
                return e.matmul(bankF(bb), WS["k"][:, j * 128:(j + 1) * 128], RB_MR,
                                start=False, stop=True)
            mlast = tick("tensor", "pe", fmean)
            if jc == 3:
                slot_free[slot_k0] = mlast
            if jc == 7:
                slot_free[slot_k1] = mlast
            w("vector", "pe", mlast)
            ev = tick("vector", "dve", lambda e, j=jc, cc=c, bb=b: e.tensor_tensor(
                kT[:, j * 4096 + cc * TC: j * 4096 + (cc + 1) * TC], bankF(bb),
                bc1[:, :], ALU.mult))
            bank_free[b] = ev
            bank_free_sem[b] = "dve"
        stream_half(c, "v", 0)
        stream_half(c, "v", 1)

        slot_v0, mv0 = slot_of[(c, "v", 0)]
        slot_v1, mv1 = slot_of[(c, "v", 1)]
        for jh in range(2):
            slot, mld = (slot_v0, mv0) if jh == 0 else (slot_v1, mv1)
            for st in range(4):
                b = 2 + st % 2
                pe_bank_wait(b)
                w("tensor", mld[0], mld[1])
                for ic in range(NIC):
                    on("tensor", lambda e, i=ic, s=slot, t=st, bb=b: e.matmul(
                        bankF(bb), xb[:, i * 512 + t * 128: i * 512 + (t + 1) * 128],
                        wap512(s, i), start=(i == 0), stop=False))

                if jh == 0 and st == 0:
                    w("tensor", "dve", ln1)

                def fmeanv(e, t=st, j=jh, bb=b):
                    return e.matmul(bankF(bb), RB_MR[:, t * 128:(t + 1) * 128],
                                    WS["v"][:, j * 512:(j + 1) * 512], start=False, stop=True)
                mlast = tick("tensor", "pe", fmeanv)
                if st == 3:
                    if jh == 0:
                        slot_free[slot_v0] = mlast
                    else:
                        slot_free[slot_v1] = mlast
                w("vector", "pe", mlast)
                if jh == 0 and st == 0:
                    w("vector", "rc2", mrc)
                ev = tick("vector", "dve", lambda e, t=st, j=jh, cc=c, bb=b: e.tensor_scalar(
                    vT[:, (cc * 4 + t) * 1024 + j * 512: (cc * 4 + t) * 1024 + (j + 1) * 512],
                    bankF(bb), ecol4[:, t:t + 1], None, ALU.mult))
                bank_free[b] = ev
                bank_free_sem[b] = "dve"
        p3_evac_last = n["dve"]
        stream_half(c, "q", 0)
        stream_half(c, "q", 1)

        slot_q0, mq0 = slot_of[(c, "q", 0)]
        slot_q1, mq1 = slot_of[(c, "q", 1)]
        q_evac = 0
        for jc in range(8):
            slot, mld = (slot_q0, mq0) if jc < 4 else (slot_q1, mq1)
            b = jc % 2
            pe_bank_wait(b)
            w("tensor", mld[0], mld[1])
            w("tensor", "ldx", m_xb)
            for ic in range(NIC):
                on("tensor", lambda e, i=ic, s=slot, j=jc, bb=b: e.matmul(
                    bankF(bb), wap(s, i, j % 4), xb[:, i * 512:(i + 1) * 512],
                    start=(i == 0), stop=False))

            def fmeanq(e, j=jc, bb=b):
                return e.matmul(bankF(bb), WS["q"][:, j * 128:(j + 1) * 128], RB_MR,
                                start=False, stop=True)
            mlast = tick("tensor", "pe", fmeanq)
            if jc == 3:
                slot_free[slot_q0] = mlast
            if jc == 7:
                slot_free[slot_q1] = mlast
            w("vector", "pe", mlast)
            q_evac = tick("vector", "dve", lambda e, j=jc, bb=b: e.tensor_tensor(
                qam[:, j * 512:(j + 1) * 512], bankF(bb), bc1[:, :], ALU.mult))
            bank_free[b] = q_evac
            bank_free_sem[b] = "dve"
        xb_free_pe = n["pe"]
        rowsB_free_pe = n["pe"]
        stream_half(c, "o", 0)
        stream_half(c, "o", 1)

        if c + 1 < NCH:
            w("gpsimd", "pe", xb_free_pe)
            m_xb_next = tick("gpsimd", "ldx", lambda e, cc=c + 1: e.dma_start(
                out=xb[:, :].rearrange("p (ic t) -> p ic t", ic=8),
                in_=xT.rearrange("(ic p) t -> p ic t", p=128)[:, :, cc * TC:(cc + 1) * TC]), 16)

        kts = 4 * c + 4
        w("scalar", "dve", res2_prev)
        if c == 0:
            w("vector", "lds", m_consts)
        pending_early1 = None
        pending_early2 = None
        pending_late = None
        for hp in range(NHP):
            if hp == 0:
                w("tensor", pstat_free_sem, pstat_free)
                w("tensor", px_free_sem, px_free)
                w("tensor", "dve", res2_prev)
            def emit_av(K, pp, ex_a, ex_b, avwA, avwB, off, P=hp, KTS=None):
                if K == 0:
                    w("tensor", "dve", oev)
                w("tensor", "act", ex_a)
                if avwA:
                    w("tensor", "dve", avwA)

                def favA(e, P=P, K=K, pp=pp, KTS=KTS, o=off):
                    return e.matmul(pO[0:64, o:512],
                                    vT[:, K * 1024 + P * 128: K * 1024 + P * 128 + 64],
                                    pB[:, pp * 1024 + o:pp * 1024 + 512],
                                    start=(K == 0), stop=(K == KTS - 1),
                                    tile_position=(0, 0), skip_group_check=True)
                on("tensor", lambda e, f=favA: f(e))
                w("tensor", "act", ex_b)
                if avwB:
                    w("tensor", "dve", avwB)

                def favB(e, P=P, K=K, pp=pp, KTS=KTS, o=off):
                    return e.matmul(pO[64:128, o:512],
                                    vT[:, K * 1024 + P * 128 + 64: K * 1024 + P * 128 + 128],
                                    pB[:, pp * 1024 + 512 + o:pp * 1024 + 1024],
                                    start=(K == 0), stop=(K == KTS - 1),
                                    tile_position=(0, 64), skip_group_check=True)
                on("tensor", lambda e, f=favB: f(e))

                if K == 0:
                    w("tensor", pstat_free_sem, pstat_free)

                def fzA(e, K=K, pp=pp, KTS=KTS, o=off):
                    return e.matmul(pstat[64:65, o:512], onescol[:, :],
                                    pB[:, pp * 1024 + o:pp * 1024 + 512],
                                    start=(K == 0), stop=(K == KTS - 1),
                                    tile_position=(0, 64), skip_group_check=True)

                def fzB(e, K=K, pp=pp, KTS=KTS, o=off):
                    return e.matmul(pstat[32:33, o:512], onescol[:, :],
                                    pB[:, pp * 1024 + 512 + o:pp * 1024 + 1024],
                                    start=(K == 0), stop=(K == KTS - 1),
                                    tile_position=(0, 32), skip_group_check=True)
                on("tensor", lambda e, f=fzA: f(e))
                last_av[pp] = tick("tensor", "pe", fzB)

            av_q = []
            for kt in range(kts):
                p = kt % 2
                s3 = kt % 3
                sAP = SETS[s3]
                m = kt - 4 * c
                off = m * 128 if m > 0 else 0
                w("tensor", "act", last_exp3[s3])
                if hp == 0 and kt < 3:
                    if s3 < 2:
                        pe_bank_wait(2 * s3)
                        pe_bank_wait(2 * s3 + 1)
                    else:
                        w("tensor", pbc_free_sem, pbc_free)
                        w("tensor", px_free_sem, px_free)
                    w("tensor", "dve", q_evac)

                def fsA(e, P=hp, K=kt, sAP=sAP, o=off):
                    return e.matmul(sAP[:, o:512],
                                    kT[0:64, P * 4096 + K * 128:P * 4096 + (K + 1) * 128],
                                    qam[0:64, P * 512 + o:(P + 1) * 512],
                                    start=True, stop=True)

                def fsB(e, P=hp, K=kt, sAP=sAP, o=off):
                    return e.matmul(sAP[:, 512 + o:1024],
                                    kT[64:128, P * 4096 + K * 128:P * 4096 + (K + 1) * 128],
                                    qam[64:128, P * 512 + o:(P + 1) * 512],
                                    start=True, stop=True)
                sc_a = tick("tensor", "pe", fsA)
                sc_done = tick("tensor", "pe", fsB)
                if kt == 2 and pending_late is not None:
                    pending_late()
                    pending_late = None
                if len(av_q) == 2:
                    emit_av(*av_q.pop(0), KTS=kts)

                if kt == 0 and pending_early1 is not None:
                    pending_early1()
                    pending_early1 = None
                if kt == 0 and pending_early2 is not None:
                    pending_early2()
                    pending_early2 = None

                w("scalar", "pe", sc_done)
                w("scalar", "pe", last_av[p])

                if off == 0:
                    def fex(e, pp=p, sAP=sAP):
                        return e.activation(pB[:, pp * 1024:(pp + 1) * 1024],
                                            sAP[:, :],
                                            AF.Exp, scale=ccol_(C_EIGHTH))
                else:
                    def fex(e, pp=p, o=off, sAP=sAP):
                        return e.activation(
                            pB[:, pp * 1024:(pp + 1) * 1024].rearrange(
                                "q (b v) -> q b v", b=2)[:, :, o:],
                            sAP.rearrange("q (b v) -> q b v", b=2)[:, :, o:],
                            AF.Exp, scale=ccol_(C_EIGHTH))
                ex_a = ex_b = tick("scalar", "act", fex)
                last_exp3[s3] = ex_b

                avwA = avwB = 0
                if m >= 0:
                    w("vector", "act", ex_a)
                    avwA = tick("vector", "dve", lambda e, pp=p, mm=m, o=off: e.tensor_tensor(
                        pB[:, pp * 1024 + o:pp * 1024 + 512],
                        pB[:, pp * 1024 + o:pp * 1024 + 512],
                        bigmask[:, (3 - mm) * 128 + o:(3 - mm) * 128 + 512], ALU.mult))
                    w("vector", "act", ex_b)
                    avwB = tick("vector", "dve", lambda e, pp=p, mm=m, o=off: e.tensor_tensor(
                        pB[:, pp * 1024 + 512 + o:pp * 1024 + 1024],
                        pB[:, pp * 1024 + 512 + o:pp * 1024 + 1024],
                        bigmask[:, (3 - mm) * 128 + o:(3 - mm) * 128 + 512], ALU.mult))

                av_q.append((kt, p, ex_a, ex_b, avwA, avwB, off))
            for entry in av_q:
                emit_av(*entry, KTS=kts)
            av_q = []

            def run_end_early1(P=hp, la=last_av[(kts - 1) % 2]):
                nonlocal zfA, oev
                w("vector", "pe", la)
                oev = tick("vector", "dve", lambda e: e.tensor_copy(
                    oy[:, P * 512:(P + 1) * 512], pO[:, :]))
                w("scalar", "pe", la)
                on("scalar", lambda e: e.activation(
                    pstat[64:65, :], pstat[64:65, :], AF.Ln))
                zfA = tick("scalar", "act", lambda e: e.activation(
                    sq[64:65, 0:512], pstat[64:65, :], AF.Exp,
                    scale=ccol[64:65, C_NEG1:C_NEG1 + 1]))

            def run_end_early2():
                nonlocal zfB, zrow_free
                on("scalar", lambda e: e.activation(
                    pstat[32:33, :], pstat[32:33, :], AF.Ln))
                zfB = tick("scalar", "act", lambda e: e.activation(
                    sq[32:33, 512:1024], pstat[32:33, :], AF.Exp,
                    scale=ccol[32:33, C_NEG1:C_NEG1 + 1]))
                zrow_free = zfB

            def run_end_late(P=hp):
                nonlocal pstat_free, pstat_free_sem, zb_mm_last, oy_scaled
                w("tensor", "act", zrow_free)
                w("tensor", pstat_free_sem, pstat_free)
                on("tensor", lambda e: e.matmul(
                    pstat[0:64, :], allones[64:65, 0:64], sq[64:65, 0:512],
                    start=True, stop=True, tile_position=(64, 0),
                    skip_group_check=True))
                zb_mm_last = tick("tensor", "pe", lambda e: e.matmul(
                    pstat[64:128, :], allones[32:33, 0:64], sq[32:33, 512:1024],
                    start=True, stop=True, tile_position=(32, 64),
                    skip_group_check=True))
                w("vector", "pe", zb_mm_last)
                pstat_free = tick("vector", "dve", lambda e: e.tensor_copy(
                    sq[:, 0:512], pstat[:, :]))
                pstat_free_sem = "dve"
                oy_scaled = tick("vector", "dve", lambda e: e.tensor_tensor(
                    oy[:, P * 512:(P + 1) * 512], oy[:, P * 512:(P + 1) * 512],
                    sq[:, 0:512], ALU.mult))
            pending_early1 = run_end_early1
            pending_early2 = run_end_early2
            pending_late = run_end_late
        pending_early1()
        pending_early1 = None
        pending_early2()
        pending_early2 = None
        pending_late()
        pending_late = None

        bank_free[0] = bank_free[1] = last_exp3[0]
        bank_free_sem[0] = bank_free_sem[1] = "act"
        bank_free[2] = bank_free[3] = last_exp3[1]
        bank_free_sem[2] = bank_free_sem[3] = "act"
        pbc_free = last_exp3[2]
        pbc_free_sem = "act"
        px_free = last_exp3[2]
        px_free_sem = "act"

        import os as _os
        if _os.environ.get("DBG_OY") == "1" and c == 0:
            d_oy = nc.dram_tensor("d_oy", [128, 4096], BF16, kind="ExternalOutput").ap()
            w("sync", "dve", oy_scaled)
            tick("sync", "out", lambda e: e.dma_start(out=d_oy, in_=oy[:, :]), 16)

        slot_o0, mo0 = slot_of[(c, "o", 0)]
        slot_o1, mo1 = slot_of[(c, "o", 1)]
        attn_evac = {}
        stats_last = 0
        for oc in range(8):
            slot, mld = (slot_o0, mo0) if oc < 4 else (slot_o1, mo1)
            b = oc % 2
            pe_bank_wait(b)
            w("tensor", mld[0], mld[1])
            w("tensor", "dve", oy_scaled)
            mlast = 0
            for jc in range(NIC):
                def fwo(e, j=jc, s=slot, o=oc, bb=b):
                    return e.matmul(bankF(bb), wap(s, j, o % 4), oy[:, j * 512:(j + 1) * 512],
                                    start=(j == 0), stop=(j == 7))
                if jc == 7:
                    mlast = tick("tensor", "pe", fwo)
                else:
                    on("tensor", lambda e, ff=fwo: ff(e))
            if oc == 3:
                slot_free[slot_o0] = mlast
            if oc == 7:
                slot_free[slot_o1] = mlast
            w("vector", "pe", mlast)
            aev = tick("vector", "dve", lambda e, o=oc, bb=b: e.tensor_copy(
                qam[:, o * 512:(o + 1) * 512], bankF(bb)))
            attn_evac[oc] = aev
            bank_free[b] = aev
            bank_free_sem[b] = "dve"
            sh = oc % 2
            w("vector", "pe", sq_free[sh])
            if oc == 0:
                w("vector", "pe", zb_mm_last)
            msq = tick("vector", "dve", lambda e, o=oc, s=sh: e.tensor_tensor(
                sq[:, s * 512:(s + 1) * 512], qam[:, o * 512:(o + 1) * 512],
                qam[:, o * 512:(o + 1) * 512], ALU.mult))
            w("tensor", "dve", msq)
            if oc == 0:
                w("tensor", pstat_free_sem, pstat_free)

            def fst(e, o=oc, s=sh):
                return e.matmul(pstat[0:1, :], onescol[:, :], sq[:, s * 512:(s + 1) * 512],
                                start=(o == 0), stop=(o == 7))
            stats_last = tick("tensor", "pe", fst)
            sq_free[sh] = stats_last
        ln3 = ln_rows(stats_last, has_mean=False)
        stream_half(c, "cos", 0)
        stream_half(c, "cos", 1)

        bcast(RB_R, bc1[:, :], ln3)
        rowsB_free_pe = last_bcast_pe
        stats_last = 0
        for ic in range(NIC):
            sh = ic % 2
            w("vector", "dve", attn_evac[ic])
            if ic == 0:
                w("vector", "lds", m_xf)
            on("vector", lambda e, i=ic: e.tensor_tensor(
                pO[:, 0:512], qam[:, i * 512:(i + 1) * 512], bc1[:, :], ALU.mult))
            tick("vector", "dve", lambda e, i=ic: e.tensor_tensor(
                xf[:, i * 512:(i + 1) * 512], xf[:, i * 512:(i + 1) * 512],
                pO[:, 0:512], ALU.add))
            w("scalar", "pe", sq_free[sh])
            w("scalar", "dve", n["dve"])
            mact = tick("scalar", "act", lambda e, i=ic, s=sh: e.activation(
                sq[:, s * 512:(s + 1) * 512], xf[:, i * 512:(i + 1) * 512],
                AF.Square))
            w("tensor", "act", mact)
            if ic == 0:
                w("tensor", pbc_free_sem, pbc_free)
                w("tensor", pstat_free_sem, pstat_free)

            def fsq2(e, s=sh, i=ic):
                return e.matmul(pbc[0:1, :], onescol[:, :], sq[:, s * 512:(s + 1) * 512],
                                start=(i == 0), stop=(i == 7))
            sq_free[sh] = tick("tensor", "pe", fsq2)
            f2 = lambda e, i=ic: e.matmul(
                pstat[0:1, :], onescol_f[:, :], xf[:, i * 512:(i + 1) * 512],
                start=(i == 0), stop=(i == 7))
            stats_last = tick("tensor", "pe", f2)
        ln2 = ln_rows(stats_last, has_mean=True)
        bcast(RB_R, bc1[:, :], ln2)
        w("vector", "pe", last_bcast_pe)
        w("vector", "rc", r1_store)
        mlo = tick("vector", "dve", lambda e: e.scalar_tensor_tensor(
            RB_R, RB_MR, crow_(C_NEG1), RFA, ALU.mult, ALU.add))
        w("tensor", pbc_free_sem, pbc_free)
        w("tensor", "dve", mlo)
        mhb = tick("tensor", "pe", lambda e: e.matmul(
            pbc[:, :], allones[0:1, :], RB_MR, start=True, stop=True))
        last_bcast_pe = mhb
        w("vector", "pe", mhb)
        pbc_free = tick("vector", "dve", lambda e: e.tensor_scalar(
            pB[:, 512:1024], pbc[:, :], ccol_(C_NEG1), None, ALU.mult))
        yn_done = 0
        for ic in range(NIC):
            on("vector", lambda e, i=ic: e.tensor_tensor(
                pO[:, 0:512], xf[:, i * 512:(i + 1) * 512], bc1[:, :], ALU.mult))
            yn_done = tick("vector", "dve", lambda e, i=ic: e.tensor_tensor(
                oy[:, i * 512:(i + 1) * 512], pO[:, 0:512], pB[:, 512:1024],
                ALU.add))

        mag_done = {-1: 0}
        stats_sq_last = 0
        real_evac = {}
        ffn_last = {}
        for ch in range(2):
            slot_c, mc_ = slot_of[(c, "cos", ch)]
            for oc in range(ch * 4, ch * 4 + 4):
                bb = oc % 4
                pe_bank_wait(bb)
                w("tensor", mc_[0], mc_[1])
                if oc == ch * 4:
                    w("tensor", "dve", yn_done)
                for ic in range(NIC):
                    on("tensor", lambda e, i=ic, s=slot_c, o=oc, bk=bb: e.matmul(
                        bankF(bk), wap(s, i, o % 4), oy[:, i * 512:(i + 1) * 512],
                        start=(i == 0), stop=False))

                def fmean3(e, o=oc, bk=bb):
                    return e.matmul(bankF(bk), WS["cos"][:, o * 128:(o + 1) * 128], RB_R,
                                    start=False, stop=True)
                ffn_last[("r", oc)] = tick("tensor", "pe", fmean3)
                w("vector", "pe", ffn_last[("r", oc)])
                real_evac[oc] = tick("vector", "dve", lambda e, o=oc, bk=bb: e.tensor_copy(
                    qam[:, o * 512:(o + 1) * 512], bankF(bk)))
                bank_free[bb] = real_evac[oc]
                bank_free_sem[bb] = "dve"
            if ch == 0:
                slot_free[slot_of[(c, "cos", 0)][0]] = ffn_last[("r", 3)]
                stream_half(c, "sin", 0)
            else:
                slot_free[slot_of[(c, "cos", 1)][0]] = ffn_last[("r", 7)]
                stream_half(c, "sin", 1)
        pend_mean4 = []
        mean4_pe = 0

        def emit_mean4(o, mact):
            nonlocal mean4_pe
            w("tensor", "act", mact)
            if o == 0:
                w("tensor", pstat_free_sem, pstat_free)
            mean4_pe = tick("tensor", "pe", lambda e, o=o: e.matmul(
                pstat[0:1, :], onescol[:, :], qam[:, o * 512:(o + 1) * 512],
                start=(o == 0), stop=(o == 7)))

        for ch in range(2):
            slot_s, ms_ = slot_of[(c, "sin", ch)]
            for oc in range(ch * 4, ch * 4 + 4):
                bb = oc % 4
                pe_bank_wait(bb)
                w("tensor", ms_[0], ms_[1])
                for ic in range(NIC):
                    on("tensor", lambda e, i=ic, s=slot_s, o=oc, bk=bb: e.matmul(
                        bankF(bk), wap(s, i, o % 4), oy[:, i * 512:(i + 1) * 512],
                        start=(i == 0), stop=False))

                def fmean4(e, o=oc, bk=bb):
                    return e.matmul(bankF(bk), WS["sin"][:, o * 128:(o + 1) * 128], RB_R,
                                    start=False, stop=True)
                ffn_last[("i", oc)] = tick("tensor", "pe", fmean4)
                if len(pend_mean4) == 2:
                    emit_mean4(*pend_mean4.pop(0))
                w("vector", "pe", ffn_last[("i", oc)])
                w("vector", "pe", sq_free[0])
                w("vector", "act", mag_done[oc - 1])
                on("vector", lambda e, bk=bb: e.tensor_copy(
                    pB[:, 0:512], bankF(bk)))
                on("vector", lambda e, o=oc: e.tensor_tensor(
                    sq[:, 0:512], qam[:, o * 512:(o + 1) * 512],
                    qam[:, o * 512:(o + 1) * 512], ALU.mult))
                on("vector", lambda e: e.tensor_tensor(
                    sq[:, 512:1024], pB[:, 0:512], pB[:, 0:512], ALU.mult))
                m2 = tick("vector", "dve", lambda e: e.tensor_tensor(
                    pO[:, 0:512], sq[:, 0:512], sq[:, 512:1024], ALU.add))
                bank_free[bb] = m2
                bank_free_sem[bb] = "dve"
                w("scalar", "dve", m2)
                on("scalar", lambda e: e.activation(pO[:, 0:512], pO[:, 0:512], AF.Ln))
                mag_act = tick("scalar", "act", lambda e, o=oc: e.activation(
                    qam[:, o * 512:(o + 1) * 512], pO[:, 0:512], AF.Exp,
                    scale=ccol_(C_HALF)))
                mag_done[oc] = mag_act
                pend_mean4.append((oc, mag_act))
            if ch == 0:
                slot_free[slot_of[(c, "sin", 0)][0]] = ffn_last[("i", 3)]
            else:
                slot_free[slot_of[(c, "sin", 1)][0]] = ffn_last[("i", 7)]
        for entry in pend_mean4:
            emit_mean4(*entry)
        pend_mean4 = []
        w("vector", "pe", mean4_pe)
        mrow = tick("vector", "dve", lambda e: e.tensor_scalar(
            RB_R, pstat[0:1, :], crow_(C_MI1024), None, ALU.mult))
        bcast(RB_R, bc1[:, :], mrow)
        w("vector", "act", mag_done[7])
        stats_sq_last = 0
        for oc in range(NIC):
            sh = oc % 2
            w("vector", "pe", sq_free[sh])
            on("vector", lambda e, o=oc: e.tensor_tensor(
                pB[:, 0:512], qam[:, o * 512:(o + 1) * 512], bc1[:, :], ALU.add))
            mdve = tick("vector", "dve", lambda e, o=oc, s=sh: e.tensor_tensor(
                sq[:, s * 512:(s + 1) * 512], pB[:, 0:512],
                qam[:, o * 512:(o + 1) * 512], ALU.mult))
            w("tensor", "dve", mdve)
            if oc == 0:
                w("tensor", pbc_free_sem, pbc_free)
            stats_sq_last = tick("tensor", "pe", lambda e, o=oc, s=sh: e.matmul(
                pbc[0:1, :], onescol[:, :], sq[:, s * 512:(s + 1) * 512],
                start=(o == 0), stop=(o == 7)))
            sq_free[sh] = stats_sq_last
        rowsB_free_pe = n["pe"]
        ln4 = ln_rows(stats_sq_last, has_mean=True, centered=True)

        bcast(RB_R, bc4[:, :], ln4)
        w("tensor", pbc_free_sem, pbc_free)
        w("tensor", "dve", ln4)
        mb4 = tick("tensor", "pe", lambda e: e.matmul(
            pbc[:, :], allones[0:1, :], RB_MR, start=True, stop=True))
        last_bcast_pe = mb4
        w("vector", "pe", mb4)
        pbc_free = tick("vector", "dve", lambda e: e.tensor_scalar(
            bcf[:, :], pbc[:, :], ccol_(C_NEG1), None, ALU.mult))
        rowsB_free_pe = last_bcast_pe
        res2_done = 0
        for ic in range(NIC):
            w("vector", "act", mag_done[ic])
            on("vector", lambda e, i=ic: e.tensor_tensor(
                pO[:, 0:512], qam[:, i * 512:(i + 1) * 512], bc4[:, :], ALU.mult))
            on("vector", lambda e, i=ic: e.tensor_tensor(
                xf[:, i * 512:(i + 1) * 512], xf[:, i * 512:(i + 1) * 512],
                pO[:, 0:512], ALU.add))
            res2_done = tick("vector", "dve", lambda e, i=ic: e.tensor_tensor(
                xf[:, i * 512:(i + 1) * 512], xf[:, i * 512:(i + 1) * 512],
                bcf[:, :], ALU.add))
        res2_prev = res2_done
        w("sync", "dve", res2_done)
        tick("sync", "out", lambda e, cc=c: e.dma_start(
            out=yT.rearrange("(ic p) t -> p ic t", p=128)[:, :, cc * TC:(cc + 1) * TC],
            in_=xf[:, :].rearrange("p (ic t) -> p ic t", ic=8)), 16)
        out_hist.append(n["out"])

    w("sync", "out", n["out"])

    with nc.Block() as blk:
        blk.tensor(lambda e: [f(e) for f in q["tensor"]])
        blk.scalar(lambda e: [f(e) for f in q["scalar"]])
        blk.vector(lambda e: [f(e) for f in q["vector"]])
        blk.gpsimd(lambda e: [f(e) for f in q["gpsimd"]])
        blk.sync(lambda e: [f(e) for f in q["sync"]])
    return nc


def kernel(L0_attn_x, L0_attn_Wq, L0_attn_Wk, L0_attn_Wv, L0_attn_Wo,
           L0_ffn_theta, L0_ffn_phi, L0_ffn_theta_noise, L0_ffn_phi_noise):
    x = np.asarray(L0_attn_x, dtype=np.float32)
    wq = np.ascontiguousarray(np.asarray(L0_attn_Wq, dtype=np.float32).T)
    wk = np.ascontiguousarray(np.asarray(L0_attn_Wk, dtype=np.float32).T)
    wv = np.ascontiguousarray(np.asarray(L0_attn_Wv, dtype=np.float32).T)
    wo = np.ascontiguousarray(np.asarray(L0_attn_Wo, dtype=np.float32).T)
    th = np.ascontiguousarray(np.asarray(L0_ffn_theta, dtype=np.float32))
    ph = np.ascontiguousarray(np.asarray(L0_ffn_phi, dtype=np.float32))
    thn = np.ascontiguousarray(np.asarray(L0_ffn_theta_noise, dtype=np.float32))
    phn = np.ascontiguousarray(np.asarray(L0_ffn_phi_noise, dtype=np.float32))

    bigmask = (np.arange(128)[:, None] <= (np.arange(896)[None, :] - 384)
               ).astype(BFNP)
    onescol = np.ones((128, 1), dtype=BFNP)
    allones = np.ones((128, 128), dtype=BFNP)

    nc = build_nc()
    in_maps = []
    for b in range(NCORE):
        in_maps.append({
            "xT": np.ascontiguousarray(x[b].T),
            "wkT": wk, "wvT": wv, "wqT": wq, "woT": wo,
            "theta": th, "theta_n": thn, "phi": ph, "phi_n": phn,
            "bigmask": bigmask, "onescol": onescol, "allones": allones,
        })
    res = run_bass_kernel_spmd(nc, in_maps, core_ids=list(range(NCORE)))
    out = np.stack([np.asarray(res.results[b]["yT"], dtype=np.float32).T
                    for b in range(NCORE)], axis=0)
    return out


# version 28
# speedup vs baseline: 1.0296x; 1.0028x over previous
import numpy as np
import ml_dtypes
import concourse.bass as bass
import concourse.mybir as mybir
from concourse.bass_utils import run_bass_kernel_spmd

F32 = mybir.dt.float32
BF16 = mybir.dt.bfloat16
AF = mybir.ActivationFunctionType
ALU = mybir.AluOpType
AX = mybir.AxisListType.X
BFNP = ml_dtypes.bfloat16

D = 1024
T = 4096
NCH = 8
TC = 512
NIC = 8
NHP = 8
NCORE = 8

(C_INV1024, C_EIGHTH, C_M05, C_THREE, C_NEG1, C_HALF, C_EPS, C_M18, C_PI2,
 C_MI1024) = range(10)


def build_nc(detect_races=True):
    nc = bass.Bass("TRN2", target_bir_lowering=False, debug=False,
                   dynamic_dma_scratch_size=16384,
                   detect_race_conditions=detect_races)

    xT = nc.dram_tensor("xT", [D, T], F32, kind="ExternalInput").ap()
    w_d = {
        "k": nc.dram_tensor("wkT", [D, D], F32, kind="ExternalInput").ap(),
        "v": nc.dram_tensor("wvT", [D, D], F32, kind="ExternalInput").ap(),
        "q": nc.dram_tensor("wqT", [D, D], F32, kind="ExternalInput").ap(),
        "o": nc.dram_tensor("woT", [D, D], F32, kind="ExternalInput").ap(),
    }
    th_d = nc.dram_tensor("theta", [D, D], F32, kind="ExternalInput").ap()
    thn_d = nc.dram_tensor("theta_n", [D, D], F32, kind="ExternalInput").ap()
    ph_d = nc.dram_tensor("phi", [D, D], F32, kind="ExternalInput").ap()
    phn_d = nc.dram_tensor("phi_n", [D, D], F32, kind="ExternalInput").ap()
    bigmask_d = nc.dram_tensor("bigmask", [128, 896], BF16, kind="ExternalInput").ap()
    onescol_d = nc.dram_tensor("onescol", [128, 1], BF16, kind="ExternalInput").ap()
    allones_d = nc.dram_tensor("allones", [128, 128], BF16, kind="ExternalInput").ap()
    yT = nc.dram_tensor("yT", [D, T], F32, kind="ExternalOutput").ap()

    WIDX = {"k": 0, "v": 1, "q": 2, "o": 3, "cos": 4, "sin": 5}
    wscr = [nc.dram_tensor(f"wscr{i}", [128, 8192], BF16, kind="Internal").ap()
            for i in range(6)]
    rscr_d = nc.dram_tensor("rscr", [1, 512], BF16, kind="Internal").ap()

    A = nc.alloc_sbuf_tensor
    kT = A("kT", [128, 32768], BF16)
    vT = A("vT", [128, 32768], BF16)
    xf = A("xf", [128, 4096], F32)
    xb = A("xb", [128, 4096], BF16)
    qam = A("qam", [128, 4096], BF16)
    oy = A("oy", [128, 4096], BF16)
    pB = A("pB", [128, 2048], BF16)
    wsl = A("wsl", [128, 8192], BF16)
    bigmask = A("bigmask_s", [128, 896], BF16)
    bc1 = A("bc1", [128, 512], BF16)
    rowsF = A("rowsF", [128, 512], F32)
    rowsB = A("rowsB", [128, 1024], BF16)
    wsums = A("wsums", [128, 5120], BF16)
    sq = A("sq", [128, 1024], BF16)
    onescol = A("onescol_s", [128, 1], BF16)
    onescol_f = A("onescol_f", [128, 1], F32)
    allones = A("allones_s", [128, 128], BF16)
    ccol = A("ccol", [128, 10], F32)
    erow = A("erow", [128, 16], F32)
    ecol4 = A("ecol4", [128, 4], F32)

    bcf = nc.alloc_sbuf_tensor_at("bcf", [128, 512], F32,
                                  offset=nc.lookup_mloc(pB).addr)
    bc4 = nc.alloc_sbuf_tensor_at("bc4", [128, 512], BF16,
                                  offset=nc.lookup_mloc(pB).addr + 2048)
    kaddr = nc.lookup_mloc(kT).addr
    vaddr = nc.lookup_mloc(vT).addr
    stage_f = nc.alloc_sbuf_tensor_at("stage_f", [128, 8192], F32, offset=kaddr)
    stage_b = nc.alloc_sbuf_tensor_at("stage_b", [128, 8192], BF16, offset=kaddr + 32768)
    stage2_f = nc.alloc_sbuf_tensor_at("stage2_f", [128, 8192], F32, offset=vaddr)
    stage_b2 = nc.alloc_sbuf_tensor_at("stage_b2", [128, 8192], BF16,
                                       offset=vaddr + 32768)

    parena = nc.alloc_psum_tensor("parena", [128, 2048], F32)
    pO = nc.alloc_psum_tensor("pO", [128, 512], F32)
    pstat = nc.alloc_psum_tensor("pstat", [128, 512], F32)
    parena2 = nc.alloc_psum_tensor("parena2", [128, 1024], F32)
    pbc = parena2[:, 0:512]
    px = parena2[:, 512:1024]

    SETS = [parena[:, 0:1024], parena[:, 1024:2048], parena2[:, 0:1024]]

    sem_names = ["pe", "act", "dve", "ldg", "lds", "out",
                 "ld0", "ld1", "ldx", "ldn", "lsw", "lsw2", "rc", "rc2"]
    sems = {k: nc.alloc_semaphore(k) for k in sem_names}
    q = {k: [] for k in ("tensor", "scalar", "vector", "gpsimd", "sync")}
    n = {k: 0 for k in sem_names}

    def on(eng, f):
        q[eng].append(f)

    def w(eng, key, v):
        if v and v > 0:
            on(eng, lambda e, s=sems[key], vv=v: e.wait_ge(s, vv))

    def tick(eng, key, f, k=1):
        n[key] += k
        on(eng, lambda e, ff=f, s=sems[key], kk=k: ff(e).then_inc(s, kk))
        return n[key]

    def ccol_(i):
        return ccol[:, i:i + 1]

    def crow_(i):
        return ccol[0:1, i:i + 1]

    RFA = bcf[0:1, 0:512]
    RFB = rowsF[0:1, 0:512]
    RB_R = rowsB[0:1, 0:512]
    RB_MR = rowsB[0:1, 512:1024]

    WS = {"k": wsums[0:1, 0:1024], "v": wsums[0:1, 1024:2048],
          "q": wsums[0:1, 2048:3072], "cos": wsums[0:1, 3072:4096],
          "sin": wsums[0:1, 4096:5120]}

    tick("sync", "lds", lambda e: e.dma_start(out=bigmask[:, :], in_=bigmask_d), 16)
    tick("sync", "lds", lambda e: e.dma_start(out=onescol[:, :], in_=onescol_d), 16)
    m_consts = tick("sync", "lds", lambda e: e.dma_start(out=allones[:, :], in_=allones_d), 16)

    for idx, val in [(C_INV1024, 1.0 / 1024), (C_EIGHTH, 0.125), (C_M05, -0.5),
                     (C_THREE, 3.0), (C_NEG1, -1.0), (C_HALF, 0.5), (C_EPS, 1e-5),
                     (C_PI2, float(np.pi / 2)), (C_MI1024, -1.0 / 1024)]:
        on("vector", lambda e, i=idx, v=val: e.memset(ccol[:, i:i + 1], v))
    on("vector", lambda e: e.memset(onescol_f[:, :], 1.0))
    m_memset = tick("vector", "dve",
                    lambda e: e.memset(ccol[:, C_M18:C_M18 + 1], -0.125))

    w("tensor", "lds", m_consts)
    w("scalar", "dve", m_memset)

    stageB = [stage_b, stage_b2]
    ssem = ["lsw", "lsw2"]
    buf_free_pe = [0, 0]
    buf_free_st = [0, 0]
    wscr_ready = {}

    prev_wsum_dve = [0]
    wsum_pe_last = [0]

    def startup_wsum(name, sb):
        w("tensor", "dve", prev_wsum_dve[0])
        for jh in range(2):
            dst = pstat[0:1, :] if jh == 0 else pbc[0:1, :]
            for ic in range(NIC):
                def f(e, d=dst, i=ic, j=jh, sb=sb):
                    return e.matmul(d, onescol[:, :],
                                    sb[:, i * 1024 + j * 512: i * 1024 + (j + 1) * 512],
                                    start=(i == 0), stop=(i == 7))
                if ic == 7 and jh == 1:
                    wsum_pe_last[0] = tick("tensor", "pe", f)
                else:
                    on("tensor", lambda e, ff=f: ff(e))
        w("vector", "pe", wsum_pe_last[0])
        on("vector", lambda e, nm=name: e.tensor_scalar(
            WS[nm][:, 0:512], pstat[0:1, :], crow_(C_NEG1), None, ALU.mult))
        mdone = tick("vector", "dve", lambda e, nm=name: e.tensor_scalar(
            WS[nm][:, 512:1024], pbc[0:1, :], crow_(C_NEG1), None, ALU.mult))
        prev_wsum_dve[0] = mdone
        return mdone

    wsum_dve = 0
    bi = 0
    for name in ["k", "v", "q"]:
        sb = stageB[bi]
        w("gpsimd", "pe", buf_free_pe[bi])
        w("gpsimd", ssem[bi], buf_free_st[bi])
        m_cast = tick("gpsimd", "ldg", lambda e, s=w_d[name], sb=sb: e.dma_start(
            out=sb[:, :], in_=s.rearrange("(ic p) j -> p ic j", p=128)), 16)
        w("tensor", "ldg", m_cast)
        wsum_dve = startup_wsum(name, sb)
        buf_free_pe[bi] = wsum_pe_last[0]
        w("sync", "dve", wsum_dve)
        for hh in range(2):
            buf_free_st[bi] = tick("sync", ssem[bi], lambda e, wi=WIDX[name], sb=sb, hh=hh:
                                   e.dma_start(
                out=wscr[wi][:, hh * 4096:(hh + 1) * 4096],
                in_=sb.rearrange("p (ic hh j) -> p hh ic j", ic=8, hh=2)[:, hh]), 16)
        wscr_ready[name] = (ssem[bi], buf_free_st[bi])
        bi ^= 1

    sb = stageB[bi]
    w("gpsimd", "pe", buf_free_pe[bi])
    w("gpsimd", ssem[bi], buf_free_st[bi])
    m_cast = tick("gpsimd", "ldg", lambda e, sb=sb: e.dma_start(
        out=sb[:, :], in_=w_d["o"].rearrange("(ic p) j -> p ic j", p=128)), 16)
    w("vector", "ldg", m_cast)
    wo_center = 0
    for ic in range(NIC):
        on("vector", lambda e, i=ic, sb=sb: e.tensor_reduce(
            erow[:, 8:9], sb[:, i * 1024:(i + 1) * 1024], AX, ALU.add))
        mneg = tick("vector", "dve", lambda e: e.tensor_scalar(
            erow[:, 8:9], erow[:, 8:9], ccol_(C_MI1024), None, ALU.mult))
        w("vector", "dve", mneg)
        wo_center = tick("vector", "dve", lambda e, i=ic, sb=sb: e.tensor_scalar(
            sb[:, i * 1024:(i + 1) * 1024],
            sb[:, i * 1024:(i + 1) * 1024], erow[:, 8:9], None, ALU.add))
    w("sync", "dve", wo_center)
    for hh in range(2):
        buf_free_st[bi] = tick("sync", ssem[bi], lambda e, sb=sb, hh=hh: e.dma_start(
            out=wscr[WIDX["o"]][:, hh * 4096:(hh + 1) * 4096],
            in_=sb.rearrange("p (ic hh j) -> p hh ic j", ic=8, hh=2)[:, hh]), 16)
    wscr_ready["o"] = (ssem[bi], buf_free_st[bi])
    bi ^= 1

    on("scalar", lambda e: e.preload_act_table(AF.Sin))
    prev_sin = 0
    prev_add = 0
    for name, base_d, noise_d, use_cos in [("cos", th_d, thn_d, True),
                                           ("sin", ph_d, phn_d, False)]:
        sb = stageB[bi]
        w("sync", "act", prev_sin)
        mf = tick("sync", "lds", lambda e, s=base_d: e.dma_start(
            out=stage_f[:, :], in_=s.rearrange("(ic p) j -> p ic j", p=128)), 16)
        w("gpsimd", "dve", prev_add)
        mg = tick("gpsimd", "ldn", lambda e, s=noise_d: e.dma_start(
            out=stage2_f[:, :], in_=s.rearrange("(ic p) j -> p ic j", p=128)), 16)
        w("vector", "lds", mf)
        w("vector", "ldn", mg)
        add_done = tick("vector", "dve", lambda e: e.tensor_tensor(
            stage_f[:, :], stage_f[:, :], stage2_f[:, :], ALU.add))
        prev_add = add_done
        w("scalar", "dve", add_done)
        w("scalar", ssem[bi], buf_free_st[bi])
        w("scalar", "pe", buf_free_pe[bi])
        if use_cos:
            sin_done = tick("scalar", "act", lambda e, sb=sb: e.activation(
                sb[:, :], stage_f[:, :], AF.Sin, bias=ccol_(C_PI2)))
        else:
            sin_done = tick("scalar", "act", lambda e, sb=sb: e.activation(
                sb[:, :], stage_f[:, :], AF.Sin))
        prev_sin = sin_done
        w("tensor", "act", sin_done)
        wsum_dve = startup_wsum(name, sb)
        buf_free_pe[bi] = wsum_pe_last[0]
        w("sync", "dve", wsum_dve)
        w("sync", "act", sin_done)
        for hh in range(2):
            buf_free_st[bi] = tick("sync", ssem[bi], lambda e, wi=WIDX[name], sb=sb, hh=hh:
                                   e.dma_start(
                out=wscr[wi][:, hh * 4096:(hh + 1) * 4096],
                in_=sb.rearrange("p (ic hh j) -> p hh ic j", ic=8, hh=2)[:, hh]), 16)
        wscr_ready[name] = (ssem[bi], buf_free_st[bi])
        bi ^= 1

    startup_lsw = buf_free_st[0]
    startup_lsw2 = buf_free_st[1]
    startup_b0_pe = buf_free_pe[0]
    startup_wsum_dve = wsum_dve

    startup_sin_act = prev_sin

    from concourse.hw_specs import get_activation_tables
    tables = list(get_activation_tables(nc.m.arch).keys())
    nle_idx = tables.index("natural_log_exp_and_others")
    on("scalar", lambda e, idx=nle_idx: e.add_instruction(mybir.InstLoadActFuncSet(
        name=e.bass.get_next_instruction_name(), act_func_set_id=idx,
        ins=[], outs=[])))

    half_seq = 0
    slot_free = [0, 0]
    slot_of = {}

    def stream_half(c, name, h):
        nonlocal half_seq
        slot = half_seq % 2
        half_seq += 1
        sem = "ld0" if slot == 0 else "ld1"
        w("gpsimd", "pe", slot_free[slot])
        w("gpsimd", *wscr_ready[name])
        src = wscr[WIDX[name]][:, h * 4096:(h + 1) * 4096]
        dst = wsl[:, slot * 4096:(slot + 1) * 4096]
        m = tick("gpsimd", sem, lambda e, s=src, d=dst: e.dma_start(out=d, in_=s), 16)
        slot_of[(c, name, h)] = (slot, (sem, m))

    def wap(slot, ic, sub):
        return wsl[:, slot * 4096 + ic * 512 + sub * 128: slot * 4096 + ic * 512 + (sub + 1) * 128]

    def wap512(slot, ic):
        return wsl[:, slot * 4096 + ic * 512: slot * 4096 + (ic + 1) * 512]

    bankF = lambda b: parena[:, b * 512:(b + 1) * 512]
    bank_free = [0, 0, 0, 0]
    bank_free_sem = ["dve", "dve", "dve", "dve"]
    pstat_free = startup_wsum_dve
    pstat_free_sem = "dve"
    px_free_sem = "dve"
    pbc_free = startup_wsum_dve
    pbc_free_sem = "dve"
    px_free = 0
    oev = 0
    oy_scaled = 0
    p3_evac_last = 0
    r1_store = 0
    rowsB_free_pe = 0
    last_bcast_pe = 0
    last_exp3 = [0, 0, 0]
    last_av = {0: 0, 1: 0}
    res2_prev = 0
    zrow_free = 0
    zfA = 0
    zfB = 0
    zb_mm_last = 0
    m_xb_next = None
    sq_free = [0, 0]

    def pe_bank_wait(b):
        w("tensor", bank_free_sem[b], bank_free[b])

    def ln_rows(stats_pe, has_mean, centered=False, fold_r=True):
        nonlocal pstat_free, pbc_free, pstat_free_sem, pbc_free_sem
        pstat_free_sem = "dve"
        pbc_free_sem = "dve"
        w("vector", "pe", stats_pe)
        w("vector", "pe", rowsB_free_pe)
        if has_mean and centered:
            on("vector", lambda e: e.tensor_scalar(RFA, pstat[0:1, :], crow_(C_INV1024), None, ALU.mult))
            mark = tick("vector", "dve", lambda e: e.tensor_scalar(
                RFB, pbc[0:1, :], crow_(C_INV1024), None, ALU.mult))
        elif has_mean:
            on("vector", lambda e: e.tensor_scalar(RFA, pstat[0:1, :], crow_(C_INV1024), None, ALU.mult))
            on("vector", lambda e: e.scalar_tensor_tensor(
                RFB, RFA, crow_(C_NEG1), RFA, ALU.mult, ALU.mult))
            mark = tick("vector", "dve", lambda e: e.scalar_tensor_tensor(
                RFB, pbc[0:1, :], crow_(C_INV1024), RFB, ALU.mult, ALU.add))
        else:
            mark = tick("vector", "dve", lambda e: e.tensor_scalar(
                RFB, pstat[0:1, :], crow_(C_INV1024), None, ALU.mult))
        pstat_free = mark
        pbc_free = mark
        w("scalar", "dve", mark)
        on("scalar", lambda e: e.activation(RFB, RFB, AF.Ln, bias=crow_(C_EPS)))
        r_act = tick("scalar", "act", lambda e: e.activation(RFB, RFB, AF.Exp, scale=crow_(C_M05)))
        w("vector", "act", r_act)
        if has_mean:
            if fold_r:
                on("vector", lambda e: e.tensor_tensor(RFA, RFA, RFB, ALU.mult))
            on("vector", lambda e: e.tensor_copy(RB_R, RFB))
            return tick("vector", "dve", lambda e: e.tensor_copy(RB_MR, RFA))
        else:
            return tick("vector", "dve", lambda e: e.tensor_copy(RB_R, RFB))

    def bcast(row_ap, dst_ap, row_ready_dve):
        nonlocal pbc_free, last_bcast_pe, pbc_free_sem
        w("tensor", pbc_free_sem, pbc_free)
        pbc_free_sem = "dve"
        w("tensor", "dve", row_ready_dve)
        m = tick("tensor", "pe", lambda e, r=row_ap: e.matmul(
            pbc[:, :], allones[0:1, :], r, start=True, stop=True))
        last_bcast_pe = m
        w("vector", "pe", m)
        pbc_free = tick("vector", "dve", lambda e, d=dst_ap: e.tensor_copy(d, pbc[:, :]))
        return pbc_free

    out_hist = [0]
    for c in range(NCH):
        w("sync", "out", out_hist[c])
        m_xf = tick("sync", "lds", lambda e, cc=c: e.dma_start(
            out=xf[:, :].rearrange("p (ic t) -> p ic t", ic=8),
            in_=xT.rearrange("(ic p) t -> p ic t", p=128)[:, :, cc * TC:(cc + 1) * TC]), 16)
        if c == 0:
            m_xb = tick("gpsimd", "ldx", lambda e, cc=c: e.dma_start(
                out=xb[:, :].rearrange("p (ic t) -> p ic t", ic=8),
                in_=xT.rearrange("(ic p) t -> p ic t", p=128)[:, :, cc * TC:(cc + 1) * TC]), 16)
        else:
            m_xb = m_xb_next

        stream_half(c, "k", 0)
        stream_half(c, "k", 1)

        if c == 0:
            w("vector", "act", startup_sin_act)
            w("vector", "lsw", startup_lsw)
            w("vector", "lsw2", startup_lsw2)
            w("vector", "pe", startup_b0_pe)
        w("scalar", "ldx", m_xb)
        stats_last = 0
        for ic in range(NIC):
            sh = ic % 2
            w("scalar", "pe", sq_free[sh])
            mact = tick("scalar", "act", lambda e, i=ic, s=sh: e.activation(
                sq[:, s * 512:(s + 1) * 512], xb[:, i * 512:(i + 1) * 512],
                AF.Square))
            w("tensor", "ldx", m_xb)
            w("tensor", "act", mact)
            if ic == 0:
                w("tensor", pstat_free_sem, pstat_free)
                w("tensor", pbc_free_sem, pbc_free)
            on("tensor", lambda e, i=ic: e.matmul(
                pstat[0:1, :], onescol[:, :], xb[:, i * 512:(i + 1) * 512],
                start=(i == 0), stop=(i == 7)))

            def fsq(e, s=sh, i=ic):
                return e.matmul(pbc[0:1, :], onescol[:, :], sq[:, s * 512:(s + 1) * 512],
                                start=(i == 0), stop=(i == 7))
            stats_last = tick("tensor", "pe", fsq)
            sq_free[sh] = stats_last
        ln1 = ln_rows(stats_last, has_mean=True, fold_r=False)
        bcast(RB_R, bc1[:, :], ln1)
        w("sync", "dve", ln1)
        w("sync", "dve", p3_evac_last)
        r1_store = tick("sync", "rc", lambda e: e.dma_start(
            out=rscr_d, in_=RB_R[0:1, 0:512]), 16)
        w("gpsimd", "rc", r1_store)
        mrc = tick("gpsimd", "rc2", lambda e: e.dma_start(
            out=ecol4[:, :], in_=rscr_d.rearrange("o (s p) -> (o p) s", p=128),
            allow_slow_non_contiguous=True), 16)

        slot_k0, mk0 = slot_of[(c, "k", 0)]
        slot_k1, mk1 = slot_of[(c, "k", 1)]
        for jc in range(8):
            slot, mld = (slot_k0, mk0) if jc < 4 else (slot_k1, mk1)
            b = jc % 2
            pe_bank_wait(b)
            w("tensor", mld[0], mld[1])
            w("tensor", "ldx", m_xb)
            for ic in range(NIC):
                on("tensor", lambda e, i=ic, s=slot, j=jc, bb=b: e.matmul(
                    bankF(bb), wap(s, i, j % 4), xb[:, i * 512:(i + 1) * 512],
                    start=(i == 0), stop=False))

            if jc == 0:
                w("tensor", "dve", ln1)

            def fmean(e, j=jc, bb=b):
                return e.matmul(bankF(bb), WS["k"][:, j * 128:(j + 1) * 128], RB_MR,
                                start=False, stop=True)
            mlast = tick("tensor", "pe", fmean)
            if jc == 3:
                slot_free[slot_k0] = mlast
            if jc == 7:
                slot_free[slot_k1] = mlast
            w("vector", "pe", mlast)
            ev = tick("vector", "dve", lambda e, j=jc, cc=c, bb=b: e.tensor_tensor(
                kT[:, j * 4096 + cc * TC: j * 4096 + (cc + 1) * TC], bankF(bb),
                bc1[:, :], ALU.mult))
            bank_free[b] = ev
            bank_free_sem[b] = "dve"
        stream_half(c, "v", 0)
        stream_half(c, "v", 1)

        slot_v0, mv0 = slot_of[(c, "v", 0)]
        slot_v1, mv1 = slot_of[(c, "v", 1)]
        for jh in range(2):
            slot, mld = (slot_v0, mv0) if jh == 0 else (slot_v1, mv1)
            for st in range(4):
                b = 2 + st % 2
                pe_bank_wait(b)
                w("tensor", mld[0], mld[1])
                for ic in range(NIC):
                    on("tensor", lambda e, i=ic, s=slot, t=st, bb=b: e.matmul(
                        bankF(bb), xb[:, i * 512 + t * 128: i * 512 + (t + 1) * 128],
                        wap512(s, i), start=(i == 0), stop=False))

                if jh == 0 and st == 0:
                    w("tensor", "dve", ln1)

                def fmeanv(e, t=st, j=jh, bb=b):
                    return e.matmul(bankF(bb), RB_MR[:, t * 128:(t + 1) * 128],
                                    WS["v"][:, j * 512:(j + 1) * 512], start=False, stop=True)
                mlast = tick("tensor", "pe", fmeanv)
                if st == 3:
                    if jh == 0:
                        slot_free[slot_v0] = mlast
                    else:
                        slot_free[slot_v1] = mlast
                w("vector", "pe", mlast)
                if jh == 0 and st == 0:
                    w("vector", "rc2", mrc)
                ev = tick("vector", "dve", lambda e, t=st, j=jh, cc=c, bb=b: e.tensor_scalar(
                    vT[:, (cc * 4 + t) * 1024 + j * 512: (cc * 4 + t) * 1024 + (j + 1) * 512],
                    bankF(bb), ecol4[:, t:t + 1], None, ALU.mult))
                bank_free[b] = ev
                bank_free_sem[b] = "dve"
        p3_evac_last = n["dve"]
        stream_half(c, "q", 0)
        stream_half(c, "q", 1)

        slot_q0, mq0 = slot_of[(c, "q", 0)]
        slot_q1, mq1 = slot_of[(c, "q", 1)]
        q_evac = 0
        for jc in range(8):
            slot, mld = (slot_q0, mq0) if jc < 4 else (slot_q1, mq1)
            b = jc % 2
            pe_bank_wait(b)
            w("tensor", mld[0], mld[1])
            w("tensor", "ldx", m_xb)
            for ic in range(NIC):
                on("tensor", lambda e, i=ic, s=slot, j=jc, bb=b: e.matmul(
                    bankF(bb), wap(s, i, j % 4), xb[:, i * 512:(i + 1) * 512],
                    start=(i == 0), stop=False))

            def fmeanq(e, j=jc, bb=b):
                return e.matmul(bankF(bb), WS["q"][:, j * 128:(j + 1) * 128], RB_MR,
                                start=False, stop=True)
            mlast = tick("tensor", "pe", fmeanq)
            if jc == 3:
                slot_free[slot_q0] = mlast
            if jc == 7:
                slot_free[slot_q1] = mlast
            w("vector", "pe", mlast)
            q_evac = tick("vector", "dve", lambda e, j=jc, bb=b: e.tensor_tensor(
                qam[:, j * 512:(j + 1) * 512], bankF(bb), bc1[:, :], ALU.mult))
            bank_free[b] = q_evac
            bank_free_sem[b] = "dve"
        xb_free_pe = n["pe"]
        rowsB_free_pe = n["pe"]
        stream_half(c, "o", 0)
        stream_half(c, "o", 1)

        if c + 1 < NCH:
            w("gpsimd", "pe", xb_free_pe)
            m_xb_next = tick("gpsimd", "ldx", lambda e, cc=c + 1: e.dma_start(
                out=xb[:, :].rearrange("p (ic t) -> p ic t", ic=8),
                in_=xT.rearrange("(ic p) t -> p ic t", p=128)[:, :, cc * TC:(cc + 1) * TC]), 16)

        kts = 4 * c + 4
        w("scalar", "dve", res2_prev)
        if c == 0:
            w("vector", "lds", m_consts)
        pending_early1 = None
        pending_early2 = None
        pending_late = None
        for hp in range(NHP):
            if hp == 0:
                w("tensor", pstat_free_sem, pstat_free)
                w("tensor", px_free_sem, px_free)
                w("tensor", "dve", res2_prev)
            def emit_av(K, pp, ex_a, ex_b, avwA, avwB, off, P=hp, KTS=None):
                if K == 0:
                    w("tensor", "dve", oev)
                w("tensor", "act", ex_a)
                if avwA:
                    w("tensor", "dve", avwA)

                def favA(e, P=P, K=K, pp=pp, KTS=KTS, o=off):
                    return e.matmul(pO[0:64, o:512],
                                    vT[:, K * 1024 + P * 128: K * 1024 + P * 128 + 64],
                                    pB[:, pp * 1024 + o:pp * 1024 + 512],
                                    start=(K == 0), stop=(K == KTS - 1),
                                    tile_position=(0, 0), skip_group_check=True)
                on("tensor", lambda e, f=favA: f(e))
                w("tensor", "act", ex_b)
                if avwB:
                    w("tensor", "dve", avwB)

                def favB(e, P=P, K=K, pp=pp, KTS=KTS, o=off):
                    return e.matmul(pO[64:128, o:512],
                                    vT[:, K * 1024 + P * 128 + 64: K * 1024 + P * 128 + 128],
                                    pB[:, pp * 1024 + 512 + o:pp * 1024 + 1024],
                                    start=(K == 0), stop=(K == KTS - 1),
                                    tile_position=(0, 64), skip_group_check=True)
                on("tensor", lambda e, f=favB: f(e))

                if K == 0:
                    w("tensor", pstat_free_sem, pstat_free)

                def fzA(e, K=K, pp=pp, KTS=KTS, o=off):
                    return e.matmul(pstat[64:65, o:512], onescol[:, :],
                                    pB[:, pp * 1024 + o:pp * 1024 + 512],
                                    start=(K == 0), stop=(K == KTS - 1),
                                    tile_position=(0, 64), skip_group_check=True)

                def fzB(e, K=K, pp=pp, KTS=KTS, o=off):
                    return e.matmul(pstat[32:33, o:512], onescol[:, :],
                                    pB[:, pp * 1024 + 512 + o:pp * 1024 + 1024],
                                    start=(K == 0), stop=(K == KTS - 1),
                                    tile_position=(0, 32), skip_group_check=True)
                on("tensor", lambda e, f=fzA: f(e))
                last_av[pp] = tick("tensor", "pe", fzB)

            av_q = []
            for kt in range(kts):
                p = kt % 2
                s3 = kt % 3
                sAP = SETS[s3]
                m = kt - 4 * c
                off = m * 128 if m > 0 else 0
                w("tensor", "act", last_exp3[s3])
                if hp == 0 and kt < 3:
                    if s3 < 2:
                        pe_bank_wait(2 * s3)
                        pe_bank_wait(2 * s3 + 1)
                    else:
                        w("tensor", pbc_free_sem, pbc_free)
                        w("tensor", px_free_sem, px_free)
                    w("tensor", "dve", q_evac)

                def fsA(e, P=hp, K=kt, sAP=sAP, o=off):
                    return e.matmul(sAP[:, o:512],
                                    kT[0:64, P * 4096 + K * 128:P * 4096 + (K + 1) * 128],
                                    qam[0:64, P * 512 + o:(P + 1) * 512],
                                    start=True, stop=True)

                def fsB(e, P=hp, K=kt, sAP=sAP, o=off):
                    return e.matmul(sAP[:, 512 + o:1024],
                                    kT[64:128, P * 4096 + K * 128:P * 4096 + (K + 1) * 128],
                                    qam[64:128, P * 512 + o:(P + 1) * 512],
                                    start=True, stop=True)
                sc_a = tick("tensor", "pe", fsA)
                sc_done = tick("tensor", "pe", fsB)
                if kt == 2 and pending_late is not None:
                    pending_late()
                    pending_late = None
                if len(av_q) == 2:
                    emit_av(*av_q.pop(0), KTS=kts)

                if kt == 0 and pending_early1 is not None:
                    pending_early1()
                    pending_early1 = None
                if kt == 0 and pending_early2 is not None:
                    pending_early2()
                    pending_early2 = None

                w("scalar", "pe", sc_done)
                w("scalar", "pe", last_av[p])

                if off == 0:
                    def fex(e, pp=p, sAP=sAP):
                        return e.activation(pB[:, pp * 1024:(pp + 1) * 1024],
                                            sAP[:, :],
                                            AF.Exp, scale=ccol_(C_EIGHTH))
                else:
                    def fex(e, pp=p, o=off, sAP=sAP):
                        return e.activation(
                            pB[:, pp * 1024:(pp + 1) * 1024].rearrange(
                                "q (b v) -> q b v", b=2)[:, :, o:],
                            sAP.rearrange("q (b v) -> q b v", b=2)[:, :, o:],
                            AF.Exp, scale=ccol_(C_EIGHTH))
                ex_a = ex_b = tick("scalar", "act", fex)
                last_exp3[s3] = ex_b

                avwA = avwB = 0
                if m >= 0:
                    w("vector", "act", ex_a)
                    avwA = tick("vector", "dve", lambda e, pp=p, mm=m, o=off: e.tensor_tensor(
                        pB[:, pp * 1024 + o:pp * 1024 + 512],
                        pB[:, pp * 1024 + o:pp * 1024 + 512],
                        bigmask[:, (3 - mm) * 128 + o:(3 - mm) * 128 + 512], ALU.mult))
                    w("vector", "act", ex_b)
                    avwB = tick("vector", "dve", lambda e, pp=p, mm=m, o=off: e.tensor_tensor(
                        pB[:, pp * 1024 + 512 + o:pp * 1024 + 1024],
                        pB[:, pp * 1024 + 512 + o:pp * 1024 + 1024],
                        bigmask[:, (3 - mm) * 128 + o:(3 - mm) * 128 + 512], ALU.mult))

                av_q.append((kt, p, ex_a, ex_b, avwA, avwB, off))
            for entry in av_q:
                emit_av(*entry, KTS=kts)
            av_q = []

            def run_end_early1(P=hp, la=last_av[(kts - 1) % 2]):
                nonlocal zfA, oev
                w("vector", "pe", la)
                oev = tick("vector", "dve", lambda e: e.tensor_copy(
                    oy[:, P * 512:(P + 1) * 512], pO[:, :]))
                w("scalar", "pe", la)
                on("scalar", lambda e: e.activation(
                    pstat[64:65, :], pstat[64:65, :], AF.Ln))
                zfA = tick("scalar", "act", lambda e: e.activation(
                    sq[64:65, 0:512], pstat[64:65, :], AF.Exp,
                    scale=ccol[64:65, C_NEG1:C_NEG1 + 1]))

            def run_end_early2():
                nonlocal zfB, zrow_free
                on("scalar", lambda e: e.activation(
                    pstat[32:33, :], pstat[32:33, :], AF.Ln))
                zfB = tick("scalar", "act", lambda e: e.activation(
                    sq[32:33, 512:1024], pstat[32:33, :], AF.Exp,
                    scale=ccol[32:33, C_NEG1:C_NEG1 + 1]))
                zrow_free = zfB

            def run_end_late(P=hp):
                nonlocal pstat_free, pstat_free_sem, zb_mm_last, oy_scaled
                w("tensor", "act", zrow_free)
                w("tensor", pstat_free_sem, pstat_free)
                on("tensor", lambda e: e.matmul(
                    pstat[0:64, :], allones[64:65, 0:64], sq[64:65, 0:512],
                    start=True, stop=True, tile_position=(64, 0),
                    skip_group_check=True))
                zb_mm_last = tick("tensor", "pe", lambda e: e.matmul(
                    pstat[64:128, :], allones[32:33, 0:64], sq[32:33, 512:1024],
                    start=True, stop=True, tile_position=(32, 64),
                    skip_group_check=True))
                w("vector", "pe", zb_mm_last)
                pstat_free = tick("vector", "dve", lambda e: e.tensor_copy(
                    sq[:, 0:512], pstat[:, :]))
                pstat_free_sem = "dve"
                oy_scaled = tick("vector", "dve", lambda e: e.tensor_tensor(
                    oy[:, P * 512:(P + 1) * 512], oy[:, P * 512:(P + 1) * 512],
                    sq[:, 0:512], ALU.mult))
            pending_early1 = run_end_early1
            pending_early2 = run_end_early2
            pending_late = run_end_late
        pending_early1()
        pending_early1 = None
        pending_early2()
        pending_early2 = None
        pending_late()
        pending_late = None

        bank_free[0] = bank_free[1] = last_exp3[0]
        bank_free_sem[0] = bank_free_sem[1] = "act"
        bank_free[2] = bank_free[3] = last_exp3[1]
        bank_free_sem[2] = bank_free_sem[3] = "act"
        pbc_free = last_exp3[2]
        pbc_free_sem = "act"
        px_free = last_exp3[2]
        px_free_sem = "act"

        import os as _os
        if _os.environ.get("DBG_OY") == "1" and c == 0:
            d_oy = nc.dram_tensor("d_oy", [128, 4096], BF16, kind="ExternalOutput").ap()
            w("sync", "dve", oy_scaled)
            tick("sync", "out", lambda e: e.dma_start(out=d_oy, in_=oy[:, :]), 16)

        slot_o0, mo0 = slot_of[(c, "o", 0)]
        slot_o1, mo1 = slot_of[(c, "o", 1)]
        attn_evac = {}
        stats_last = 0
        for oc in range(8):
            slot, mld = (slot_o0, mo0) if oc < 4 else (slot_o1, mo1)
            b = oc % 2
            pe_bank_wait(b)
            w("tensor", mld[0], mld[1])
            w("tensor", "dve", oy_scaled)
            mlast = 0
            for jc in range(NIC):
                def fwo(e, j=jc, s=slot, o=oc, bb=b):
                    return e.matmul(bankF(bb), wap(s, j, o % 4), oy[:, j * 512:(j + 1) * 512],
                                    start=(j == 0), stop=(j == 7))
                if jc == 7:
                    mlast = tick("tensor", "pe", fwo)
                else:
                    on("tensor", lambda e, ff=fwo: ff(e))
            if oc == 3:
                slot_free[slot_o0] = mlast
            if oc == 7:
                slot_free[slot_o1] = mlast
            w("vector", "pe", mlast)
            aev = tick("vector", "dve", lambda e, o=oc, bb=b: e.tensor_copy(
                qam[:, o * 512:(o + 1) * 512], bankF(bb)))
            attn_evac[oc] = aev
            bank_free[b] = aev
            bank_free_sem[b] = "dve"
            sh = oc % 2
            w("vector", "pe", sq_free[sh])
            if oc == 0:
                w("vector", "pe", zb_mm_last)
            msq = tick("vector", "dve", lambda e, o=oc, s=sh: e.tensor_tensor(
                sq[:, s * 512:(s + 1) * 512], qam[:, o * 512:(o + 1) * 512],
                qam[:, o * 512:(o + 1) * 512], ALU.mult))
            w("tensor", "dve", msq)
            if oc == 0:
                w("tensor", pstat_free_sem, pstat_free)

            def fst(e, o=oc, s=sh):
                return e.matmul(pstat[0:1, :], onescol[:, :], sq[:, s * 512:(s + 1) * 512],
                                start=(o == 0), stop=(o == 7))
            stats_last = tick("tensor", "pe", fst)
            sq_free[sh] = stats_last
        ln3 = ln_rows(stats_last, has_mean=False)
        stream_half(c, "cos", 0)
        stream_half(c, "cos", 1)

        bcast(RB_R, bc1[:, :], ln3)
        rowsB_free_pe = last_bcast_pe
        stats_last = 0
        for ic in range(NIC):
            sh = ic % 2
            w("vector", "dve", attn_evac[ic])
            if ic == 0:
                w("vector", "lds", m_xf)
            on("vector", lambda e, i=ic: e.tensor_tensor(
                pO[:, 0:512], qam[:, i * 512:(i + 1) * 512], bc1[:, :], ALU.mult))
            tick("vector", "dve", lambda e, i=ic: e.tensor_tensor(
                xf[:, i * 512:(i + 1) * 512], xf[:, i * 512:(i + 1) * 512],
                pO[:, 0:512], ALU.add))
            w("scalar", "pe", sq_free[sh])
            w("scalar", "dve", n["dve"])
            mact = tick("scalar", "act", lambda e, i=ic, s=sh: e.activation(
                sq[:, s * 512:(s + 1) * 512], xf[:, i * 512:(i + 1) * 512],
                AF.Square))
            w("tensor", "act", mact)
            if ic == 0:
                w("tensor", pbc_free_sem, pbc_free)
                w("tensor", pstat_free_sem, pstat_free)

            def fsq2(e, s=sh, i=ic):
                return e.matmul(pbc[0:1, :], onescol[:, :], sq[:, s * 512:(s + 1) * 512],
                                start=(i == 0), stop=(i == 7))
            sq_free[sh] = tick("tensor", "pe", fsq2)
            f2 = lambda e, i=ic: e.matmul(
                pstat[0:1, :], onescol_f[:, :], xf[:, i * 512:(i + 1) * 512],
                start=(i == 0), stop=(i == 7))
            stats_last = tick("tensor", "pe", f2)
        ln2 = ln_rows(stats_last, has_mean=True)
        bcast(RB_R, bc1[:, :], ln2)
        w("vector", "pe", last_bcast_pe)
        w("vector", "rc", r1_store)
        mlo = tick("vector", "dve", lambda e: e.scalar_tensor_tensor(
            RB_R, RB_MR, crow_(C_NEG1), RFA, ALU.mult, ALU.add))
        w("tensor", pbc_free_sem, pbc_free)
        w("tensor", "dve", mlo)
        mhb = tick("tensor", "pe", lambda e: e.matmul(
            pbc[:, :], allones[0:1, :], RB_MR, start=True, stop=True))
        last_bcast_pe = mhb
        w("vector", "pe", mhb)
        pbc_free = tick("vector", "dve", lambda e: e.tensor_scalar(
            pB[:, 512:1024], pbc[:, :], ccol_(C_NEG1), None, ALU.mult))
        yn_done = 0
        for ic in range(NIC):
            on("vector", lambda e, i=ic: e.tensor_tensor(
                pO[:, 0:512], xf[:, i * 512:(i + 1) * 512], bc1[:, :], ALU.mult))
            yn_done = tick("vector", "dve", lambda e, i=ic: e.tensor_tensor(
                oy[:, i * 512:(i + 1) * 512], pO[:, 0:512], pB[:, 512:1024],
                ALU.add))

        mag_done = {-1: 0}
        stats_sq_last = 0
        real_evac = {}
        ffn_last = {}
        for ch in range(2):
            slot_c, mc_ = slot_of[(c, "cos", ch)]
            for oc in range(ch * 4, ch * 4 + 4):
                bb = oc % 4
                pe_bank_wait(bb)
                w("tensor", mc_[0], mc_[1])
                if oc == ch * 4:
                    w("tensor", "dve", yn_done)
                for ic in range(NIC):
                    on("tensor", lambda e, i=ic, s=slot_c, o=oc, bk=bb: e.matmul(
                        bankF(bk), wap(s, i, o % 4), oy[:, i * 512:(i + 1) * 512],
                        start=(i == 0), stop=False))

                def fmean3(e, o=oc, bk=bb):
                    return e.matmul(bankF(bk), WS["cos"][:, o * 128:(o + 1) * 128], RB_R,
                                    start=False, stop=True)
                ffn_last[("r", oc)] = tick("tensor", "pe", fmean3)
                w("vector", "pe", ffn_last[("r", oc)])
                real_evac[oc] = tick("vector", "dve", lambda e, o=oc, bk=bb: e.tensor_copy(
                    qam[:, o * 512:(o + 1) * 512], bankF(bk)))
                bank_free[bb] = real_evac[oc]
                bank_free_sem[bb] = "dve"
            if ch == 0:
                slot_free[slot_of[(c, "cos", 0)][0]] = ffn_last[("r", 3)]
                stream_half(c, "sin", 0)
            else:
                slot_free[slot_of[(c, "cos", 1)][0]] = ffn_last[("r", 7)]
                stream_half(c, "sin", 1)
        pend_mean4 = []
        mean4_pe = 0

        def emit_mean4(o, mact):
            nonlocal mean4_pe
            w("tensor", "act", mact)
            if o == 0:
                w("tensor", pstat_free_sem, pstat_free)
            mean4_pe = tick("tensor", "pe", lambda e, o=o: e.matmul(
                pstat[0:1, :], onescol[:, :], qam[:, o * 512:(o + 1) * 512],
                start=(o == 0), stop=(o == 7)))

        for ch in range(2):
            slot_s, ms_ = slot_of[(c, "sin", ch)]
            for oc in range(ch * 4, ch * 4 + 4):
                bb = oc % 4
                pe_bank_wait(bb)
                w("tensor", ms_[0], ms_[1])
                for ic in range(NIC):
                    on("tensor", lambda e, i=ic, s=slot_s, o=oc, bk=bb: e.matmul(
                        bankF(bk), wap(s, i, o % 4), oy[:, i * 512:(i + 1) * 512],
                        start=(i == 0), stop=False))

                def fmean4(e, o=oc, bk=bb):
                    return e.matmul(bankF(bk), WS["sin"][:, o * 128:(o + 1) * 128], RB_R,
                                    start=False, stop=True)
                ffn_last[("i", oc)] = tick("tensor", "pe", fmean4)
                if len(pend_mean4) == 2:
                    emit_mean4(*pend_mean4.pop(0))
                w("vector", "pe", ffn_last[("i", oc)])
                w("vector", "pe", sq_free[0])
                w("vector", "act", mag_done[oc - 1])
                on("vector", lambda e, bk=bb: e.tensor_copy(
                    pB[:, 0:512], bankF(bk)))
                on("vector", lambda e, o=oc: e.tensor_tensor(
                    sq[:, 0:512], qam[:, o * 512:(o + 1) * 512],
                    qam[:, o * 512:(o + 1) * 512], ALU.mult))
                on("vector", lambda e: e.tensor_tensor(
                    sq[:, 512:1024], pB[:, 0:512], pB[:, 0:512], ALU.mult))
                m2 = tick("vector", "dve", lambda e: e.tensor_tensor(
                    pO[:, 0:512], sq[:, 0:512], sq[:, 512:1024], ALU.add))
                bank_free[bb] = m2
                bank_free_sem[bb] = "dve"
                w("scalar", "dve", m2)
                on("scalar", lambda e: e.activation(pO[:, 0:512], pO[:, 0:512], AF.Ln))
                mag_act = tick("scalar", "act", lambda e, o=oc: e.activation(
                    qam[:, o * 512:(o + 1) * 512], pO[:, 0:512], AF.Exp,
                    scale=ccol_(C_HALF)))
                mag_done[oc] = mag_act
                pend_mean4.append((oc, mag_act))
            if ch == 0:
                slot_free[slot_of[(c, "sin", 0)][0]] = ffn_last[("i", 3)]
            else:
                slot_free[slot_of[(c, "sin", 1)][0]] = ffn_last[("i", 7)]
        for entry in pend_mean4:
            emit_mean4(*entry)
        pend_mean4 = []
        w("vector", "pe", mean4_pe)
        mrow = tick("vector", "dve", lambda e: e.tensor_scalar(
            RB_R, pstat[0:1, :], crow_(C_MI1024), None, ALU.mult))
        bcast(RB_R, bc1[:, :], mrow)
        w("vector", "act", mag_done[7])
        stats_sq_last = 0
        for oc in range(NIC):
            sh = oc % 2
            w("vector", "pe", sq_free[sh])
            on("vector", lambda e, o=oc: e.tensor_tensor(
                pB[:, 0:512], qam[:, o * 512:(o + 1) * 512], bc1[:, :], ALU.add))
            mdve = tick("vector", "dve", lambda e, o=oc, s=sh: e.tensor_tensor(
                sq[:, s * 512:(s + 1) * 512], pB[:, 0:512],
                qam[:, o * 512:(o + 1) * 512], ALU.mult))
            w("tensor", "dve", mdve)
            if oc == 0:
                w("tensor", pbc_free_sem, pbc_free)
            stats_sq_last = tick("tensor", "pe", lambda e, o=oc, s=sh: e.matmul(
                pbc[0:1, :], onescol[:, :], sq[:, s * 512:(s + 1) * 512],
                start=(o == 0), stop=(o == 7)))
            sq_free[sh] = stats_sq_last
        rowsB_free_pe = n["pe"]
        ln4 = ln_rows(stats_sq_last, has_mean=True, centered=True)

        bcast(RB_R, bc4[:, :], ln4)
        w("tensor", pbc_free_sem, pbc_free)
        w("tensor", "dve", ln4)
        mb4 = tick("tensor", "pe", lambda e: e.matmul(
            pbc[:, :], allones[0:1, :], RB_MR, start=True, stop=True))
        last_bcast_pe = mb4
        w("vector", "pe", mb4)
        pbc_free = tick("vector", "dve", lambda e: e.tensor_scalar(
            bcf[:, :], pbc[:, :], ccol_(C_NEG1), None, ALU.mult))
        rowsB_free_pe = last_bcast_pe
        res2_done = 0
        for ic in range(NIC):
            w("vector", "act", mag_done[ic])
            on("vector", lambda e, i=ic: e.tensor_tensor(
                pO[:, 0:512], qam[:, i * 512:(i + 1) * 512], bc4[:, :], ALU.mult))
            on("vector", lambda e, i=ic: e.tensor_tensor(
                xf[:, i * 512:(i + 1) * 512], xf[:, i * 512:(i + 1) * 512],
                pO[:, 0:512], ALU.add))
            res2_done = tick("vector", "dve", lambda e, i=ic: e.tensor_tensor(
                xf[:, i * 512:(i + 1) * 512], xf[:, i * 512:(i + 1) * 512],
                bcf[:, :], ALU.add))
        res2_prev = res2_done
        w("sync", "dve", res2_done)
        tick("sync", "out", lambda e, cc=c: e.dma_start(
            out=yT.rearrange("(ic p) t -> p ic t", p=128)[:, :, cc * TC:(cc + 1) * TC],
            in_=xf[:, :].rearrange("p (ic t) -> p ic t", ic=8)), 16)
        out_hist.append(n["out"])

    w("sync", "out", n["out"])

    with nc.Block() as blk:
        blk.tensor(lambda e: [f(e) for f in q["tensor"]])
        blk.scalar(lambda e: [f(e) for f in q["scalar"]])
        blk.vector(lambda e: [f(e) for f in q["vector"]])
        blk.gpsimd(lambda e: [f(e) for f in q["gpsimd"]])
        blk.sync(lambda e: [f(e) for f in q["sync"]])
    return nc


def kernel(L0_attn_x, L0_attn_Wq, L0_attn_Wk, L0_attn_Wv, L0_attn_Wo,
           L0_ffn_theta, L0_ffn_phi, L0_ffn_theta_noise, L0_ffn_phi_noise):
    x = np.asarray(L0_attn_x, dtype=np.float32)
    wq = np.ascontiguousarray(np.asarray(L0_attn_Wq, dtype=np.float32).T)
    wk = np.ascontiguousarray(np.asarray(L0_attn_Wk, dtype=np.float32).T)
    wv = np.ascontiguousarray(np.asarray(L0_attn_Wv, dtype=np.float32).T)
    wo = np.ascontiguousarray(np.asarray(L0_attn_Wo, dtype=np.float32).T)
    th = np.ascontiguousarray(np.asarray(L0_ffn_theta, dtype=np.float32))
    ph = np.ascontiguousarray(np.asarray(L0_ffn_phi, dtype=np.float32))
    thn = np.ascontiguousarray(np.asarray(L0_ffn_theta_noise, dtype=np.float32))
    phn = np.ascontiguousarray(np.asarray(L0_ffn_phi_noise, dtype=np.float32))

    bigmask = (np.arange(128)[:, None] <= (np.arange(896)[None, :] - 384)
               ).astype(BFNP)
    onescol = np.ones((128, 1), dtype=BFNP)
    allones = np.ones((128, 128), dtype=BFNP)

    nc = build_nc()
    in_maps = []
    for b in range(NCORE):
        in_maps.append({
            "xT": np.ascontiguousarray(x[b].T),
            "wkT": wk, "wvT": wv, "wqT": wq, "woT": wo,
            "theta": th, "theta_n": thn, "phi": ph, "phi_n": phn,
            "bigmask": bigmask, "onescol": onescol, "allones": allones,
        })
    res = run_bass_kernel_spmd(nc, in_maps, core_ids=list(range(NCORE)))
    out = np.stack([np.asarray(res.results[b]["yT"], dtype=np.float32).T
                    for b in range(NCORE)], axis=0)
    return out


# version 29
# speedup vs baseline: 1.0330x; 1.0033x over previous
import numpy as np
import ml_dtypes
import concourse.bass as bass
import concourse.mybir as mybir
from concourse.bass_utils import run_bass_kernel_spmd

F32 = mybir.dt.float32
BF16 = mybir.dt.bfloat16
AF = mybir.ActivationFunctionType
ALU = mybir.AluOpType
AX = mybir.AxisListType.X
BFNP = ml_dtypes.bfloat16

D = 1024
T = 4096
NCH = 8
TC = 512
NIC = 8
NHP = 8
NCORE = 8

(C_INV1024, C_EIGHTH, C_M05, C_THREE, C_NEG1, C_HALF, C_EPS, C_M18, C_PI2,
 C_MI1024) = range(10)


def build_nc(detect_races=True):
    nc = bass.Bass("TRN2", target_bir_lowering=False, debug=False,
                   dynamic_dma_scratch_size=16384,
                   detect_race_conditions=detect_races)

    xT = nc.dram_tensor("xT", [D, T], F32, kind="ExternalInput").ap()
    w_d = {
        "k": nc.dram_tensor("wkT", [D, D], F32, kind="ExternalInput").ap(),
        "v": nc.dram_tensor("wvT", [D, D], F32, kind="ExternalInput").ap(),
        "q": nc.dram_tensor("wqT", [D, D], F32, kind="ExternalInput").ap(),
        "o": nc.dram_tensor("woT", [D, D], F32, kind="ExternalInput").ap(),
    }
    th_d = nc.dram_tensor("theta", [D, D], F32, kind="ExternalInput").ap()
    thn_d = nc.dram_tensor("theta_n", [D, D], F32, kind="ExternalInput").ap()
    ph_d = nc.dram_tensor("phi", [D, D], F32, kind="ExternalInput").ap()
    phn_d = nc.dram_tensor("phi_n", [D, D], F32, kind="ExternalInput").ap()
    bigmask_d = nc.dram_tensor("bigmask", [128, 896], BF16, kind="ExternalInput").ap()
    onescol_d = nc.dram_tensor("onescol", [128, 1], BF16, kind="ExternalInput").ap()
    allones_d = nc.dram_tensor("allones", [128, 128], BF16, kind="ExternalInput").ap()
    yT = nc.dram_tensor("yT", [D, T], F32, kind="ExternalOutput").ap()

    WIDX = {"k": 0, "v": 1, "q": 2, "o": 3, "cos": 4, "sin": 5}
    wscr = [nc.dram_tensor(f"wscr{i}", [128, 8192], BF16, kind="Internal").ap()
            for i in range(6)]
    rscr_d = nc.dram_tensor("rscr", [1, 512], BF16, kind="Internal").ap()

    A = nc.alloc_sbuf_tensor
    kT = A("kT", [128, 32768], BF16)
    vT = A("vT", [128, 32768], BF16)
    xf = A("xf", [128, 4096], F32)
    xb = A("xb", [128, 4096], BF16)
    qam = A("qam", [128, 4096], BF16)
    oy = A("oy", [128, 4096], BF16)
    pB = A("pB", [128, 2048], BF16)
    wsl = A("wsl", [128, 8192], BF16)
    bigmask = A("bigmask_s", [128, 896], BF16)
    bc1 = A("bc1", [128, 512], BF16)
    rowsF = A("rowsF", [128, 512], F32)
    rowsB = A("rowsB", [128, 1024], BF16)
    wsums = A("wsums", [128, 5120], BF16)
    sq = A("sq", [128, 1024], BF16)
    onescol = A("onescol_s", [128, 1], BF16)
    onescol_f = A("onescol_f", [128, 1], F32)
    allones = A("allones_s", [128, 128], BF16)
    ccol = A("ccol", [128, 10], F32)
    erow = A("erow", [128, 16], F32)
    ecol4 = A("ecol4", [128, 4], F32)

    bcf = nc.alloc_sbuf_tensor_at("bcf", [128, 512], F32,
                                  offset=nc.lookup_mloc(pB).addr)
    bc4 = nc.alloc_sbuf_tensor_at("bc4", [128, 512], BF16,
                                  offset=nc.lookup_mloc(pB).addr + 2048)
    kaddr = nc.lookup_mloc(kT).addr
    vaddr = nc.lookup_mloc(vT).addr
    stage_f = nc.alloc_sbuf_tensor_at("stage_f", [128, 8192], F32, offset=kaddr)
    stage_b = nc.alloc_sbuf_tensor_at("stage_b", [128, 8192], BF16, offset=kaddr + 32768)
    stage2_f = nc.alloc_sbuf_tensor_at("stage2_f", [128, 8192], F32, offset=vaddr)
    stage_b2 = nc.alloc_sbuf_tensor_at("stage_b2", [128, 8192], BF16,
                                       offset=vaddr + 32768)

    parena = nc.alloc_psum_tensor("parena", [128, 2048], F32)
    pO = nc.alloc_psum_tensor("pO", [128, 512], F32)
    pstat = nc.alloc_psum_tensor("pstat", [128, 512], F32)
    parena2 = nc.alloc_psum_tensor("parena2", [128, 1024], F32)
    pbc = parena2[:, 0:512]
    px = parena2[:, 512:1024]

    SETS = [parena[:, 0:1024], parena[:, 1024:2048], parena2[:, 0:1024]]

    sem_names = ["pe", "act", "dve", "ldg", "lds", "out",
                 "ld0", "ld1", "ldx", "ldn", "lsw", "lsw2", "rc", "rc2"]
    sems = {k: nc.alloc_semaphore(k) for k in sem_names}
    q = {k: [] for k in ("tensor", "scalar", "vector", "gpsimd", "sync")}
    n = {k: 0 for k in sem_names}

    def on(eng, f):
        q[eng].append(f)

    def w(eng, key, v):
        if v and v > 0:
            on(eng, lambda e, s=sems[key], vv=v: e.wait_ge(s, vv))

    def tick(eng, key, f, k=1):
        n[key] += k
        on(eng, lambda e, ff=f, s=sems[key], kk=k: ff(e).then_inc(s, kk))
        return n[key]

    def ccol_(i):
        return ccol[:, i:i + 1]

    def crow_(i):
        return ccol[0:1, i:i + 1]

    RFA = bcf[0:1, 0:512]
    RFB = rowsF[0:1, 0:512]
    RB_R = rowsB[0:1, 0:512]
    RB_MR = rowsB[0:1, 512:1024]

    WS = {"k": wsums[0:1, 0:1024], "v": wsums[0:1, 1024:2048],
          "q": wsums[0:1, 2048:3072], "cos": wsums[0:1, 3072:4096],
          "sin": wsums[0:1, 4096:5120]}

    tick("sync", "lds", lambda e: e.dma_start(out=bigmask[:, :], in_=bigmask_d), 16)
    tick("sync", "lds", lambda e: e.dma_start(out=onescol[:, :], in_=onescol_d), 16)
    m_consts = tick("sync", "lds", lambda e: e.dma_start(out=allones[:, :], in_=allones_d), 16)

    for idx, val in [(C_INV1024, 1.0 / 1024), (C_EIGHTH, 0.125), (C_M05, -0.5),
                     (C_THREE, 3.0), (C_NEG1, -1.0), (C_HALF, 0.5), (C_EPS, 1e-5),
                     (C_PI2, float(np.pi / 2)), (C_MI1024, -1.0 / 1024)]:
        on("vector", lambda e, i=idx, v=val: e.memset(ccol[:, i:i + 1], v))
    on("vector", lambda e: e.memset(onescol_f[:, :], 1.0))
    m_memset = tick("vector", "dve",
                    lambda e: e.memset(ccol[:, C_M18:C_M18 + 1], -0.125))

    w("tensor", "lds", m_consts)
    w("scalar", "dve", m_memset)

    stageB = [stage_b, stage_b2]
    ssem = ["lsw", "lsw2"]
    buf_free_pe = [0, 0]
    buf_free_st = [0, 0]
    wscr_ready = {}

    prev_wsum_dve = [0]
    wsum_pe_last = [0]

    def startup_wsum(name, sb):
        w("tensor", "dve", prev_wsum_dve[0])
        for jh in range(2):
            dst = pstat[0:1, :] if jh == 0 else pbc[0:1, :]
            for ic in range(NIC):
                def f(e, d=dst, i=ic, j=jh, sb=sb):
                    return e.matmul(d, onescol[:, :],
                                    sb[:, i * 1024 + j * 512: i * 1024 + (j + 1) * 512],
                                    start=(i == 0), stop=(i == 7))
                if ic == 7 and jh == 1:
                    wsum_pe_last[0] = tick("tensor", "pe", f)
                else:
                    on("tensor", lambda e, ff=f: ff(e))
        w("vector", "pe", wsum_pe_last[0])
        on("vector", lambda e, nm=name: e.tensor_scalar(
            WS[nm][:, 0:512], pstat[0:1, :], crow_(C_NEG1), None, ALU.mult))
        mdone = tick("vector", "dve", lambda e, nm=name: e.tensor_scalar(
            WS[nm][:, 512:1024], pbc[0:1, :], crow_(C_NEG1), None, ALU.mult))
        prev_wsum_dve[0] = mdone
        return mdone

    wsum_dve = 0
    bi = 0
    for name in ["k", "v", "q"]:
        sb = stageB[bi]
        w("gpsimd", "pe", buf_free_pe[bi])
        w("gpsimd", ssem[bi], buf_free_st[bi])
        m_cast = tick("gpsimd", "ldg", lambda e, s=w_d[name], sb=sb: e.dma_start(
            out=sb[:, :], in_=s.rearrange("(ic p) j -> p ic j", p=128)), 16)
        w("tensor", "ldg", m_cast)
        wsum_dve = startup_wsum(name, sb)
        buf_free_pe[bi] = wsum_pe_last[0]
        w("sync", "dve", wsum_dve)
        buf_free_st[bi] = tick("sync", ssem[bi], lambda e, wi=WIDX[name], sb=sb:
                               e.dma_start(out=wscr[wi], in_=sb[:, :]), 16)
        wscr_ready[name] = (ssem[bi], buf_free_st[bi])
        bi ^= 1

    sb = stageB[bi]
    w("gpsimd", "pe", buf_free_pe[bi])
    w("gpsimd", ssem[bi], buf_free_st[bi])
    m_cast = tick("gpsimd", "ldg", lambda e, sb=sb: e.dma_start(
        out=sb[:, :], in_=w_d["o"].rearrange("(ic p) j -> p ic j", p=128)), 16)
    w("vector", "ldg", m_cast)
    wo_center = 0
    for ic in range(NIC):
        on("vector", lambda e, i=ic, sb=sb: e.tensor_reduce(
            erow[:, 8:9], sb[:, i * 1024:(i + 1) * 1024], AX, ALU.add))
        mneg = tick("vector", "dve", lambda e: e.tensor_scalar(
            erow[:, 8:9], erow[:, 8:9], ccol_(C_MI1024), None, ALU.mult))
        w("vector", "dve", mneg)
        wo_center = tick("vector", "dve", lambda e, i=ic, sb=sb: e.tensor_scalar(
            sb[:, i * 1024:(i + 1) * 1024],
            sb[:, i * 1024:(i + 1) * 1024], erow[:, 8:9], None, ALU.add))
    w("sync", "dve", wo_center)
    buf_free_st[bi] = tick("sync", ssem[bi], lambda e, sb=sb: e.dma_start(
        out=wscr[WIDX["o"]], in_=sb[:, :]), 16)
    wscr_ready["o"] = (ssem[bi], buf_free_st[bi])
    bi ^= 1

    on("scalar", lambda e: e.preload_act_table(AF.Sin))
    prev_sin = 0
    prev_add = 0
    for name, base_d, noise_d, use_cos in [("cos", th_d, thn_d, True),
                                           ("sin", ph_d, phn_d, False)]:
        sb = stageB[bi]
        w("sync", "act", prev_sin)
        mf = tick("sync", "lds", lambda e, s=base_d: e.dma_start(
            out=stage_f[:, :], in_=s.rearrange("(ic p) j -> p ic j", p=128)), 16)
        w("gpsimd", "dve", prev_add)
        mg = tick("gpsimd", "ldn", lambda e, s=noise_d: e.dma_start(
            out=stage2_f[:, :], in_=s.rearrange("(ic p) j -> p ic j", p=128)), 16)
        w("vector", "lds", mf)
        w("vector", "ldn", mg)
        add_done = tick("vector", "dve", lambda e: e.tensor_tensor(
            stage_f[:, :], stage_f[:, :], stage2_f[:, :], ALU.add))
        prev_add = add_done
        w("scalar", "dve", add_done)
        w("scalar", ssem[bi], buf_free_st[bi])
        w("scalar", "pe", buf_free_pe[bi])
        if use_cos:
            sin_done = tick("scalar", "act", lambda e, sb=sb: e.activation(
                sb[:, :], stage_f[:, :], AF.Sin, bias=ccol_(C_PI2)))
        else:
            sin_done = tick("scalar", "act", lambda e, sb=sb: e.activation(
                sb[:, :], stage_f[:, :], AF.Sin))
        prev_sin = sin_done
        w("tensor", "act", sin_done)
        wsum_dve = startup_wsum(name, sb)
        buf_free_pe[bi] = wsum_pe_last[0]
        w("sync", "dve", wsum_dve)
        w("sync", "act", sin_done)
        buf_free_st[bi] = tick("sync", ssem[bi], lambda e, wi=WIDX[name], sb=sb:
                               e.dma_start(out=wscr[wi], in_=sb[:, :]), 16)
        wscr_ready[name] = (ssem[bi], buf_free_st[bi])
        bi ^= 1

    startup_lsw = buf_free_st[0]
    startup_lsw2 = buf_free_st[1]
    startup_b0_pe = buf_free_pe[0]
    startup_wsum_dve = wsum_dve

    startup_sin_act = prev_sin

    from concourse.hw_specs import get_activation_tables
    tables = list(get_activation_tables(nc.m.arch).keys())
    nle_idx = tables.index("natural_log_exp_and_others")
    on("scalar", lambda e, idx=nle_idx: e.add_instruction(mybir.InstLoadActFuncSet(
        name=e.bass.get_next_instruction_name(), act_func_set_id=idx,
        ins=[], outs=[])))

    half_seq = 0
    slot_free = [0, 0]
    slot_of = {}

    def stream_half(c, name, h):
        nonlocal half_seq
        slot = half_seq % 2
        half_seq += 1
        sem = "ld0" if slot == 0 else "ld1"
        w("gpsimd", "pe", slot_free[slot])
        w("gpsimd", *wscr_ready[name])
        src = wscr[WIDX[name]].rearrange("p (ic hh j) -> p ic hh j", ic=8, hh=2)[:, :, h, :]
        dst = wsl[:, slot * 4096:(slot + 1) * 4096].rearrange("p (ic j) -> p ic j", ic=8)
        m = tick("gpsimd", sem, lambda e, s=src, d=dst: e.dma_start(out=d, in_=s), 16)
        slot_of[(c, name, h)] = (slot, (sem, m))

    def wap(slot, ic, sub):
        return wsl[:, slot * 4096 + ic * 512 + sub * 128: slot * 4096 + ic * 512 + (sub + 1) * 128]

    def wap512(slot, ic):
        return wsl[:, slot * 4096 + ic * 512: slot * 4096 + (ic + 1) * 512]

    bankF = lambda b: parena[:, b * 512:(b + 1) * 512]
    bank_free = [0, 0, 0, 0]
    bank_free_sem = ["dve", "dve", "dve", "dve"]
    pstat_free = startup_wsum_dve
    pstat_free_sem = "dve"
    px_free_sem = "dve"
    pbc_free = startup_wsum_dve
    pbc_free_sem = "dve"
    px_free = 0
    oev = 0
    oy_scaled = 0
    p3_evac_last = 0
    r1_store = 0
    rowsB_free_pe = 0
    last_bcast_pe = 0
    last_exp3 = [0, 0, 0]
    last_av = {0: 0, 1: 0}
    res2_prev = 0
    zrow_free = 0
    zfA = 0
    zfB = 0
    zb_mm_last = 0
    m_xb_next = None
    sq_free = [0, 0]

    def pe_bank_wait(b):
        w("tensor", bank_free_sem[b], bank_free[b])

    def ln_rows(stats_pe, has_mean, centered=False, fold_r=True):
        nonlocal pstat_free, pbc_free, pstat_free_sem, pbc_free_sem
        pstat_free_sem = "dve"
        pbc_free_sem = "dve"
        w("vector", "pe", stats_pe)
        w("vector", "pe", rowsB_free_pe)
        if has_mean and centered:
            on("vector", lambda e: e.tensor_scalar(RFA, pstat[0:1, :], crow_(C_INV1024), None, ALU.mult))
            mark = tick("vector", "dve", lambda e: e.tensor_scalar(
                RFB, pbc[0:1, :], crow_(C_INV1024), None, ALU.mult))
        elif has_mean:
            on("vector", lambda e: e.tensor_scalar(RFA, pstat[0:1, :], crow_(C_INV1024), None, ALU.mult))
            on("vector", lambda e: e.scalar_tensor_tensor(
                RFB, RFA, crow_(C_NEG1), RFA, ALU.mult, ALU.mult))
            mark = tick("vector", "dve", lambda e: e.scalar_tensor_tensor(
                RFB, pbc[0:1, :], crow_(C_INV1024), RFB, ALU.mult, ALU.add))
        else:
            mark = tick("vector", "dve", lambda e: e.tensor_scalar(
                RFB, pstat[0:1, :], crow_(C_INV1024), None, ALU.mult))
        pstat_free = mark
        pbc_free = mark
        w("scalar", "dve", mark)
        on("scalar", lambda e: e.activation(RFB, RFB, AF.Ln, bias=crow_(C_EPS)))
        r_act = tick("scalar", "act", lambda e: e.activation(RFB, RFB, AF.Exp, scale=crow_(C_M05)))
        w("vector", "act", r_act)
        if has_mean:
            if fold_r:
                on("vector", lambda e: e.tensor_tensor(RFA, RFA, RFB, ALU.mult))
            on("vector", lambda e: e.tensor_copy(RB_R, RFB))
            return tick("vector", "dve", lambda e: e.tensor_copy(RB_MR, RFA))
        else:
            return tick("vector", "dve", lambda e: e.tensor_copy(RB_R, RFB))

    def bcast(row_ap, dst_ap, row_ready_dve):
        nonlocal pbc_free, last_bcast_pe, pbc_free_sem
        w("tensor", pbc_free_sem, pbc_free)
        pbc_free_sem = "dve"
        w("tensor", "dve", row_ready_dve)
        m = tick("tensor", "pe", lambda e, r=row_ap: e.matmul(
            pbc[:, :], allones[0:1, :], r, start=True, stop=True))
        last_bcast_pe = m
        w("vector", "pe", m)
        pbc_free = tick("vector", "dve", lambda e, d=dst_ap: e.tensor_copy(d, pbc[:, :]))
        return pbc_free

    out_hist = [0]
    for c in range(NCH):
        w("sync", "out", out_hist[c])
        m_xf = tick("sync", "lds", lambda e, cc=c: e.dma_start(
            out=xf[:, :].rearrange("p (ic t) -> p ic t", ic=8),
            in_=xT.rearrange("(ic p) t -> p ic t", p=128)[:, :, cc * TC:(cc + 1) * TC]), 16)
        if c == 0:
            m_xb = tick("gpsimd", "ldx", lambda e, cc=c: e.dma_start(
                out=xb[:, :].rearrange("p (ic t) -> p ic t", ic=8),
                in_=xT.rearrange("(ic p) t -> p ic t", p=128)[:, :, cc * TC:(cc + 1) * TC]), 16)
        else:
            m_xb = m_xb_next

        stream_half(c, "k", 0)
        stream_half(c, "k", 1)

        if c == 0:
            w("vector", "act", startup_sin_act)
            w("vector", "lsw", startup_lsw)
            w("vector", "lsw2", startup_lsw2)
            w("vector", "pe", startup_b0_pe)
        w("scalar", "ldx", m_xb)
        stats_last = 0
        for ic in range(NIC):
            sh = ic % 2
            w("scalar", "pe", sq_free[sh])
            mact = tick("scalar", "act", lambda e, i=ic, s=sh: e.activation(
                sq[:, s * 512:(s + 1) * 512], xb[:, i * 512:(i + 1) * 512],
                AF.Square))
            w("tensor", "ldx", m_xb)
            w("tensor", "act", mact)
            if ic == 0:
                w("tensor", pstat_free_sem, pstat_free)
                w("tensor", pbc_free_sem, pbc_free)
            on("tensor", lambda e, i=ic: e.matmul(
                pstat[0:1, :], onescol[:, :], xb[:, i * 512:(i + 1) * 512],
                start=(i == 0), stop=(i == 7)))

            def fsq(e, s=sh, i=ic):
                return e.matmul(pbc[0:1, :], onescol[:, :], sq[:, s * 512:(s + 1) * 512],
                                start=(i == 0), stop=(i == 7))
            stats_last = tick("tensor", "pe", fsq)
            sq_free[sh] = stats_last
        ln1 = ln_rows(stats_last, has_mean=True, fold_r=False)
        bcast(RB_R, bc1[:, :], ln1)
        w("sync", "dve", ln1)
        w("sync", "dve", p3_evac_last)
        r1_store = tick("sync", "rc", lambda e: e.dma_start(
            out=rscr_d, in_=RB_R[0:1, 0:512]), 16)
        w("gpsimd", "rc", r1_store)
        mrc = tick("gpsimd", "rc2", lambda e: e.dma_start(
            out=ecol4[:, :], in_=rscr_d.rearrange("o (s p) -> (o p) s", p=128),
            allow_slow_non_contiguous=True), 16)

        slot_k0, mk0 = slot_of[(c, "k", 0)]
        slot_k1, mk1 = slot_of[(c, "k", 1)]
        for jc in range(8):
            slot, mld = (slot_k0, mk0) if jc < 4 else (slot_k1, mk1)
            b = jc % 2
            pe_bank_wait(b)
            w("tensor", mld[0], mld[1])
            w("tensor", "ldx", m_xb)
            for ic in range(NIC):
                on("tensor", lambda e, i=ic, s=slot, j=jc, bb=b: e.matmul(
                    bankF(bb), wap(s, i, j % 4), xb[:, i * 512:(i + 1) * 512],
                    start=(i == 0), stop=False))

            if jc == 0:
                w("tensor", "dve", ln1)

            def fmean(e, j=jc, bb=b):
                return e.matmul(bankF(bb), WS["k"][:, j * 128:(j + 1) * 128], RB_MR,
                                start=False, stop=True)
            mlast = tick("tensor", "pe", fmean)
            if jc == 3:
                slot_free[slot_k0] = mlast
            if jc == 7:
                slot_free[slot_k1] = mlast
            w("vector", "pe", mlast)
            ev = tick("vector", "dve", lambda e, j=jc, cc=c, bb=b: e.tensor_tensor(
                kT[:, j * 4096 + cc * TC: j * 4096 + (cc + 1) * TC], bankF(bb),
                bc1[:, :], ALU.mult))
            bank_free[b] = ev
            bank_free_sem[b] = "dve"
        stream_half(c, "v", 0)
        stream_half(c, "v", 1)

        slot_v0, mv0 = slot_of[(c, "v", 0)]
        slot_v1, mv1 = slot_of[(c, "v", 1)]
        for jh in range(2):
            slot, mld = (slot_v0, mv0) if jh == 0 else (slot_v1, mv1)
            for st in range(4):
                b = 2 + st % 2
                pe_bank_wait(b)
                w("tensor", mld[0], mld[1])
                for ic in range(NIC):
                    on("tensor", lambda e, i=ic, s=slot, t=st, bb=b: e.matmul(
                        bankF(bb), xb[:, i * 512 + t * 128: i * 512 + (t + 1) * 128],
                        wap512(s, i), start=(i == 0), stop=False))

                if jh == 0 and st == 0:
                    w("tensor", "dve", ln1)

                def fmeanv(e, t=st, j=jh, bb=b):
                    return e.matmul(bankF(bb), RB_MR[:, t * 128:(t + 1) * 128],
                                    WS["v"][:, j * 512:(j + 1) * 512], start=False, stop=True)
                mlast = tick("tensor", "pe", fmeanv)
                if st == 3:
                    if jh == 0:
                        slot_free[slot_v0] = mlast
                    else:
                        slot_free[slot_v1] = mlast
                w("vector", "pe", mlast)
                if jh == 0 and st == 0:
                    w("vector", "rc2", mrc)
                ev = tick("vector", "dve", lambda e, t=st, j=jh, cc=c, bb=b: e.tensor_scalar(
                    vT[:, (cc * 4 + t) * 1024 + j * 512: (cc * 4 + t) * 1024 + (j + 1) * 512],
                    bankF(bb), ecol4[:, t:t + 1], None, ALU.mult))
                bank_free[b] = ev
                bank_free_sem[b] = "dve"
        p3_evac_last = n["dve"]
        stream_half(c, "q", 0)
        stream_half(c, "q", 1)

        slot_q0, mq0 = slot_of[(c, "q", 0)]
        slot_q1, mq1 = slot_of[(c, "q", 1)]
        q_evac = 0
        for jc in range(8):
            slot, mld = (slot_q0, mq0) if jc < 4 else (slot_q1, mq1)
            b = jc % 2
            pe_bank_wait(b)
            w("tensor", mld[0], mld[1])
            w("tensor", "ldx", m_xb)
            for ic in range(NIC):
                on("tensor", lambda e, i=ic, s=slot, j=jc, bb=b: e.matmul(
                    bankF(bb), wap(s, i, j % 4), xb[:, i * 512:(i + 1) * 512],
                    start=(i == 0), stop=False))

            def fmeanq(e, j=jc, bb=b):
                return e.matmul(bankF(bb), WS["q"][:, j * 128:(j + 1) * 128], RB_MR,
                                start=False, stop=True)
            mlast = tick("tensor", "pe", fmeanq)
            if jc == 3:
                slot_free[slot_q0] = mlast
            if jc == 7:
                slot_free[slot_q1] = mlast
            w("vector", "pe", mlast)
            q_evac = tick("vector", "dve", lambda e, j=jc, bb=b: e.tensor_tensor(
                qam[:, j * 512:(j + 1) * 512], bankF(bb), bc1[:, :], ALU.mult))
            bank_free[b] = q_evac
            bank_free_sem[b] = "dve"
        xb_free_pe = n["pe"]
        rowsB_free_pe = n["pe"]
        stream_half(c, "o", 0)
        stream_half(c, "o", 1)

        if c + 1 < NCH:
            w("gpsimd", "pe", xb_free_pe)
            m_xb_next = tick("gpsimd", "ldx", lambda e, cc=c + 1: e.dma_start(
                out=xb[:, :].rearrange("p (ic t) -> p ic t", ic=8),
                in_=xT.rearrange("(ic p) t -> p ic t", p=128)[:, :, cc * TC:(cc + 1) * TC]), 16)

        kts = 4 * c + 4
        w("scalar", "dve", res2_prev)
        if c == 0:
            w("vector", "lds", m_consts)
        pending_early1 = None
        pending_early2 = None
        pending_late = None
        for hp in range(NHP):
            if hp == 0:
                w("tensor", pstat_free_sem, pstat_free)
                w("tensor", px_free_sem, px_free)
                w("tensor", "dve", res2_prev)
            def emit_av(K, pp, ex_a, ex_b, avwA, avwB, off, P=hp, KTS=None):
                if K == 0:
                    w("tensor", "dve", oev)
                w("tensor", "act", ex_a)
                if avwA:
                    w("tensor", "dve", avwA)

                def favA(e, P=P, K=K, pp=pp, KTS=KTS, o=off):
                    return e.matmul(pO[0:64, o:512],
                                    vT[:, K * 1024 + P * 128: K * 1024 + P * 128 + 64],
                                    pB[:, pp * 1024 + o:pp * 1024 + 512],
                                    start=(K == 0), stop=(K == KTS - 1),
                                    tile_position=(0, 0), skip_group_check=True)
                on("tensor", lambda e, f=favA: f(e))
                w("tensor", "act", ex_b)
                if avwB:
                    w("tensor", "dve", avwB)

                def favB(e, P=P, K=K, pp=pp, KTS=KTS, o=off):
                    return e.matmul(pO[64:128, o:512],
                                    vT[:, K * 1024 + P * 128 + 64: K * 1024 + P * 128 + 128],
                                    pB[:, pp * 1024 + 512 + o:pp * 1024 + 1024],
                                    start=(K == 0), stop=(K == KTS - 1),
                                    tile_position=(0, 64), skip_group_check=True)
                on("tensor", lambda e, f=favB: f(e))

                if K == 0:
                    w("tensor", pstat_free_sem, pstat_free)

                def fzA(e, K=K, pp=pp, KTS=KTS, o=off):
                    return e.matmul(pstat[64:65, o:512], onescol[:, :],
                                    pB[:, pp * 1024 + o:pp * 1024 + 512],
                                    start=(K == 0), stop=(K == KTS - 1),
                                    tile_position=(0, 64), skip_group_check=True)

                def fzB(e, K=K, pp=pp, KTS=KTS, o=off):
                    return e.matmul(pstat[32:33, o:512], onescol[:, :],
                                    pB[:, pp * 1024 + 512 + o:pp * 1024 + 1024],
                                    start=(K == 0), stop=(K == KTS - 1),
                                    tile_position=(0, 32), skip_group_check=True)
                on("tensor", lambda e, f=fzA: f(e))
                last_av[pp] = tick("tensor", "pe", fzB)

            av_q = []
            for kt in range(kts):
                p = kt % 2
                s3 = kt % 3
                sAP = SETS[s3]
                m = kt - 4 * c
                off = m * 128 if m > 0 else 0
                w("tensor", "act", last_exp3[s3])
                if hp == 0 and kt < 3:
                    if s3 < 2:
                        pe_bank_wait(2 * s3)
                        pe_bank_wait(2 * s3 + 1)
                    else:
                        w("tensor", pbc_free_sem, pbc_free)
                        w("tensor", px_free_sem, px_free)
                    w("tensor", "dve", q_evac)

                def fsA(e, P=hp, K=kt, sAP=sAP, o=off):
                    return e.matmul(sAP[:, o:512],
                                    kT[0:64, P * 4096 + K * 128:P * 4096 + (K + 1) * 128],
                                    qam[0:64, P * 512 + o:(P + 1) * 512],
                                    start=True, stop=True)

                def fsB(e, P=hp, K=kt, sAP=sAP, o=off):
                    return e.matmul(sAP[:, 512 + o:1024],
                                    kT[64:128, P * 4096 + K * 128:P * 4096 + (K + 1) * 128],
                                    qam[64:128, P * 512 + o:(P + 1) * 512],
                                    start=True, stop=True)
                on("tensor", lambda e, f=fsA: f(e))
                sc_done = tick("tensor", "pe", fsB)
                if kt == 2 and pending_late is not None:
                    pending_late()
                    pending_late = None
                if len(av_q) == 2:
                    emit_av(*av_q.pop(0), KTS=kts)

                if kt == 0 and pending_early1 is not None:
                    pending_early1()
                    pending_early1 = None
                if kt == 0 and pending_early2 is not None:
                    pending_early2()
                    pending_early2 = None

                w("scalar", "pe", sc_done)
                w("scalar", "pe", last_av[p])

                if off == 0:
                    def fex(e, pp=p, sAP=sAP):
                        return e.activation(pB[:, pp * 1024:(pp + 1) * 1024],
                                            sAP[:, :],
                                            AF.Exp, scale=ccol_(C_EIGHTH))
                else:
                    def fex(e, pp=p, o=off, sAP=sAP):
                        return e.activation(
                            pB[:, pp * 1024:(pp + 1) * 1024].rearrange(
                                "q (b v) -> q b v", b=2)[:, :, o:],
                            sAP.rearrange("q (b v) -> q b v", b=2)[:, :, o:],
                            AF.Exp, scale=ccol_(C_EIGHTH))
                ex_a = ex_b = tick("scalar", "act", fex)
                last_exp3[s3] = ex_b

                avwA = avwB = 0
                if m >= 0:
                    w("vector", "act", ex_a)
                    avwA = tick("vector", "dve", lambda e, pp=p, mm=m, o=off: e.tensor_tensor(
                        pB[:, pp * 1024 + o:pp * 1024 + 512],
                        pB[:, pp * 1024 + o:pp * 1024 + 512],
                        bigmask[:, (3 - mm) * 128 + o:(3 - mm) * 128 + 512], ALU.mult))
                    w("vector", "act", ex_b)
                    avwB = tick("vector", "dve", lambda e, pp=p, mm=m, o=off: e.tensor_tensor(
                        pB[:, pp * 1024 + 512 + o:pp * 1024 + 1024],
                        pB[:, pp * 1024 + 512 + o:pp * 1024 + 1024],
                        bigmask[:, (3 - mm) * 128 + o:(3 - mm) * 128 + 512], ALU.mult))

                av_q.append((kt, p, ex_a, ex_b, avwA, avwB, off))
            for entry in av_q:
                emit_av(*entry, KTS=kts)
            av_q = []

            def run_end_early1(P=hp, la=last_av[(kts - 1) % 2]):
                nonlocal zfA, oev
                w("vector", "pe", la)
                oev = tick("vector", "dve", lambda e: e.tensor_copy(
                    oy[:, P * 512:(P + 1) * 512], pO[:, :]))
                w("scalar", "pe", la)
                on("scalar", lambda e: e.activation(
                    pstat[64:65, :], pstat[64:65, :], AF.Ln))
                zfA = tick("scalar", "act", lambda e: e.activation(
                    sq[64:65, 0:512], pstat[64:65, :], AF.Exp,
                    scale=ccol[64:65, C_NEG1:C_NEG1 + 1]))

            def run_end_early2():
                nonlocal zfB, zrow_free
                on("scalar", lambda e: e.activation(
                    pstat[32:33, :], pstat[32:33, :], AF.Ln))
                zfB = tick("scalar", "act", lambda e: e.activation(
                    sq[32:33, 512:1024], pstat[32:33, :], AF.Exp,
                    scale=ccol[32:33, C_NEG1:C_NEG1 + 1]))
                zrow_free = zfB

            def run_end_late(P=hp):
                nonlocal pstat_free, pstat_free_sem, zb_mm_last, oy_scaled
                w("tensor", "act", zrow_free)
                w("tensor", pstat_free_sem, pstat_free)
                on("tensor", lambda e: e.matmul(
                    pstat[0:64, :], allones[64:65, 0:64], sq[64:65, 0:512],
                    start=True, stop=True, tile_position=(64, 0),
                    skip_group_check=True))
                zb_mm_last = tick("tensor", "pe", lambda e: e.matmul(
                    pstat[64:128, :], allones[32:33, 0:64], sq[32:33, 512:1024],
                    start=True, stop=True, tile_position=(32, 64),
                    skip_group_check=True))
                w("vector", "pe", zb_mm_last)
                pstat_free = tick("vector", "dve", lambda e: e.tensor_copy(
                    sq[:, 0:512], pstat[:, :]))
                pstat_free_sem = "dve"
                oy_scaled = tick("vector", "dve", lambda e: e.tensor_tensor(
                    oy[:, P * 512:(P + 1) * 512], oy[:, P * 512:(P + 1) * 512],
                    sq[:, 0:512], ALU.mult))
            pending_early1 = run_end_early1
            pending_early2 = run_end_early2
            pending_late = run_end_late
        pending_early1()
        pending_early1 = None
        pending_early2()
        pending_early2 = None
        pending_late()
        pending_late = None

        bank_free[0] = bank_free[1] = last_exp3[0]
        bank_free_sem[0] = bank_free_sem[1] = "act"
        bank_free[2] = bank_free[3] = last_exp3[1]
        bank_free_sem[2] = bank_free_sem[3] = "act"
        pbc_free = last_exp3[2]
        pbc_free_sem = "act"
        px_free = last_exp3[2]
        px_free_sem = "act"

        import os as _os
        if _os.environ.get("DBG_OY") == "1" and c == 0:
            d_oy = nc.dram_tensor("d_oy", [128, 4096], BF16, kind="ExternalOutput").ap()
            w("sync", "dve", oy_scaled)
            tick("sync", "out", lambda e: e.dma_start(out=d_oy, in_=oy[:, :]), 16)

        slot_o0, mo0 = slot_of[(c, "o", 0)]
        slot_o1, mo1 = slot_of[(c, "o", 1)]
        attn_evac = {}
        stats_last = 0
        for oc in range(8):
            slot, mld = (slot_o0, mo0) if oc < 4 else (slot_o1, mo1)
            b = oc % 2
            pe_bank_wait(b)
            w("tensor", mld[0], mld[1])
            w("tensor", "dve", oy_scaled)
            mlast = 0
            for jc in range(NIC):
                def fwo(e, j=jc, s=slot, o=oc, bb=b):
                    return e.matmul(bankF(bb), wap(s, j, o % 4), oy[:, j * 512:(j + 1) * 512],
                                    start=(j == 0), stop=(j == 7))
                if jc == 7:
                    mlast = tick("tensor", "pe", fwo)
                else:
                    on("tensor", lambda e, ff=fwo: ff(e))
            if oc == 3:
                slot_free[slot_o0] = mlast
            if oc == 7:
                slot_free[slot_o1] = mlast
            w("vector", "pe", mlast)
            aev = tick("vector", "dve", lambda e, o=oc, bb=b: e.tensor_copy(
                qam[:, o * 512:(o + 1) * 512], bankF(bb)))
            attn_evac[oc] = aev
            bank_free[b] = aev
            bank_free_sem[b] = "dve"
            sh = oc % 2
            w("vector", "pe", sq_free[sh])
            if oc == 0:
                w("vector", "pe", zb_mm_last)
            msq = tick("vector", "dve", lambda e, o=oc, s=sh: e.tensor_tensor(
                sq[:, s * 512:(s + 1) * 512], qam[:, o * 512:(o + 1) * 512],
                qam[:, o * 512:(o + 1) * 512], ALU.mult))
            w("tensor", "dve", msq)
            if oc == 0:
                w("tensor", pstat_free_sem, pstat_free)

            def fst(e, o=oc, s=sh):
                return e.matmul(pstat[0:1, :], onescol[:, :], sq[:, s * 512:(s + 1) * 512],
                                start=(o == 0), stop=(o == 7))
            stats_last = tick("tensor", "pe", fst)
            sq_free[sh] = stats_last
        ln3 = ln_rows(stats_last, has_mean=False)
        stream_half(c, "cos", 0)
        stream_half(c, "cos", 1)

        bcast(RB_R, bc1[:, :], ln3)
        rowsB_free_pe = last_bcast_pe
        stats_last = 0
        for ic in range(NIC):
            sh = ic % 2
            w("vector", "dve", attn_evac[ic])
            if ic == 0:
                w("vector", "lds", m_xf)
            on("vector", lambda e, i=ic: e.tensor_tensor(
                pO[:, 0:512], qam[:, i * 512:(i + 1) * 512], bc1[:, :], ALU.mult))
            tick("vector", "dve", lambda e, i=ic: e.tensor_tensor(
                xf[:, i * 512:(i + 1) * 512], xf[:, i * 512:(i + 1) * 512],
                pO[:, 0:512], ALU.add))
            w("scalar", "pe", sq_free[sh])
            w("scalar", "dve", n["dve"])
            mact = tick("scalar", "act", lambda e, i=ic, s=sh: e.activation(
                sq[:, s * 512:(s + 1) * 512], xf[:, i * 512:(i + 1) * 512],
                AF.Square))
            w("tensor", "act", mact)
            if ic == 0:
                w("tensor", pbc_free_sem, pbc_free)
                w("tensor", pstat_free_sem, pstat_free)

            def fsq2(e, s=sh, i=ic):
                return e.matmul(pbc[0:1, :], onescol[:, :], sq[:, s * 512:(s + 1) * 512],
                                start=(i == 0), stop=(i == 7))
            sq_free[sh] = tick("tensor", "pe", fsq2)
            f2 = lambda e, i=ic: e.matmul(
                pstat[0:1, :], onescol_f[:, :], xf[:, i * 512:(i + 1) * 512],
                start=(i == 0), stop=(i == 7))
            stats_last = tick("tensor", "pe", f2)
        ln2 = ln_rows(stats_last, has_mean=True)
        bcast(RB_R, bc1[:, :], ln2)
        w("vector", "pe", last_bcast_pe)
        w("vector", "rc", r1_store)
        mlo = tick("vector", "dve", lambda e: e.scalar_tensor_tensor(
            RB_R, RB_MR, crow_(C_NEG1), RFA, ALU.mult, ALU.add))
        w("tensor", pbc_free_sem, pbc_free)
        w("tensor", "dve", mlo)
        mhb = tick("tensor", "pe", lambda e: e.matmul(
            pbc[:, :], allones[0:1, :], RB_MR, start=True, stop=True))
        last_bcast_pe = mhb
        w("vector", "pe", mhb)
        pbc_free = tick("vector", "dve", lambda e: e.tensor_scalar(
            pB[:, 512:1024], pbc[:, :], ccol_(C_NEG1), None, ALU.mult))
        yn_done = 0
        for ic in range(NIC):
            on("vector", lambda e, i=ic: e.tensor_tensor(
                pO[:, 0:512], xf[:, i * 512:(i + 1) * 512], bc1[:, :], ALU.mult))
            yn_done = tick("vector", "dve", lambda e, i=ic: e.tensor_tensor(
                oy[:, i * 512:(i + 1) * 512], pO[:, 0:512], pB[:, 512:1024],
                ALU.add))

        mag_done = {-1: 0}
        stats_sq_last = 0
        real_evac = {}
        ffn_last = {}
        for ch in range(2):
            slot_c, mc_ = slot_of[(c, "cos", ch)]
            for oc in range(ch * 4, ch * 4 + 4):
                bb = oc % 4
                pe_bank_wait(bb)
                w("tensor", mc_[0], mc_[1])
                if oc == ch * 4:
                    w("tensor", "dve", yn_done)
                for ic in range(NIC):
                    on("tensor", lambda e, i=ic, s=slot_c, o=oc, bk=bb: e.matmul(
                        bankF(bk), wap(s, i, o % 4), oy[:, i * 512:(i + 1) * 512],
                        start=(i == 0), stop=False))

                def fmean3(e, o=oc, bk=bb):
                    return e.matmul(bankF(bk), WS["cos"][:, o * 128:(o + 1) * 128], RB_R,
                                    start=False, stop=True)
                ffn_last[("r", oc)] = tick("tensor", "pe", fmean3)
                w("vector", "pe", ffn_last[("r", oc)])
                real_evac[oc] = tick("vector", "dve", lambda e, o=oc, bk=bb: e.tensor_copy(
                    qam[:, o * 512:(o + 1) * 512], bankF(bk)))
                bank_free[bb] = real_evac[oc]
                bank_free_sem[bb] = "dve"
            if ch == 0:
                slot_free[slot_of[(c, "cos", 0)][0]] = ffn_last[("r", 3)]
                stream_half(c, "sin", 0)
            else:
                slot_free[slot_of[(c, "cos", 1)][0]] = ffn_last[("r", 7)]
                stream_half(c, "sin", 1)
        pend_mean4 = []
        mean4_pe = 0

        def emit_mean4(o, mact):
            nonlocal mean4_pe
            w("tensor", "act", mact)
            if o == 0:
                w("tensor", pstat_free_sem, pstat_free)
            mean4_pe = tick("tensor", "pe", lambda e, o=o: e.matmul(
                pstat[0:1, :], onescol[:, :], qam[:, o * 512:(o + 1) * 512],
                start=(o == 0), stop=(o == 7)))

        for ch in range(2):
            slot_s, ms_ = slot_of[(c, "sin", ch)]
            for oc in range(ch * 4, ch * 4 + 4):
                bb = oc % 4
                pe_bank_wait(bb)
                w("tensor", ms_[0], ms_[1])
                for ic in range(NIC):
                    on("tensor", lambda e, i=ic, s=slot_s, o=oc, bk=bb: e.matmul(
                        bankF(bk), wap(s, i, o % 4), oy[:, i * 512:(i + 1) * 512],
                        start=(i == 0), stop=False))

                def fmean4(e, o=oc, bk=bb):
                    return e.matmul(bankF(bk), WS["sin"][:, o * 128:(o + 1) * 128], RB_R,
                                    start=False, stop=True)
                ffn_last[("i", oc)] = tick("tensor", "pe", fmean4)
                if len(pend_mean4) == 2:
                    emit_mean4(*pend_mean4.pop(0))
                w("vector", "pe", ffn_last[("i", oc)])
                w("vector", "pe", sq_free[0])
                w("vector", "act", mag_done[oc - 1])
                on("vector", lambda e, bk=bb: e.tensor_copy(
                    pB[:, 0:512], bankF(bk)))
                on("vector", lambda e, o=oc: e.tensor_tensor(
                    sq[:, 0:512], qam[:, o * 512:(o + 1) * 512],
                    qam[:, o * 512:(o + 1) * 512], ALU.mult))
                on("vector", lambda e: e.tensor_tensor(
                    sq[:, 512:1024], pB[:, 0:512], pB[:, 0:512], ALU.mult))
                m2 = tick("vector", "dve", lambda e: e.tensor_tensor(
                    pO[:, 0:512], sq[:, 0:512], sq[:, 512:1024], ALU.add))
                bank_free[bb] = m2
                bank_free_sem[bb] = "dve"
                w("scalar", "dve", m2)
                on("scalar", lambda e: e.activation(pO[:, 0:512], pO[:, 0:512], AF.Ln))
                mag_act = tick("scalar", "act", lambda e, o=oc: e.activation(
                    qam[:, o * 512:(o + 1) * 512], pO[:, 0:512], AF.Exp,
                    scale=ccol_(C_HALF)))
                mag_done[oc] = mag_act
                pend_mean4.append((oc, mag_act))
            if ch == 0:
                slot_free[slot_of[(c, "sin", 0)][0]] = ffn_last[("i", 3)]
            else:
                slot_free[slot_of[(c, "sin", 1)][0]] = ffn_last[("i", 7)]
        for entry in pend_mean4:
            emit_mean4(*entry)
        pend_mean4 = []
        w("vector", "pe", mean4_pe)
        mrow = tick("vector", "dve", lambda e: e.tensor_scalar(
            RB_R, pstat[0:1, :], crow_(C_MI1024), None, ALU.mult))
        bcast(RB_R, bc1[:, :], mrow)
        w("vector", "act", mag_done[7])
        stats_sq_last = 0
        for oc in range(NIC):
            sh = oc % 2
            w("vector", "pe", sq_free[sh])
            on("vector", lambda e, o=oc: e.tensor_tensor(
                pB[:, 0:512], qam[:, o * 512:(o + 1) * 512], bc1[:, :], ALU.add))
            mdve = tick("vector", "dve", lambda e, o=oc, s=sh: e.tensor_tensor(
                sq[:, s * 512:(s + 1) * 512], pB[:, 0:512],
                qam[:, o * 512:(o + 1) * 512], ALU.mult))
            w("tensor", "dve", mdve)
            if oc == 0:
                w("tensor", pbc_free_sem, pbc_free)
            stats_sq_last = tick("tensor", "pe", lambda e, o=oc, s=sh: e.matmul(
                pbc[0:1, :], onescol[:, :], sq[:, s * 512:(s + 1) * 512],
                start=(o == 0), stop=(o == 7)))
            sq_free[sh] = stats_sq_last
        rowsB_free_pe = n["pe"]
        ln4 = ln_rows(stats_sq_last, has_mean=True, centered=True)

        bcast(RB_R, bc4[:, :], ln4)
        w("tensor", pbc_free_sem, pbc_free)
        w("tensor", "dve", ln4)
        mb4 = tick("tensor", "pe", lambda e: e.matmul(
            pbc[:, :], allones[0:1, :], RB_MR, start=True, stop=True))
        last_bcast_pe = mb4
        w("vector", "pe", mb4)
        pbc_free = tick("vector", "dve", lambda e: e.tensor_scalar(
            bcf[:, :], pbc[:, :], ccol_(C_NEG1), None, ALU.mult))
        rowsB_free_pe = last_bcast_pe
        res2_done = 0
        for ic in range(NIC):
            w("vector", "act", mag_done[ic])
            on("vector", lambda e, i=ic: e.tensor_tensor(
                pO[:, 0:512], qam[:, i * 512:(i + 1) * 512], bc4[:, :], ALU.mult))
            on("vector", lambda e, i=ic: e.tensor_tensor(
                xf[:, i * 512:(i + 1) * 512], xf[:, i * 512:(i + 1) * 512],
                pO[:, 0:512], ALU.add))
            res2_done = tick("vector", "dve", lambda e, i=ic: e.tensor_tensor(
                xf[:, i * 512:(i + 1) * 512], xf[:, i * 512:(i + 1) * 512],
                bcf[:, :], ALU.add))
        res2_prev = res2_done
        w("sync", "dve", res2_done)
        tick("sync", "out", lambda e, cc=c: e.dma_start(
            out=yT.rearrange("(ic p) t -> p ic t", p=128)[:, :, cc * TC:(cc + 1) * TC],
            in_=xf[:, :].rearrange("p (ic t) -> p ic t", ic=8)), 16)
        out_hist.append(n["out"])

    w("sync", "out", n["out"])

    with nc.Block() as blk:
        blk.tensor(lambda e: [f(e) for f in q["tensor"]])
        blk.scalar(lambda e: [f(e) for f in q["scalar"]])
        blk.vector(lambda e: [f(e) for f in q["vector"]])
        blk.gpsimd(lambda e: [f(e) for f in q["gpsimd"]])
        blk.sync(lambda e: [f(e) for f in q["sync"]])
    return nc


def kernel(L0_attn_x, L0_attn_Wq, L0_attn_Wk, L0_attn_Wv, L0_attn_Wo,
           L0_ffn_theta, L0_ffn_phi, L0_ffn_theta_noise, L0_ffn_phi_noise):
    x = np.asarray(L0_attn_x, dtype=np.float32)
    wq = np.ascontiguousarray(np.asarray(L0_attn_Wq, dtype=np.float32).T)
    wk = np.ascontiguousarray(np.asarray(L0_attn_Wk, dtype=np.float32).T)
    wv = np.ascontiguousarray(np.asarray(L0_attn_Wv, dtype=np.float32).T)
    wo = np.ascontiguousarray(np.asarray(L0_attn_Wo, dtype=np.float32).T)
    th = np.ascontiguousarray(np.asarray(L0_ffn_theta, dtype=np.float32))
    ph = np.ascontiguousarray(np.asarray(L0_ffn_phi, dtype=np.float32))
    thn = np.ascontiguousarray(np.asarray(L0_ffn_theta_noise, dtype=np.float32))
    phn = np.ascontiguousarray(np.asarray(L0_ffn_phi_noise, dtype=np.float32))

    bigmask = (np.arange(128)[:, None] <= (np.arange(896)[None, :] - 384)
               ).astype(BFNP)
    onescol = np.ones((128, 1), dtype=BFNP)
    allones = np.ones((128, 128), dtype=BFNP)

    nc = build_nc()
    in_maps = []
    for b in range(NCORE):
        in_maps.append({
            "xT": np.ascontiguousarray(x[b].T),
            "wkT": wk, "wvT": wv, "wqT": wq, "woT": wo,
            "theta": th, "theta_n": thn, "phi": ph, "phi_n": phn,
            "bigmask": bigmask, "onescol": onescol, "allones": allones,
        })
    res = run_bass_kernel_spmd(nc, in_maps, core_ids=list(range(NCORE)))
    out = np.stack([np.asarray(res.results[b]["yT"], dtype=np.float32).T
                    for b in range(NCORE)], axis=0)
    return out
